# version 07
# baseline (speedup 1.0000x reference)
import math
import numpy as np
import ml_dtypes

import concourse.bass as bass
import concourse.mybir as mybir
from concourse.bass_utils import run_bass_kernel_spmd
from contextlib import ExitStack

F32 = mybir.dt.float32
BF16 = mybir.dt.bfloat16
AX = mybir.AxisListType.X
ALU = mybir.AluOpType
ACT = mybir.ActivationFunctionType
BF = ml_dtypes.bfloat16

NE = 9
FT = 8
SCL = 1.0 / math.sqrt(128.0)
EORD = [8, 0, 1, 2, 3, 4, 5, 6, 7]


class Ctr:
    def __init__(self, h):
        self.h = h
        self.n = 0


def tick(ctr, inst, k=1):
    inst.then_inc(ctr.h, k)
    ctr.n += k
    return ctr.n


def _build():
    nc = bass.Bass("TRN2", target_bir_lowering=False, debug=False)

    x_h = nc.dram_tensor("x", [2048, 2048], F32, kind="ExternalInput").ap()
    xo_h = nc.dram_tensor("xo", [1024, 2048], F32, kind="ExternalInput").ap()
    wqk_h = nc.dram_tensor("wqk", [16, 128, 2048], F32, kind="ExternalInput").ap()
    wv_h = nc.dram_tensor("wv", [2, 128, 8192], F32, kind="ExternalInput").ap()
    wpj_h = nc.dram_tensor("wpj", [128, 16, 2048], F32, kind="ExternalInput").ap()
    bqk_h = nc.dram_tensor("bqk", [128, 16], F32, kind="ExternalInput").ap()
    bv_h = nc.dram_tensor("bv", [1, 1024], F32, kind="ExternalInput").ap()
    bpj_h = nc.dram_tensor("bpj", [1, 2048], F32, kind="ExternalInput").ap()
    rt_h = nc.dram_tensor("rt", [128, 128], F32, kind="ExternalInput").ap()
    rmsw_h = nc.dram_tensor("rmsw", [128, 16], F32, kind="ExternalInput").ap()
    gu_h = nc.dram_tensor("gu", [NE, FT, 2, 128, 2048], F32, kind="ExternalInput").ap()
    dn_h = nc.dram_tensor("dn", [NE, 4, 128, 4096], F32, kind="ExternalInput").ap()
    msk_h = nc.dram_tensor("msk", [128, 2048], BF16, kind="ExternalInput").ap()
    selb_h = nc.dram_tensor("selb", [8, 1024], BF16, kind="ExternalInput").ap()
    id32_h = nc.dram_tensor("id32", [128, 128], F32, kind="ExternalInput").ap()
    out_h = nc.dram_tensor("out", [1024, 2048], F32, kind="ExternalOutput").ap()

    cc_in = nc.dram_tensor("cc_in", [1024, 2048], BF16).ap()
    cc_out = nc.dram_tensor("cc_out", [8192, 2048], BF16, addr_space="Shared").ap()

    with ExitStack() as ctx0:
        def mksem(name):
            return Ctr(ctx0.enter_context(nc.semaphore(name)))

        sT, sV, sS = mksem("sT"), mksem("sV"), mksem("sS")
        dX = [mksem(f"dX{i}") for i in range(4)]
        dTR = [mksem(f"dTR{i}") for i in range(4)]
        dWQK = [mksem("dWQK0"), mksem("dWQK1")]
        dWV = mksem("dWV")
        dWPJ = mksem("dWPJ")
        dGU = [mksem("dGU0"), mksem("dGU1")]
        dDN = [mksem(f"dDN{i}") for i in range(4)]
        dMISC = mksem("dMISC")
        dMISCG = mksem("dMISCG")
        dMSK = mksem("dMSK")
        dXO = mksem("dXO")
        dCCI = mksem("dCCI")
        dCCO = mksem("dCCO")
        dHT = [mksem(f"dHT{i}") for i in range(4)]
        dOUT = mksem("dOUT")
        cS = mksem("cS")

        PA = ctx0.enter_context(nc.psum_tensor("PA", [128, 2048], F32))
        PB = ctx0.enter_context(nc.psum_tensor("PB", [128, 2048], F32))

        def sbt(name, shape, dtype, **kw):
            return ctx0.enter_context(nc.sbuf_tensor(name, shape, dtype, **kw))

        ones_c = sbt("ones_c", [128, 1], BF16)
        ones_f = sbt("ones_f", [128, 128], BF16)
        ones_r = sbt("ones_r", [1, 128], BF16)
        qkc = sbt("qkc", [128, 4], F32)
        id32 = sbt("id32s", [128, 128], F32)
        bqk = sbt("bqks", [128, 16], F32)
        bv = sbt("bvs", [1, 1024], BF16)
        bpj = sbt("bpjs", [1, 2048], BF16)
        rts = sbt("rts", [128, 128], BF16)
        rmsw = sbt("rmsws", [128, 16], F32)
        negc = sbt("negc", [128, 8], F32)
        bn6s = [sbt(f"bn6_{i}", [128, 24], F32) for i in range(4)]
        mvs = [sbt(f"mv_{i}", [128, 2], F32) for i in range(4)]
        tcs = [sbt(f"tc_{i}", [128, 2], F32) for i in range(4)]
        acs = [sbt(f"ac_{i}", [128, 1], F32) for i in range(4)]
        bcs = [sbt(f"bc_{i}", [128, 1], F32) for i in range(4)]
        sm1 = sbt("sm1", [1, 4], F32)
        sm1b = sbt("sm1b", [1, 1], BF16)
        lgT = sbt("lgT", [128, 64], F32)
        e_sb = sbt("e_sb", [128, 64], F32)
        msel = sbt("msel", [128, 64], F32)
        mcol = sbt("mcol", [128, 8], F32)
        mneg = sbt("mneg", [128, 8], F32)
        wd_tok = sbt("wd_tok", [128, 64], F32)
        wd8 = sbt("wd8", [8, 1024], BF16)
        selb = sbt("selbs", [8, 1024], BF16)
        wdB = sbt("wdB", [128, 1024], BF16)

        blk = ctx0.enter_context(nc.Block())

        def _(v):
            v.memset(ones_c[:, :], 1.0)
            v.memset(ones_f[:, :], 1.0)
            tick(sV, v.memset(ones_r[:, :], 1.0))
        blk.vector(_)
        v_ones = sV.n

        def _(sy):
            tick(dMISC, sy.dma_start(out=id32[:, :], in_=id32_h), 16)
            tick(dMISC, sy.dma_start(out=selb[:, :], in_=selb_h), 16)
            tick(dMISC, sy.dma_start(out=bqk[:, :], in_=bqk_h), 16)
            tick(dMISC, sy.dma_start(out=rmsw[:, :], in_=rmsw_h), 16)
        blk.sync(_)
        misc_done = dMISC.n

        def _(g):
            tick(dMISCG, g.dma_start(out=bv[:, :], in_=bv_h), 16)
            tick(dMISCG, g.dma_start(out=bpj[:, :], in_=bpj_h), 16)
            tick(dMISCG, g.dma_start(out=rts[:, :], in_=rt_h), 16)
        blk.gpsimd(_)
        misc_g = dMISCG.n

        with ExitStack() as big1:
            ln1T = big1.enter_context(
                nc.sbuf_tensor("ln1T", [128, 16 * 2048], BF16, is_transpose_dest=True))

            v_all = big1.enter_context(nc.sbuf_tensor("v_all", [128, 16 * 1024], BF16))
            with ExitStack() as sub1a:
                x_t = [sub1a.enter_context(nc.sbuf_tensor(f"x_t{i}", [128, 2048], F32))
                       for i in range(4)]
                l_t = [sub1a.enter_context(nc.sbuf_tensor(f"l_t{i}", [128, 2048], BF16))
                       for i in range(4)]
                wvw = [sub1a.enter_context(nc.sbuf_tensor(f"wvw{i}", [128, 8192], BF16))
                       for i in range(2)]

                def _(g):
                    tick(dWV, g.dma_start(out=wvw[0][:, :], in_=wv_h[0]), 16)
                    tick(dWV, g.dma_start(out=wvw[1][:, :], in_=wv_h[1]), 16)
                blk.gpsimd(_)
                wv_done = dWV.n

                s_app = {}
                tr_done = {}
                v_ev = {}
                rc_vs = {}

                def emitA(i):
                    sl = i % 4

                    def _(sy, i=i, sl=sl):
                        if i >= 4:
                            sy.wait_ge(sS.h, s_app[i - 4])
                        tick(dX[sl], sy.dma_start(
                            out=x_t[sl][:, :], in_=x_h[i * 128:(i + 1) * 128, :]), 16)
                    blk.sync(_)

                    def _(v, i=i, sl=sl):
                        v.wait_ge(dX[sl].h, dX[sl].n)
                        if i >= 4:
                            v.wait_ge(sS.h, s_app[i - 4])
                        for cc in range(3):
                            v.bn_stats(bn6s[sl][:, cc * 6:(cc + 1) * 6],
                                       x_t[sl][:, cc * 512:(cc + 1) * 512])
                        n1 = tick(sV, v.bn_stats(bn6s[sl][:, 18:24], x_t[sl][:, 1536:2048]))
                        v.wait_ge(sV.h, n1)
                        n2 = tick(sV, v.bn_aggr(mvs[sl][:, :], bn6s[sl][:, :]))
                        v.wait_ge(sV.h, n2)
                        n3 = tick(sV, v.tensor_scalar_add(tcs[sl][:, 0:1], mvs[sl][:, 1:2], 1e-5))
                        v.wait_ge(sV.h, n3)
                        tick(sV, v.reciprocal(tcs[sl][:, 1:2], tcs[sl][:, 0:1]))
                    blk.vector(_)
                    rc_vs[i] = sV.n

                def emitB(i):
                    sl = i % 4

                    def _(s, i=i, sl=sl):
                        s.wait_ge(sV.h, rc_vs[i])
                        tick(sS, s.sqrt(acs[sl][:, :], tcs[sl][:, 1:2]))
                    blk.scalar(_)
                    sq_s = sS.n

                    def _(v, i=i, sl=sl, sq_s=sq_s):
                        v.wait_ge(sS.h, sq_s)
                        tick(sV, v.scalar_tensor_tensor(bcs[sl][:, :], mvs[sl][:, 0:1], -1.0,
                                                        acs[sl][:, :], op0=ALU.mult,
                                                        op1=ALU.mult))
                    blk.vector(_)
                    ab_v = sV.n

                    def _(s, i=i, sl=sl, ab_v=ab_v):
                        s.wait_ge(sV.h, ab_v)
                        if i >= 4:
                            s.wait_ge(dTR[sl].h, 16 * (i // 4))
                        tick(sS, s.activation(l_t[sl][:, :], x_t[sl][:, :], ACT.Identity,
                                              bias=bcs[sl][:, :], scale=acs[sl][:, :]))
                    blk.scalar(_)
                    s_app[i] = sS.n

                    def _(sy, i=i, sl=sl):
                        sy.wait_ge(sS.h, s_app[i])
                        o_v = ln1T[:, :].rearrange("p (ct tt) -> p ct tt", ct=16)[:, :, i * 128:(i + 1) * 128]
                        i_v = l_t[sl][:, :].rearrange("t (ct p) -> t ct p", ct=16)
                        tick(dTR[sl], sy.dma_start_transpose(o_v, i_v), 16)
                    blk.sync(_)
                    tr_done[i] = (sl, dTR[sl].n)

                    def _(t, i=i):
                        sl_, tgt = tr_done[i]
                        t.wait_ge(dTR[sl_].h, tgt)
                        if i == 0:
                            t.wait_ge(dMISCG.h, misc_g)
                            t.wait_ge(dWV.h, wv_done)
                            t.wait_ge(sV.h, v_ones)
                        if i >= 2:
                            t.wait_ge(sV.h, v_ev[i - 2])
                        base = (i % 2) * 2
                        for vch in range(2):
                            po = PA[:, (base + vch) * 512:(base + vch + 1) * 512]
                            t.matmul(po, ones_r[:, :], bv[:, vch * 512:(vch + 1) * 512],
                                     start=True, stop=False)
                            for kt in range(16):
                                mm = t.matmul(
                                    po,
                                    ln1T[:, kt * 2048 + i * 128: kt * 2048 + (i + 1) * 128],
                                    wvw[vch][:, kt * 512:(kt + 1) * 512],
                                    start=False, stop=(kt == 15))
                        tick(sT, mm)
                    blk.tensor(_)
                    v_mms[i] = sT.n

                def emitEv(i):
                    def _(v, i=i):
                        v.wait_ge(sT.h, v_mms[i])
                        base = (i % 2) * 2
                        tick(sV, v.tensor_copy(v_all[:, i * 1024:(i + 1) * 1024],
                                               PA[:, base * 512:(base + 2) * 512]))
                    blk.vector(_)
                    v_ev[i] = sV.n

                v_mms = {}
                for i in range(17):
                    if i < 16:
                        emitA(i)
                    if i >= 1:
                        emitB(i - 1)
                    if i >= 2:
                        emitEv(i - 2)
                emitEv(15)
                v_done_v = sV.n
                v_mm_last = sT.n

            with ExitStack() as sub3:
                yT = sub3.enter_context(nc.sbuf_tensor("yT", [128, 8 * 2048], BF16))
                expT = sub3.enter_context(nc.sbuf_tensor("expT", [128, 2048], BF16))
                msk = sub3.enter_context(nc.sbuf_tensor("msks", [128, 2048], BF16))
                qT = sub3.enter_context(nc.sbuf_tensor("qT", [128, 2048], BF16))
                kTl = sub3.enter_context(nc.sbuf_tensor("kTl", [128, 2048], BF16))
                rcp = sub3.enter_context(nc.sbuf_tensor("rcp", [128, 512], F32))
                rcpb = sub3.enter_context(nc.sbuf_tensor("rcpb", [128, 512], BF16))
                wqkw = [sub3.enter_context(nc.sbuf_tensor(f"wqkw{i}", [128, 2048], BF16))
                        for i in range(2)]

                def _(sy):
                    sy.wait_ge(sT.h, v_mm_last)
                    tick(dMSK, sy.dma_start(out=msk[:, :], in_=msk_h), 16)
                blk.sync(_)
                msk_done = dMSK.n

                qk_mm = {}
                wqk_tgt = {}

                def load_wqk(o):
                    sl = o % 2
                    m = (o >> 1) + 8 * (o & 1)

                    def _(g, o=o, sl=sl, m=m):
                        if o >= 2:
                            g.wait_ge(sT.h, qk_mm[o - 2])
                        else:
                            g.wait_ge(sT.h, v_mm_last)
                        tick(dWQK[sl], g.dma_start(out=wqkw[sl][:, :], in_=wqk_h[m]), 16)
                    blk.gpsimd(_)
                    wqk_tgt[o] = dWQK[sl].n

                load_wqk(0)
                load_wqk(1)

                paS = 0
                paV = v_done_v
                att_t_last = 0
                exp_last_s = 0
                y_ev_v = 0
                sum_rd_v = 0
                bc_ev_s = 0
                negc_prev_v = 0
                km_v = 0

                for h in range(8):
                    for which in range(2):
                        o = 2 * h + which
                        m = h if which == 0 else 8 + h
                        sl = o % 2
                        dst = qT if which == 0 else kTl

                        def _(t, o=o, sl=sl, paS=paS, paV=paV, exp_last_s=exp_last_s):
                            t.wait_ge(dWQK[sl].h, wqk_tgt[o])
                            if paS:
                                t.wait_ge(sS.h, paS)
                            if exp_last_s:
                                t.wait_ge(sS.h, exp_last_s)
                            if paV:
                                t.wait_ge(sV.h, paV)
                            for kt in range(16):
                                for cc in range(4):
                                    mm = t.matmul(
                                        PA[:, cc * 512:(cc + 1) * 512],
                                        wqkw[sl][:, kt * 128:(kt + 1) * 128],
                                        ln1T[:, kt * 2048 + cc * 512: kt * 2048 + (cc + 1) * 512],
                                        start=(kt == 0), stop=(kt == 15))
                            tick(sT, mm)
                        blk.tensor(_)
                        qk_mm[o] = sT.n

                        def _(s, o=o, m=m, dst=dst):
                            s.wait_ge(sT.h, qk_mm[o])
                            if o == 0:
                                s.wait_ge(dMISC.h, misc_done)
                            tick(sS, s.activation(dst[:, :], PA[:, :], ACT.Identity,
                                                  bias=bqk[:, m:m + 1], scale=1.0))
                        blk.scalar(_)
                        paS = sS.n
                        paV = 0
                        exp_last_s = 0
                        if o + 2 < 16:
                            load_wqk(o + 2)

                        if which == 0:
                            q_ev = sS.n

                            def _(v, h=h, q_ev=q_ev, att_t_last=att_t_last):
                                v.wait_ge(sS.h, q_ev)
                                if att_t_last:
                                    v.wait_ge(sT.h, att_t_last)
                                tick(sV, v.tensor_tensor(expT[:, :], qT[:, :], qT[:, :],
                                                         op=ALU.mult))
                            blk.vector(_)
                            sqq_v = sV.n

                            def _(t, h=h, sqq_v=sqq_v, y_ev_v=y_ev_v, sum_rd_v=sum_rd_v):
                                t.wait_ge(sV.h, sqq_v)
                                if y_ev_v:
                                    t.wait_ge(sV.h, y_ev_v)
                                if sum_rd_v:
                                    t.wait_ge(sV.h, sum_rd_v)
                                for cc in range(3):
                                    t.matmul(PB[:, cc * 512:(cc + 1) * 512], ones_f[:, :],
                                             expT[:, cc * 512:(cc + 1) * 512],
                                             start=True, stop=True)
                                tick(sT, t.matmul(PB[:, 1536:2048], ones_f[:, :],
                                                  expT[:, 1536:2048], start=True, stop=True))
                            blk.tensor(_)
                            nq_t = sT.n

                            def _(v, nq_t=nq_t):
                                v.wait_ge(sT.h, nq_t)
                                tick(sV, v.reduce_max(qkc[:, 0:1], PB[:, 0:2048], axis=AX))
                            blk.vector(_)
                            redq_v = sV.n
                    qk_ev = sS.n

                    def _(v, h=h, qk_ev=qk_ev, nq_t=nq_t):
                        v.wait_ge(sS.h, qk_ev)
                        v.wait_ge(sT.h, nq_t)
                        tick(sV, v.tensor_tensor(expT[:, :], kTl[:, :], kTl[:, :],
                                                 op=ALU.mult))
                    blk.vector(_)
                    sqk_v = sV.n

                    def _(t, h=h, sqk_v=sqk_v, redq_v=redq_v):
                        t.wait_ge(sV.h, sqk_v)
                        t.wait_ge(sV.h, redq_v)
                        for cc in range(3):
                            t.matmul(PB[:, cc * 512:(cc + 1) * 512], ones_f[:, :],
                                     expT[:, cc * 512:(cc + 1) * 512], start=True, stop=True)
                        tick(sT, t.matmul(PB[:, 1536:2048], ones_f[:, :],
                                          expT[:, 1536:2048], start=True, stop=True))
                    blk.tensor(_)
                    nk_t = sT.n

                    def _(v, h=h, nk_t=nk_t):
                        v.wait_ge(sT.h, nk_t)
                        n1 = tick(sV, v.reduce_max(qkc[:, 1:2], PB[:, 0:2048], axis=AX))
                        v.wait_ge(sV.h, n1)
                        n2 = tick(sV, v.tensor_tensor(qkc[:, 2:3], qkc[:, 0:1], qkc[:, 1:2],
                                                      op=ALU.mult))
                        v.wait_ge(sV.h, n2)
                        tick(sV, v.tensor_scalar_mul(qkc[:, 2:3], qkc[:, 2:3], 1.0 / 128.0))
                    blk.vector(_)
                    km_v = sV.n

                    def _(s, h=h, km_v=km_v):
                        s.wait_ge(sV.h, km_v)
                        tick(sS, s.sqrt(qkc[:, 3:4], qkc[:, 2:3]))
                    blk.scalar(_)
                    cqs_s = sS.n

                    def _(v, h=h, cqs_s=cqs_s):
                        v.wait_ge(sS.h, cqs_s)
                        tick(sV, v.tensor_scalar_mul(negc[:, h:h + 1], qkc[:, 3:4], -1.0))
                    blk.vector(_)
                    negc_s = sV.n

                    exp_s = {0: 0, 1: 0}
                    av_t = {0: att_t_last, 1: att_t_last}
                    for cc4 in range(4):
                        nk = 4 * (cc4 + 1)
                        npair = nk // 2
                        for p in range(npair):
                            slot = p % 2

                            def _(t, h=h, cc4=cc4, p=p, slot=slot, km_v=km_v):
                                if exp_s[slot]:
                                    t.wait_ge(sS.h, exp_s[slot])
                                if cc4 == 0 and p < 2:
                                    t.wait_ge(sV.h, km_v)
                                for q in range(2):
                                    kt = 2 * p + q
                                    bank = 2 * slot + q
                                    mm = t.matmul(PA[:, bank * 512:(bank + 1) * 512],
                                                  kTl[:, kt * 128:(kt + 1) * 128],
                                                  qT[:, cc4 * 512:(cc4 + 1) * 512],
                                                  start=True, stop=True)
                                tick(sT, mm)
                            blk.tensor(_)
                            sc_t = sT.n

                            def _(s, h=h, slot=slot, sc_t=sc_t, negc_s=negc_s):
                                s.wait_ge(sT.h, sc_t)
                                s.wait_ge(sV.h, negc_s)
                                if av_t[slot]:
                                    s.wait_ge(sT.h, av_t[slot])
                                tick(sS, s.activation(expT[:, slot * 1024:(slot + 1) * 1024],
                                                      PA[:, slot * 1024:(slot + 1) * 1024],
                                                      ACT.Exp, bias=negc[:, h:h + 1],
                                                      scale=SCL))
                            blk.scalar(_)
                            exp_s[slot] = sS.n
                            cur_exp = sS.n

                            mv_cur = 0
                            if p >= 2 * cc4:
                                moff = (p - 2 * cc4) * 1024
                                def _(v, cur_exp=cur_exp, h=h, cc4=cc4, slot=slot,
                                      moff=moff):
                                    if h == 0 and cc4 == 0 and moff == 0:
                                        v.wait_ge(dMSK.h, msk_done)
                                    v.wait_ge(sS.h, cur_exp)
                                    sl_ap = expT[:, slot * 1024:(slot + 1) * 1024]
                                    tick(sV, v.tensor_tensor(sl_ap, sl_ap,
                                                             msk[:, moff:moff + 1024],
                                                             op=ALU.mult))
                                blk.vector(_)
                                mv_cur = sV.n

                            def _(t, h=h, cc4=cc4, p=p, slot=slot, nk=nk,
                                  cur_exp=cur_exp, mv_cur=mv_cur,
                                  y_ev_v=y_ev_v, sum_rd_v=sum_rd_v):
                                t.wait_ge(sS.h, cur_exp)
                                if mv_cur:
                                    t.wait_ge(sV.h, mv_cur)
                                if p == 0:
                                    if y_ev_v:
                                        t.wait_ge(sV.h, y_ev_v)
                                    if sum_rd_v:
                                        t.wait_ge(sV.h, sum_rd_v)
                                for q in range(2):
                                    kt = 2 * p + q
                                    rhs = expT[:, slot * 1024 + q * 512: slot * 1024 + (q + 1) * 512]
                                    t.matmul(PB[:, 512:1024],
                                             v_all[:, kt * 1024 + h * 128: kt * 1024 + (h + 1) * 128],
                                             rhs, start=(kt == 0), stop=(kt == nk - 1))
                                    mm = t.matmul(PB[:, 1024:1536], ones_f[:, :], rhs,
                                                  start=(kt == 0), stop=(kt == nk - 1))
                                tick(sT, mm)
                            blk.tensor(_)
                            av_t[slot] = sT.n
                        av_fin = sT.n

                        def _(v, av_fin=av_fin):
                            v.wait_ge(sT.h, av_fin)
                            n1 = tick(sV, v.reciprocal(rcp[:, :], PB[:, 1024:1536]))
                            v.wait_ge(sV.h, n1)
                            tick(sV, v.tensor_copy(rcpb[:, :], rcp[:, :]))
                        blk.vector(_)
                        sum_rd_v = sV.n

                        def _(v, h=h, cc4=cc4, sum_rd_v=sum_rd_v):
                            v.wait_ge(sV.h, sum_rd_v)
                            tick(sV, v.tensor_tensor(
                                yT[:, h * 2048 + cc4 * 512: h * 2048 + (cc4 + 1) * 512],
                                PB[:, 512:1024], rcpb[:, :], op=ALU.mult))
                        blk.vector(_)
                        y_ev_v = sV.n
                    att_t_last = sT.n
                    exp_last_s = max(exp_s[0], exp_s[1])
                    paS = 0
                    paV = 0
                y_done_v = sV.n

                def _(g):
                    g.wait_ge(sV.h, y_done_v)
                    o_v = cc_in.rearrange("(hh p) t -> p hh t", p=128)
                    i_v = yT[:, :].rearrange("p (hh t) -> p hh t", hh=8)
                    tick(dCCI, g.dma_start(out=o_v, in_=i_v), 16)
                    g.wait_ge(dCCI.h, dCCI.n)
                    tick(cS, g.collective_compute(
                        "AllGather", ALU.bypass,
                        replica_groups=[[0, 1, 2, 3, 4, 5, 6, 7]],
                        ins=[cc_in], outs=[cc_out]))
                blk.gpsimd(_)

        x1 = ctx0.enter_context(nc.sbuf_tensor("x1", [128, 8 * 2048], F32))
        with ExitStack() as big2:
            wpj = big2.enter_context(nc.sbuf_tensor("wpjs", [128, 16 * 2048], BF16))
            y_all = big2.enter_context(nc.sbuf_tensor("y_all", [128, 16 * 1024], BF16))

            def _(sy):
                sy.wait_ge(sV.h, y_done_v)
                sy.wait_ge(dCCI.h, 16)
                tick(dXO, sy.dma_start(
                    out=x1[:, :].rearrange("p (tt c) -> p tt c", tt=8),
                    in_=xo_h.rearrange("(tt p) c -> p tt c", p=128)), 16)
            blk.sync(_)
            xo_done = dXO.n

            def _(g):
                for yf in range(16):
                    tick(dWPJ, g.dma_start(out=wpj[:, yf * 2048:(yf + 1) * 2048],
                                           in_=wpj_h[:, yf, :]), 16)
            blk.gpsimd(_)
            wpj_done = dWPJ.n

            def _(g):
                g.wait_ge(cS.h, cS.n)
                pid = nc.partition_id(engines=[mybir.EngineType.Pool])
                j = pid % 2
                base = pid - j
                for half in range(2):
                    src = cc_out[bass.ts(base + half, 1024), :][:, bass.ts(j, 1024)]
                    src = src.rearrange("(hh p) t -> p hh t", p=128)
                    dst = y_all[:, half * 8192:(half + 1) * 8192].rearrange(
                        "p (hh t) -> p hh t", hh=8)
                    tick(dCCO, g.dma_start(out=dst, in_=src), 16)
            blk.gpsimd(_)
            cco_done = dCCO.n

            pj_mm = {}
            pj_ev = {}
            for tt in range(8):
                PX = PA if tt % 2 == 0 else PB

                def _(t, tt=tt, PX=PX):
                    if tt == 0:
                        t.wait_ge(dCCO.h, cco_done)
                        t.wait_ge(dWPJ.h, wpj_done)
                        t.wait_ge(dMISCG.h, misc_g)
                    if tt >= 2:
                        t.wait_ge(sV.h, pj_ev[tt - 2])
                    for dq in range(4):
                        t.matmul(PX[:, dq * 512:(dq + 1) * 512], ones_r[:, :],
                                 bpj[:, dq * 512:(dq + 1) * 512], start=True, stop=False)
                    for yf in range(16):
                        for dq in range(4):
                            mm = t.matmul(
                                PX[:, dq * 512:(dq + 1) * 512],
                                y_all[:, yf * 1024 + tt * 128: yf * 1024 + (tt + 1) * 128],
                                wpj[:, yf * 2048 + dq * 512: yf * 2048 + (dq + 1) * 512],
                                start=False, stop=(yf == 15))
                    tick(sT, mm)
                blk.tensor(_)
                pj_mm[tt] = sT.n

                def _(v, tt=tt, PX=PX):
                    v.wait_ge(sT.h, pj_mm[tt])
                    if tt == 0:
                        v.wait_ge(dXO.h, xo_done)
                    for dq in range(3):
                        sl = x1[:, tt * 2048 + dq * 512: tt * 2048 + (dq + 1) * 512]
                        v.tensor_tensor(sl, PX[:, dq * 512:(dq + 1) * 512], sl, op=ALU.add)
                    sl = x1[:, tt * 2048 + 1536: tt * 2048 + 2048]
                    tick(sV, v.tensor_tensor(sl, PX[:, 1536:2048], sl, op=ALU.add))
                blk.vector(_)
                pj_ev[tt] = sV.n
            proj_done_v = sV.n

        with ExitStack() as big3:
            hT = big3.enter_context(
                nc.sbuf_tensor("hT", [128, 16 * 1024], BF16, is_transpose_dest=True))
            su = [big3.enter_context(nc.sbuf_tensor(f"su{i}", [128, 8 * 1024], BF16))
                  for i in range(2)]
            guw = [big3.enter_context(nc.sbuf_tensor(f"guw{i}", [128, 4096], BF16))
                   for i in range(2)]
            dnw = [big3.enter_context(nc.sbuf_tensor(f"dnw{i}", [128, 4096], BF16))
                   for i in range(4)]
            h_t = [big3.enter_context(nc.sbuf_tensor(f"h_t{i}", [128, 2048], BF16))
                   for i in range(2)]
            silu_sb = big3.enter_context(nc.sbuf_tensor("silu_sb", [128, 1024], BF16))
            lg_sb = big3.enter_context(nc.sbuf_tensor("lg_sb", [8, 1024], F32))

            s_h = {}
            rc_vs2 = {}

            def emitA2(i):
                sl = i % 4

                def _(v, i=i, sl=sl):
                    v.wait_ge(sV.h, pj_ev[i])
                    if i == 0:
                        v.wait_ge(sT.h, pj_mm[7])
                    if i >= 4:
                        v.wait_ge(sS.h, s_h[i - 4])
                    for cc in range(3):
                        v.bn_stats(bn6s[sl][:, cc * 6:(cc + 1) * 6],
                                   x1[:, i * 2048 + cc * 512: i * 2048 + (cc + 1) * 512])
                    n1 = tick(sV, v.bn_stats(bn6s[sl][:, 18:24],
                                             x1[:, i * 2048 + 1536: i * 2048 + 2048]))
                    v.wait_ge(sV.h, n1)
                    n2 = tick(sV, v.bn_aggr(mvs[sl][:, :], bn6s[sl][:, :]))
                    v.wait_ge(sV.h, n2)
                    n3 = tick(sV, v.tensor_scalar(tcs[sl][:, 0:1], mvs[sl][:, 1:2],
                                                  1.000001, 1e-11,
                                                  op0=ALU.mult, op1=ALU.add))
                    v.wait_ge(sV.h, n3)
                    tick(sV, v.reciprocal(tcs[sl][:, 1:2], tcs[sl][:, 0:1]))
                blk.vector(_)
                rc_vs2[i] = sV.n

            def emitB2(i):
                sl = i % 4
                hsl = i % 2

                def _(s, i=i, sl=sl):
                    s.wait_ge(sV.h, rc_vs2[i])
                    if i == 0:
                        s.wait_ge(sT.h, pj_mm[7])
                    tick(sS, s.sqrt(acs[sl][:, :], tcs[sl][:, 1:2]))
                blk.scalar(_)
                sq_s = sS.n

                def _(v, i=i, sl=sl, sq_s=sq_s):
                    v.wait_ge(sS.h, sq_s)
                    tick(sV, v.scalar_tensor_tensor(bcs[sl][:, :], mvs[sl][:, 0:1], -1.0,
                                                    acs[sl][:, :], op0=ALU.mult, op1=ALU.mult))
                blk.vector(_)
                ab_v = sV.n

                def _(s, i=i, sl=sl, hsl=hsl, ab_v=ab_v):
                    s.wait_ge(sV.h, ab_v)
                    if i >= 2:
                        s.wait_ge(dHT[hsl].h, 16 * (i // 2))
                    tick(sS, s.activation(h_t[hsl][:, :], x1[:, i * 2048:(i + 1) * 2048],
                                          ACT.Identity, bias=bcs[sl][:, :], scale=acs[sl][:, :]))
                blk.scalar(_)
                s_h[i] = sS.n

                def _(sy, i=i, hsl=hsl):
                    sy.wait_ge(sS.h, s_h[i])
                    if i == 0:
                        sy.wait_ge(sT.h, pj_mm[7])
                    o_v = hT[:, :].rearrange("p (dt tt) -> p dt tt", dt=16)[:, :, i * 128:(i + 1) * 128]
                    i_v = h_t[hsl][:, :].rearrange("t (dt p) -> t dt p", dt=16)
                    tick(dHT[hsl], sy.dma_start_transpose(o_v, i_v), 16)
                blk.sync(_)

            for i in range(9):
                if i < 8:
                    emitA2(i)
                if i >= 1:
                    emitB2(i - 1)
            ht_done = tuple(dHT[i].n for i in range(4))

            def _(s):
                for i in range(4):
                    s.wait_ge(dHT[i].h, ht_done[i])
                s.wait_ge(dMISC.h, misc_done)
                for dt in range(15):
                    s.activation(hT[:, dt * 1024:(dt + 1) * 1024],
                                 hT[:, dt * 1024:(dt + 1) * 1024],
                                 ACT.Identity, bias=0.0, scale=rmsw[:, dt:dt + 1])
                tick(sS, s.activation(hT[:, 15 * 1024:16 * 1024], hT[:, 15 * 1024:16 * 1024],
                                      ACT.Identity, bias=0.0, scale=rmsw[:, 15:16]))
            blk.scalar(_)
            hT_ready = sS.n

            def _(t):
                t.wait_ge(sS.h, hT_ready)
                t.wait_ge(dMISCG.h, misc_g)
                for cc in range(2):
                    for kt in range(16):
                        mm = t.matmul(PA[0:8, cc * 512:(cc + 1) * 512],
                                      rts[:, kt * 8:(kt + 1) * 8],
                                      hT[:, kt * 1024 + cc * 512: kt * 1024 + (cc + 1) * 512],
                                      start=(kt == 0), stop=(kt == 15))
                tick(sT, mm)
            blk.tensor(_)
            lg_t = sT.n

            def _(v):
                v.wait_ge(sT.h, lg_t)
                tick(sV, v.tensor_copy(lg_sb[:, :], PA[0:8, 0:1024]))
            blk.vector(_)
            lg_v = sV.n

            def _(t):
                t.wait_ge(sV.h, lg_v)
                t.wait_ge(dMISC.h, misc_done)
                for i in range(7):
                    t.transpose(PB[:, i * 8:(i + 1) * 8], lg_sb[:, i * 128:(i + 1) * 128],
                                id32[0:8, 0:8])
                tick(sT, t.transpose(PB[:, 56:64], lg_sb[:, 896:1024], id32[0:8, 0:8]))
            blk.tensor(_)
            lgT_t = sT.n

            def _(v):
                v.wait_ge(sT.h, lgT_t)
                n1 = tick(sV, v.tensor_copy(lgT[:, :], PB[:, 0:64]))
                v.wait_ge(sV.h, n1)
                for i in range(7):
                    v.reduce_max(mcol[:, i:i + 1], lgT[:, i * 8:(i + 1) * 8], axis=AX)
                n2 = tick(sV, v.reduce_max(mcol[:, 7:8], lgT[:, 56:64], axis=AX))
                v.wait_ge(sV.h, n2)
                tick(sV, v.tensor_scalar_mul(mneg[:, :], mcol[:, :], -1.0))
            blk.vector(_)
            m1_v = sV.n

            def _(s):
                s.wait_ge(sV.h, m1_v)
                for i in range(7):
                    s.activation(e_sb[:, i * 8:(i + 1) * 8], lgT[:, i * 8:(i + 1) * 8],
                                 ACT.Exp, bias=mneg[:, i:i + 1], scale=1.0)
                tick(sS, s.activation(e_sb[:, 56:64], lgT[:, 56:64],
                                      ACT.Exp, bias=mneg[:, 7:8], scale=1.0))
            blk.scalar(_)
            e_s = sS.n

            def _(v):
                v.wait_ge(sS.h, e_s)
                n1 = tick(sV, v.tensor_scalar(msel[:, :], e_sb[:, :], 1.0, -1.0,
                                              op0=ALU.is_ge, op1=ALU.mult))
                v.wait_ge(sV.h, n1)
                n2 = tick(sV, v.tensor_scalar_add(msel[:, :], msel[:, :], 1.0))
                v.wait_ge(sV.h, n2)
                n3 = tick(sV, v.tensor_tensor(msel[:, :], e_sb[:, :], msel[:, :], op=ALU.mult))
                v.wait_ge(sV.h, n3)
                for i in range(7):
                    v.reduce_max(mcol[:, i:i + 1], msel[:, i * 8:(i + 1) * 8], axis=AX)
                n4 = tick(sV, v.reduce_max(mcol[:, 7:8], msel[:, 56:64], axis=AX))
                v.wait_ge(sV.h, n4)
                for i in range(7):
                    v.tensor_scalar(wd_tok[:, i * 8:(i + 1) * 8], e_sb[:, i * 8:(i + 1) * 8],
                                    mcol[:, i:i + 1], None, op0=ALU.is_ge)
                n5 = tick(sV, v.tensor_scalar(wd_tok[:, 56:64], e_sb[:, 56:64],
                                              mcol[:, 7:8], None, op0=ALU.is_ge))
                v.wait_ge(sV.h, n5)
                n6 = tick(sV, v.tensor_tensor(wd_tok[:, :], e_sb[:, :], wd_tok[:, :], op=ALU.mult))
                v.wait_ge(sV.h, n6)
                n7 = tick(sV, v.tensor_scalar_add(mcol[:, :], mcol[:, :], 1.0))
                v.wait_ge(sV.h, n7)
                tick(sV, v.reciprocal(mcol[:, :], mcol[:, :]))
            blk.vector(_)
            w2_v = sV.n

            def _(s):
                s.wait_ge(sV.h, w2_v)
                for i in range(7):
                    s.activation(wd_tok[:, i * 8:(i + 1) * 8], wd_tok[:, i * 8:(i + 1) * 8],
                                 ACT.Identity, bias=0.0, scale=mcol[:, i:i + 1])
                tick(sS, s.activation(wd_tok[:, 56:64], wd_tok[:, 56:64],
                                      ACT.Identity, bias=0.0, scale=mcol[:, 7:8]))
            blk.scalar(_)
            wd_s = sS.n

            gu_mm = {}
            dn_mm = {}
            gu_tgt = {}
            dn_tgt = {}
            su_v = {}
            silu_s = {}
            ev_v = {}
            wdT_done = 0
            wdb_v = 0
            wdb_cp_v = 0

            def load_gu(idx, ft):
                e = EORD[idx]
                sl = ft % 2

                def _(g, idx=idx, e=e, ft=ft, sl=sl):
                    if (idx, ft) == (0, 0):
                        g.wait_ge(sT.h, pj_mm[7])
                    prev = (EORD[idx], ft - 2) if ft >= 2 else \
                        ((EORD[idx - 1], ft + 6) if idx >= 1 else None)
                    if prev in gu_mm:
                        g.wait_ge(sT.h, gu_mm[prev])
                    tick(dGU[sl], g.dma_start(out=guw[sl][:, 0:2048], in_=gu_h[e, ft, 0]), 16)
                    tick(dGU[sl], g.dma_start(out=guw[sl][:, 2048:4096], in_=gu_h[e, ft, 1]), 16)
                blk.gpsimd(_)
                gu_tgt[(e, ft)] = dGU[sl].n

            def load_dn(idx, dq):
                e = EORD[idx]

                def _(g, idx=idx, e=e, dq=dq):
                    if (idx, dq) == (0, 0):
                        g.wait_ge(sT.h, pj_mm[7])
                    if idx >= 1:
                        prev = (EORD[idx - 1], dq // 2)
                        if prev in dn_mm:
                            g.wait_ge(sT.h, dn_mm[prev])
                    tick(dDN[dq], g.dma_start(out=dnw[dq][:, :], in_=dn_h[e, dq]), 16)
                blk.gpsimd(_)
                dn_tgt[(e, dq)] = dDN[dq].n

            load_gu(0, 0)
            load_gu(0, 1)
            for dq in range(4):
                load_dn(0, dq)

            for idx in range(NE):
                e = EORD[idx]
                esl = idx % 2

                if e != 8:
                    def _(t, e=e, idx=idx, wdb_v=wdb_v):
                        t.wait_ge(sV.h, wdb_v)
                        t.wait_ge(sV.h, ev_v[(EORD[idx - 1], 1, 3)])
                        t.matmul(PB[:, 0:512], selb[:, e * 128:(e + 1) * 128],
                                 wd8[:, 0:512], start=True, stop=True)
                        tick(sT, t.matmul(PB[:, 512:1024], selb[:, e * 128:(e + 1) * 128],
                                          wd8[:, 512:1024], start=True, stop=True))
                    blk.tensor(_)
                    wdb_t = sT.n

                    def _(v, wdb_t=wdb_t):
                        v.wait_ge(sT.h, wdb_t)
                        tick(sV, v.tensor_copy(wdB[:, :], PB[:, 0:1024]))
                    blk.vector(_)
                    wdb_cp_v = sV.n

                for ft in range(FT):
                    sl = ft % 2

                    def _(t, idx=idx, e=e, ft=ft, sl=sl):
                        t.wait_ge(dGU[sl].h, gu_tgt[(e, ft)])
                        if (idx, ft) == (0, 0):
                            t.wait_ge(sS.h, hT_ready)
                            t.wait_ge(sV.h, lg_v)
                        prevk = (e, ft - 1) if ft >= 1 else \
                            ((EORD[idx - 1], 7) if idx >= 1 else None)
                        if prevk in su_v:
                            t.wait_ge(sV.h, su_v[prevk])
                        if ft == 0 and idx >= 1:
                            t.wait_ge(sV.h, ev_v[(EORD[idx - 1], 1, 3)])
                        for gui in range(2):
                            for kt in range(16):
                                for cc in range(2):
                                    bank = 2 * gui + cc
                                    mm = t.matmul(
                                        PA[:, bank * 512:(bank + 1) * 512],
                                        guw[sl][:, gui * 2048 + kt * 128: gui * 2048 + (kt + 1) * 128],
                                        hT[:, kt * 1024 + cc * 512: kt * 1024 + (cc + 1) * 512],
                                        start=(kt == 0), stop=(kt == 15))
                        tick(sT, mm)
                    blk.tensor(_)
                    gu_mm[(e, ft)] = sT.n

                    def _(s, e=e, ft=ft, idx=idx):
                        s.wait_ge(sT.h, gu_mm[(e, ft)])
                        prevk = (e, ft - 1) if ft >= 1 else \
                            ((EORD[idx - 1], 7) if idx >= 1 else None)
                        if prevk in su_v:
                            s.wait_ge(sV.h, su_v[prevk])
                        tick(sS, s.activation(silu_sb[:, :], PA[:, 0:1024], ACT.Silu,
                                              bias=0.0, scale=1.0))
                    blk.scalar(_)
                    silu_s[(e, ft)] = sS.n

                    def _(v, e=e, ft=ft, idx=idx, esl=esl, wdb_cp_v=wdb_cp_v):
                        v.wait_ge(sS.h, silu_s[(e, ft)])
                        if e != 8 and ft == 0:
                            v.wait_ge(sV.h, wdb_cp_v)
                        if ft == 0 and idx >= 2:
                            v.wait_ge(sT.h, dn_mm[(EORD[idx - 2], 1)])
                        for cc in range(2):
                            ap = su[esl][:, ft * 1024 + cc * 512: ft * 1024 + (cc + 1) * 512]
                            ii = v.tensor_tensor(ap, silu_sb[:, cc * 512:(cc + 1) * 512],
                                                 PA[:, (2 + cc) * 512:(3 + cc) * 512],
                                                 op=ALU.mult)
                        n1 = tick(sV, ii)
                        if e != 8:
                            v.wait_ge(sV.h, n1)
                            for cc in range(2):
                                ap = su[esl][:, ft * 1024 + cc * 512: ft * 1024 + (cc + 1) * 512]
                                ii = v.tensor_tensor(ap, ap, wdB[:, cc * 512:(cc + 1) * 512],
                                                     op=ALU.mult)
                            tick(sV, ii)
                    blk.vector(_)
                    su_v[(e, ft)] = sV.n

                    nxt = (idx, ft + 2) if ft + 2 < FT else (idx + 1, ft + 2 - FT)
                    if nxt[0] < NE:
                        load_gu(*nxt)

                if idx == 0:
                    def _(t):
                        t.wait_ge(sS.h, wd_s)
                        for i in range(7):
                            t.transpose(PB[0:8, 512 + i * 128: 512 + (i + 1) * 128],
                                        wd_tok[:, i * 8:(i + 1) * 8], id32[:, :])
                        tick(sT, t.transpose(PB[0:8, 1408:1536], wd_tok[:, 56:64], id32[:, :]))
                    blk.tensor(_)
                    wdT_done = sT.n

                    def _(v):
                        v.wait_ge(sT.h, wdT_done)
                        tick(sV, v.tensor_copy(wd8[:, :], PB[0:8, 512:1536]))
                    blk.vector(_)
                    wdb_v = sV.n

                for dqp in range(2):
                    for tq in range(4):
                        PX = PB if tq % 2 == 0 else PA

                        def _(t, idx=idx, e=e, dqp=dqp, tq=tq, PX=PX, wdb_v=wdb_v):
                            if tq == 0:
                                t.wait_ge(dDN[2 * dqp].h, dn_tgt[(e, 2 * dqp)])
                                t.wait_ge(dDN[2 * dqp + 1].h, dn_tgt[(e, 2 * dqp + 1)])
                                t.wait_ge(sV.h, su_v[(e, 7)])
                            if idx == 0 and dqp == 0 and tq < 2:
                                t.wait_ge(sV.h, wdb_v)
                            if tq >= 2:
                                prev = (e, dqp, tq - 2)
                            elif dqp == 1:
                                prev = (e, 0, tq + 2)
                            elif idx >= 1:
                                prev = (EORD[idx - 1], 1, tq + 2)
                            else:
                                prev = None
                            if prev in ev_v:
                                t.wait_ge(sV.h, ev_v[prev])
                            for ft in range(FT):
                                for q in range(2):
                                    tt = tq * 2 + q
                                    for dqi in range(2):
                                        bank = q * 2 + dqi
                                        mm = t.matmul(
                                            PX[:, bank * 512:(bank + 1) * 512],
                                            su[idx % 2][:, ft * 1024 + tt * 128: ft * 1024 + (tt + 1) * 128],
                                            dnw[2 * dqp + dqi][:, ft * 512:(ft + 1) * 512],
                                            start=(ft == 0), stop=(ft == FT - 1))
                            tick(sT, mm)
                        blk.tensor(_)
                        dn_mm[(e, dqp, tq)] = sT.n
                        if tq == 3:
                            dn_mm[(e, dqp)] = sT.n

                        def _(v, idx=idx, e=e, dqp=dqp, tq=tq, PX=PX):
                            v.wait_ge(sT.h, dn_mm[(e, dqp, tq)])
                            if idx == 0 and dqp == 0 and tq == 0:
                                v.wait_ge(sV.h, proj_done_v)
                                v.wait_ge(sS.h, hT_ready)
                            for q in range(2):
                                tt = tq * 2 + q
                                for dqi in range(2):
                                    dq = 2 * dqp + dqi
                                    bank = q * 2 + dqi
                                    ap = x1[:, tt * 2048 + dq * 512: tt * 2048 + (dq + 1) * 512]
                                    ii = v.tensor_tensor(ap, PX[:, bank * 512:(bank + 1) * 512],
                                                         ap, op=ALU.add)
                            tick(sV, ii)
                        blk.vector(_)
                        ev_v[(e, dqp, tq)] = sV.n
                if idx + 1 < NE:
                    for dq in range(4):
                        load_dn(idx + 1, dq)
            ffn_done_v = sV.n

            def _(sy):
                sy.wait_ge(sV.h, ffn_done_v)
                tick(dOUT, sy.dma_start(
                    out=out_h.rearrange("(tt p) c -> p tt c", p=128),
                    in_=x1[:, :].rearrange("p (tt c) -> p tt c", tt=8)), 16)
                sy.wait_ge(dOUT.h, dOUT.n)
            blk.sync(_)

            def _(g):
                g.wait_ge(dOUT.h, dOUT.n)
            blk.gpsimd(_)

    return nc


def _pack_core(inp, c):
    b, j = c // 2, c % 2
    wa = np.asarray(inp["L0_attn_w_attn"])
    qb, kb, vb = 1024 * j, 2048 + 1024 * j, 4096 + 1024 * j
    d = {}
    x = np.asarray(inp["L0_attn_x"])
    d["x"] = np.ascontiguousarray(x[b])
    d["xo"] = np.ascontiguousarray(x[b][j * 1024:(j + 1) * 1024])
    qk = np.concatenate([wa[:, qb:qb + 1024], wa[:, kb:kb + 1024]], axis=1)
    d["wqk"] = np.ascontiguousarray(
        qk.reshape(16, 128, 16, 128).transpose(2, 1, 0, 3).reshape(16, 128, 2048))
    wv = wa[:, vb:vb + 1024]
    d["wv"] = np.ascontiguousarray(
        wv.reshape(16, 128, 2, 512).transpose(2, 1, 0, 3).reshape(2, 128, 8192))
    wp = np.asarray(inp["L0_attn_w_proj"])
    d["wpj"] = np.ascontiguousarray(wp.reshape(16, 128, 2048).transpose(1, 0, 2))
    ba = np.asarray(inp["L0_attn_b_attn"])
    d["bqk"] = np.ascontiguousarray(
        np.concatenate([ba[qb:qb + 1024], ba[kb:kb + 1024]]).reshape(16, 128).T)
    d["bv"] = np.ascontiguousarray(ba[vb:vb + 1024].reshape(1, 1024))
    d["bpj"] = np.ascontiguousarray(np.asarray(inp["L0_attn_b_proj"]).reshape(1, 2048))
    rw = np.asarray(inp["L0_ffn_router_w"])
    d["rt"] = np.ascontiguousarray(rw.reshape(16, 128, 8).transpose(1, 0, 2).reshape(128, 128))
    d["rmsw"] = np.ascontiguousarray(np.asarray(inp["L0_ffn_rms_weight"]).reshape(16, 128).T)
    wg = np.asarray(inp["L0_ffn_w_gate"])
    wu = np.asarray(inp["L0_ffn_w_up"])
    wd = np.asarray(inp["L0_ffn_w_down"])
    swg = np.asarray(inp["L0_ffn_sw_gate"])
    swu = np.asarray(inp["L0_ffn_sw_up"])
    swd = np.asarray(inp["L0_ffn_sw_down"])
    gu = np.empty((NE, FT, 2, 128, 2048), np.float32)
    for e in range(NE):
        g = wg[e] if e < 8 else swg
        u = wu[e] if e < 8 else swu
        gu[e, :, 0] = g.reshape(16, 128, 8, 128).transpose(2, 1, 0, 3).reshape(8, 128, 2048)
        gu[e, :, 1] = u.reshape(16, 128, 8, 128).transpose(2, 1, 0, 3).reshape(8, 128, 2048)
    d["gu"] = np.ascontiguousarray(gu)
    dn = np.empty((NE, 4, 128, 4096), np.float32)
    for e in range(NE):
        D = wd[e] if e < 8 else swd
        dn[e] = D.reshape(8, 128, 4, 512).transpose(2, 1, 0, 3).reshape(4, 128, 4096)
    d["dn"] = np.ascontiguousarray(dn)
    i_idx = np.arange(128)[:, None]
    u_idx = np.arange(512)[None, :]
    mskb = np.empty((128, 2048), BF)
    for s in range(4):
        mskb[:, s * 512:(s + 1) * 512] = (u_idx >= i_idx + 128 * s).astype(BF)
    d["msk"] = mskb
    d["id32"] = np.eye(128, dtype=np.float32)
    selb = np.zeros((8, 1024), BF)
    for e in range(8):
        selb[e, e * 128:(e + 1) * 128] = 1
    d["selb"] = selb
    return d


def kernel(**inputs):
    in_maps = [_pack_core(inputs, c) for c in range(8)]
    nc = _build()
    res = run_bass_kernel_spmd(nc, in_maps, core_ids=list(range(8)))
    out = np.empty((4, 2048, 2048), np.float32)
    for c in range(8):
        b, j = c // 2, c % 2
        out[b, j * 1024:(j + 1) * 1024, :] = res.results[c]["out"]
    return out


# version 08
# speedup vs baseline: 1.0101x; 1.0101x over previous
import math
import numpy as np
import ml_dtypes

import concourse.bass as bass
import concourse.mybir as mybir
from concourse.bass_utils import run_bass_kernel_spmd
from contextlib import ExitStack

F32 = mybir.dt.float32
BF16 = mybir.dt.bfloat16
AX = mybir.AxisListType.X
ALU = mybir.AluOpType
ACT = mybir.ActivationFunctionType
BF = ml_dtypes.bfloat16

NE = 9
FT = 8
SCL = 1.0 / math.sqrt(128.0)
EORD = [8, 0, 1, 2, 3, 4, 5, 6, 7]


class Ctr:
    def __init__(self, h):
        self.h = h
        self.n = 0


def tick(ctr, inst, k=1):
    inst.then_inc(ctr.h, k)
    ctr.n += k
    return ctr.n


def _build():
    nc = bass.Bass("TRN2", target_bir_lowering=False, debug=False)

    x_h = nc.dram_tensor("x", [2048, 2048], F32, kind="ExternalInput").ap()
    xo_h = nc.dram_tensor("xo", [1024, 2048], F32, kind="ExternalInput").ap()
    wqk_h = nc.dram_tensor("wqk", [16, 128, 2048], F32, kind="ExternalInput").ap()
    wv_h = nc.dram_tensor("wv", [2, 128, 8192], F32, kind="ExternalInput").ap()
    wpj_h = nc.dram_tensor("wpj", [128, 16, 2048], F32, kind="ExternalInput").ap()
    bqk_h = nc.dram_tensor("bqk", [128, 16], F32, kind="ExternalInput").ap()
    bv_h = nc.dram_tensor("bv", [1, 1024], F32, kind="ExternalInput").ap()
    bpj_h = nc.dram_tensor("bpj", [1, 2048], F32, kind="ExternalInput").ap()
    rt_h = nc.dram_tensor("rt", [128, 128], F32, kind="ExternalInput").ap()
    rmsw_h = nc.dram_tensor("rmsw", [128, 16], F32, kind="ExternalInput").ap()
    gu_h = nc.dram_tensor("gu", [NE, FT, 2, 128, 2048], F32, kind="ExternalInput").ap()
    dn_h = nc.dram_tensor("dn", [NE, 4, 128, 4096], F32, kind="ExternalInput").ap()
    msk_h = nc.dram_tensor("msk", [128, 2048], BF16, kind="ExternalInput").ap()
    selb_h = nc.dram_tensor("selb", [8, 1024], BF16, kind="ExternalInput").ap()
    id32_h = nc.dram_tensor("id32", [128, 128], F32, kind="ExternalInput").ap()
    out_h = nc.dram_tensor("out", [1024, 2048], F32, kind="ExternalOutput").ap()

    wu_in = nc.dram_tensor("wu_in", [8, 128], BF16).ap()
    wu_out = nc.dram_tensor("wu_out", [64, 128], BF16, addr_space="Shared").ap()
    cc_in = nc.dram_tensor("cc_in", [1024, 2048], BF16).ap()
    cc_out = nc.dram_tensor("cc_out", [8192, 2048], BF16, addr_space="Shared").ap()

    with ExitStack() as ctx0:
        def mksem(name):
            return Ctr(ctx0.enter_context(nc.semaphore(name)))

        sT, sV, sS = mksem("sT"), mksem("sV"), mksem("sS")
        dX = [mksem(f"dX{i}") for i in range(4)]
        dTR = [mksem(f"dTR{i}") for i in range(4)]
        dWQK = [mksem("dWQK0"), mksem("dWQK1")]
        dWV = mksem("dWV")
        dWPJ = mksem("dWPJ")
        dGU = [mksem("dGU0"), mksem("dGU1")]
        dDN = [mksem(f"dDN{i}") for i in range(4)]
        dMISC = mksem("dMISC")
        dMISCG = mksem("dMISCG")
        dMSK = mksem("dMSK")
        dXO = mksem("dXO")
        dCCI = mksem("dCCI")
        dCCO = mksem("dCCO")
        dHT = [mksem(f"dHT{i}") for i in range(4)]
        dOUT = mksem("dOUT")
        cS = mksem("cS")
        wS = mksem("wS")

        PA = ctx0.enter_context(nc.psum_tensor("PA", [128, 2048], F32))
        PB = ctx0.enter_context(nc.psum_tensor("PB", [128, 2048], F32))

        def sbt(name, shape, dtype, **kw):
            return ctx0.enter_context(nc.sbuf_tensor(name, shape, dtype, **kw))

        ones_c = sbt("ones_c", [128, 1], BF16)
        ones_f = sbt("ones_f", [128, 128], BF16)
        ones_r = sbt("ones_r", [1, 128], BF16)
        qkc = sbt("qkc", [128, 4], F32)
        id32 = sbt("id32s", [128, 128], F32)
        bqk = sbt("bqks", [128, 16], F32)
        bv = sbt("bvs", [1, 1024], BF16)
        bpj = sbt("bpjs", [1, 2048], BF16)
        rts = sbt("rts", [128, 128], BF16)
        rmsw = sbt("rmsws", [128, 16], F32)
        negc = sbt("negc", [128, 8], F32)
        bn6s = [sbt(f"bn6_{i}", [128, 24], F32) for i in range(8)]
        mvs = [sbt(f"mv_{i}", [128, 2], F32) for i in range(8)]
        tcs = [sbt(f"tc_{i}", [128, 2], F32) for i in range(8)]
        acs = [sbt(f"ac_{i}", [128, 1], F32) for i in range(8)]
        bcs = [sbt(f"bc_{i}", [128, 1], F32) for i in range(8)]
        sm1 = sbt("sm1", [1, 4], F32)
        sm1b = sbt("sm1b", [1, 1], BF16)
        lgT = sbt("lgT", [128, 64], F32)
        e_sb = sbt("e_sb", [128, 64], F32)
        msel = sbt("msel", [128, 64], F32)
        mcol = sbt("mcol", [128, 8], F32)
        mneg = sbt("mneg", [128, 8], F32)
        wd_tok = sbt("wd_tok", [128, 64], F32)
        wd8 = sbt("wd8", [8, 1024], BF16)
        selb = sbt("selbs", [8, 1024], BF16)
        wdB = sbt("wdB", [128, 1024], BF16)

        blk = ctx0.enter_context(nc.Block())

        def _(v):
            v.memset(ones_c[:, :], 1.0)
            v.memset(ones_f[:, :], 1.0)
            tick(sV, v.memset(ones_r[:, :], 1.0))
        blk.vector(_)
        v_ones = sV.n

        def _(sy):
            tick(dMISC, sy.dma_start(out=id32[:, :], in_=id32_h), 16)
            tick(dMISC, sy.dma_start(out=selb[:, :], in_=selb_h), 16)
            tick(dMISC, sy.dma_start(out=bqk[:, :], in_=bqk_h), 16)
            tick(dMISC, sy.dma_start(out=rmsw[:, :], in_=rmsw_h), 16)
        blk.sync(_)
        misc_done = dMISC.n

        def _(g):
            tick(dMISCG, g.dma_start(out=bv[:, :], in_=bv_h), 16)
            tick(dMISCG, g.dma_start(out=bpj[:, :], in_=bpj_h), 16)
            tick(dMISCG, g.dma_start(out=rts[:, :], in_=rt_h), 16)
            g.dma_start(out=wu_in, in_=bv_h[0:1, 0:1024].rearrange("o (a b) -> (o a) b", a=8)).then_inc(dMISCG.h, 16)
            g.wait_ge(dMISCG.h, 64)
            tick(wS, g.collective_compute(
                "AllGather", ALU.bypass,
                replica_groups=[[0, 1, 2, 3, 4, 5, 6, 7]],
                ins=[wu_in], outs=[wu_out]))
        blk.gpsimd(_)
        dMISCG.n += 16
        misc_g = 48

        with ExitStack() as big1:
            ln1T = big1.enter_context(
                nc.sbuf_tensor("ln1T", [128, 16 * 2048], BF16, is_transpose_dest=True))

            v_all = big1.enter_context(nc.sbuf_tensor("v_all", [128, 16 * 1024], BF16))
            with ExitStack() as sub1a:
                x_t = [sub1a.enter_context(nc.sbuf_tensor(f"x_t{i}", [128, 2048], F32))
                       for i in range(4)]
                l_t = [sub1a.enter_context(nc.sbuf_tensor(f"l_t{i}", [128, 2048], BF16))
                       for i in range(4)]
                wvw = [sub1a.enter_context(nc.sbuf_tensor(f"wvw{i}", [128, 8192], BF16))
                       for i in range(2)]

                def _(g):
                    tick(dWV, g.dma_start(out=wvw[0][:, :], in_=wv_h[0]), 16)
                    tick(dWV, g.dma_start(out=wvw[1][:, :], in_=wv_h[1]), 16)
                blk.gpsimd(_)
                wv_done = dWV.n

                s_app = {}
                tr_done = {}
                v_ev = {}
                rc_vs = {}

                def emitA(i):
                    sl = i % 4

                    def _(sy, i=i, sl=sl):
                        if i >= 4:
                            sy.wait_ge(sS.h, s_app[i - 4])
                        tick(dX[sl], sy.dma_start(
                            out=x_t[sl][:, :], in_=x_h[i * 128:(i + 1) * 128, :]), 16)
                    blk.sync(_)

                    def _(v, i=i, sl=sl):
                        v.wait_ge(dX[sl].h, dX[sl].n)
                        if i >= 4:
                            v.wait_ge(sS.h, s_app[i - 4])
                        for cc in range(3):
                            v.bn_stats(bn6s[sl][:, cc * 6:(cc + 1) * 6],
                                       x_t[sl][:, cc * 512:(cc + 1) * 512])
                        n1 = tick(sV, v.bn_stats(bn6s[sl][:, 18:24], x_t[sl][:, 1536:2048]))
                        v.wait_ge(sV.h, n1)
                        n2 = tick(sV, v.bn_aggr(mvs[sl][:, :], bn6s[sl][:, :]))
                        v.wait_ge(sV.h, n2)
                        n3 = tick(sV, v.tensor_scalar_add(tcs[sl][:, 0:1], mvs[sl][:, 1:2], 1e-5))
                        v.wait_ge(sV.h, n3)
                        tick(sV, v.reciprocal(tcs[sl][:, 1:2], tcs[sl][:, 0:1]))
                    blk.vector(_)
                    rc_vs[i] = sV.n

                def emitB(i):
                    sl = i % 4

                    def _(s, i=i, sl=sl):
                        s.wait_ge(sV.h, rc_vs[i])
                        tick(sS, s.sqrt(acs[sl][:, :], tcs[sl][:, 1:2]))
                    blk.scalar(_)
                    sq_s = sS.n

                    def _(v, i=i, sl=sl, sq_s=sq_s):
                        v.wait_ge(sS.h, sq_s)
                        tick(sV, v.scalar_tensor_tensor(bcs[sl][:, :], mvs[sl][:, 0:1], -1.0,
                                                        acs[sl][:, :], op0=ALU.mult,
                                                        op1=ALU.mult))
                    blk.vector(_)
                    ab_v = sV.n

                    def _(s, i=i, sl=sl, ab_v=ab_v):
                        s.wait_ge(sV.h, ab_v)
                        if i >= 4:
                            s.wait_ge(dTR[sl].h, 16 * (i // 4))
                        tick(sS, s.activation(l_t[sl][:, :], x_t[sl][:, :], ACT.Identity,
                                              bias=bcs[sl][:, :], scale=acs[sl][:, :]))
                    blk.scalar(_)
                    s_app[i] = sS.n

                    def _(sy, i=i, sl=sl):
                        sy.wait_ge(sS.h, s_app[i])
                        o_v = ln1T[:, :].rearrange("p (ct tt) -> p ct tt", ct=16)[:, :, i * 128:(i + 1) * 128]
                        i_v = l_t[sl][:, :].rearrange("t (ct p) -> t ct p", ct=16)
                        tick(dTR[sl], sy.dma_start_transpose(o_v, i_v), 16)
                    blk.sync(_)
                    tr_done[i] = (sl, dTR[sl].n)

                    def _(t, i=i):
                        sl_, tgt = tr_done[i]
                        t.wait_ge(dTR[sl_].h, tgt)
                        if i == 0:
                            t.wait_ge(dMISCG.h, misc_g)
                            t.wait_ge(dWV.h, wv_done)
                            t.wait_ge(sV.h, v_ones)
                        if i >= 2:
                            t.wait_ge(sV.h, v_ev[i - 2])
                        base = (i % 2) * 2
                        for vch in range(2):
                            po = PA[:, (base + vch) * 512:(base + vch + 1) * 512]
                            t.matmul(po, ones_r[:, :], bv[:, vch * 512:(vch + 1) * 512],
                                     start=True, stop=False)
                            for kt in range(16):
                                mm = t.matmul(
                                    po,
                                    ln1T[:, kt * 2048 + i * 128: kt * 2048 + (i + 1) * 128],
                                    wvw[vch][:, kt * 512:(kt + 1) * 512],
                                    start=False, stop=(kt == 15))
                        tick(sT, mm)
                    blk.tensor(_)
                    v_mms[i] = sT.n

                def emitEv(i):
                    def _(v, i=i):
                        v.wait_ge(sT.h, v_mms[i])
                        base = (i % 2) * 2
                        tick(sV, v.tensor_copy(v_all[:, i * 1024:(i + 1) * 1024],
                                               PA[:, base * 512:(base + 2) * 512]))
                    blk.vector(_)
                    v_ev[i] = sV.n

                v_mms = {}
                for i in range(17):
                    if i < 16:
                        emitA(i)
                    if i >= 1:
                        emitB(i - 1)
                    if i >= 2:
                        emitEv(i - 2)
                emitEv(15)
                v_done_v = sV.n
                v_mm_last = sT.n

            with ExitStack() as sub3:
                yT = sub3.enter_context(nc.sbuf_tensor("yT", [128, 8 * 2048], BF16))
                expT = sub3.enter_context(nc.sbuf_tensor("expT", [128, 2048], BF16))
                msk = sub3.enter_context(nc.sbuf_tensor("msks", [128, 2048], BF16))
                qT = sub3.enter_context(nc.sbuf_tensor("qT", [128, 2048], BF16))
                kTl = sub3.enter_context(nc.sbuf_tensor("kTl", [128, 2048], BF16))
                rcp = sub3.enter_context(nc.sbuf_tensor("rcp", [128, 512], F32))
                rcpb = sub3.enter_context(nc.sbuf_tensor("rcpb", [128, 512], BF16))
                wqkw = [sub3.enter_context(nc.sbuf_tensor(f"wqkw{i}", [128, 2048], BF16))
                        for i in range(2)]

                def _(sy):
                    sy.wait_ge(sT.h, v_mm_last)
                    tick(dMSK, sy.dma_start(out=msk[:, :], in_=msk_h), 16)
                blk.sync(_)
                msk_done = dMSK.n

                qk_mm = {}
                wqk_tgt = {}

                def load_wqk(o):
                    sl = o % 2
                    m = (o >> 1) + 8 * (o & 1)

                    def _(g, o=o, sl=sl, m=m):
                        if o >= 2:
                            g.wait_ge(sT.h, qk_mm[o - 2])
                        else:
                            g.wait_ge(sT.h, v_mm_last)
                        tick(dWQK[sl], g.dma_start(out=wqkw[sl][:, :], in_=wqk_h[m]), 16)
                    blk.gpsimd(_)
                    wqk_tgt[o] = dWQK[sl].n

                load_wqk(0)
                load_wqk(1)

                paS = 0
                paV = v_done_v
                att_t_last = 0
                exp_last_s = 0
                y_ev_v = 0
                sum_rd_v = 0
                bc_ev_s = 0
                negc_prev_v = 0
                km_v = 0

                for h in range(8):
                    for which in range(2):
                        o = 2 * h + which
                        m = h if which == 0 else 8 + h
                        sl = o % 2
                        dst = qT if which == 0 else kTl

                        def _(t, o=o, sl=sl, paS=paS, paV=paV, exp_last_s=exp_last_s):
                            t.wait_ge(dWQK[sl].h, wqk_tgt[o])
                            if paS:
                                t.wait_ge(sS.h, paS)
                            if exp_last_s:
                                t.wait_ge(sS.h, exp_last_s)
                            if paV:
                                t.wait_ge(sV.h, paV)
                            for kt in range(16):
                                for cc in range(4):
                                    mm = t.matmul(
                                        PA[:, cc * 512:(cc + 1) * 512],
                                        wqkw[sl][:, kt * 128:(kt + 1) * 128],
                                        ln1T[:, kt * 2048 + cc * 512: kt * 2048 + (cc + 1) * 512],
                                        start=(kt == 0), stop=(kt == 15))
                            tick(sT, mm)
                        blk.tensor(_)
                        qk_mm[o] = sT.n

                        def _(s, o=o, m=m, dst=dst):
                            s.wait_ge(sT.h, qk_mm[o])
                            if o == 0:
                                s.wait_ge(dMISC.h, misc_done)
                            tick(sS, s.activation(dst[:, :], PA[:, :], ACT.Identity,
                                                  bias=bqk[:, m:m + 1], scale=1.0))
                        blk.scalar(_)
                        paS = sS.n
                        paV = 0
                        exp_last_s = 0
                        if o + 2 < 16:
                            load_wqk(o + 2)

                        if which == 0:
                            q_ev = sS.n

                            def _(v, h=h, q_ev=q_ev, att_t_last=att_t_last):
                                v.wait_ge(sS.h, q_ev)
                                if att_t_last:
                                    v.wait_ge(sT.h, att_t_last)
                                tick(sV, v.tensor_tensor(expT[:, :], qT[:, :], qT[:, :],
                                                         op=ALU.mult))
                            blk.vector(_)
                            sqq_v = sV.n

                            def _(t, h=h, sqq_v=sqq_v, y_ev_v=y_ev_v, sum_rd_v=sum_rd_v):
                                t.wait_ge(sV.h, sqq_v)
                                if y_ev_v:
                                    t.wait_ge(sV.h, y_ev_v)
                                if sum_rd_v:
                                    t.wait_ge(sV.h, sum_rd_v)
                                for cc in range(3):
                                    t.matmul(PB[:, cc * 512:(cc + 1) * 512], ones_f[:, :],
                                             expT[:, cc * 512:(cc + 1) * 512],
                                             start=True, stop=True)
                                tick(sT, t.matmul(PB[:, 1536:2048], ones_f[:, :],
                                                  expT[:, 1536:2048], start=True, stop=True))
                            blk.tensor(_)
                            nq_t = sT.n

                            def _(v, nq_t=nq_t):
                                v.wait_ge(sT.h, nq_t)
                                tick(sV, v.reduce_max(qkc[:, 0:1], PB[:, 0:2048], axis=AX))
                            blk.vector(_)
                            redq_v = sV.n
                    qk_ev = sS.n

                    def _(v, h=h, qk_ev=qk_ev, nq_t=nq_t):
                        v.wait_ge(sS.h, qk_ev)
                        v.wait_ge(sT.h, nq_t)
                        tick(sV, v.tensor_tensor(expT[:, :], kTl[:, :], kTl[:, :],
                                                 op=ALU.mult))
                    blk.vector(_)
                    sqk_v = sV.n

                    def _(t, h=h, sqk_v=sqk_v, redq_v=redq_v):
                        t.wait_ge(sV.h, sqk_v)
                        t.wait_ge(sV.h, redq_v)
                        for cc in range(3):
                            t.matmul(PB[:, cc * 512:(cc + 1) * 512], ones_f[:, :],
                                     expT[:, cc * 512:(cc + 1) * 512], start=True, stop=True)
                        tick(sT, t.matmul(PB[:, 1536:2048], ones_f[:, :],
                                          expT[:, 1536:2048], start=True, stop=True))
                    blk.tensor(_)
                    nk_t = sT.n

                    def _(v, h=h, nk_t=nk_t):
                        v.wait_ge(sT.h, nk_t)
                        n1 = tick(sV, v.reduce_max(qkc[:, 1:2], PB[:, 0:2048], axis=AX))
                        v.wait_ge(sV.h, n1)
                        n2 = tick(sV, v.tensor_tensor(qkc[:, 2:3], qkc[:, 0:1], qkc[:, 1:2],
                                                      op=ALU.mult))
                        v.wait_ge(sV.h, n2)
                        tick(sV, v.tensor_scalar_mul(qkc[:, 2:3], qkc[:, 2:3], 1.0 / 128.0))
                    blk.vector(_)
                    km_v = sV.n

                    def _(s, h=h, km_v=km_v):
                        s.wait_ge(sV.h, km_v)
                        tick(sS, s.sqrt(qkc[:, 3:4], qkc[:, 2:3]))
                    blk.scalar(_)
                    cqs_s = sS.n

                    def _(v, h=h, cqs_s=cqs_s):
                        v.wait_ge(sS.h, cqs_s)
                        tick(sV, v.tensor_scalar_mul(negc[:, h:h + 1], qkc[:, 3:4], -1.0))
                    blk.vector(_)
                    negc_s = sV.n

                    exp_s = {0: 0, 1: 0}
                    av_t = {0: att_t_last, 1: att_t_last}
                    for cc4 in range(4):
                        nk = 4 * (cc4 + 1)
                        npair = nk // 2
                        for p in range(npair):
                            slot = p % 2

                            def _(t, h=h, cc4=cc4, p=p, slot=slot, km_v=km_v):
                                if exp_s[slot]:
                                    t.wait_ge(sS.h, exp_s[slot])
                                if cc4 == 0 and p < 2:
                                    t.wait_ge(sV.h, km_v)
                                for q in range(2):
                                    kt = 2 * p + q
                                    bank = 2 * slot + q
                                    mm = t.matmul(PA[:, bank * 512:(bank + 1) * 512],
                                                  kTl[:, kt * 128:(kt + 1) * 128],
                                                  qT[:, cc4 * 512:(cc4 + 1) * 512],
                                                  start=True, stop=True)
                                tick(sT, mm)
                            blk.tensor(_)
                            sc_t = sT.n

                            def _(s, h=h, slot=slot, sc_t=sc_t, negc_s=negc_s):
                                s.wait_ge(sT.h, sc_t)
                                s.wait_ge(sV.h, negc_s)
                                if av_t[slot]:
                                    s.wait_ge(sT.h, av_t[slot])
                                tick(sS, s.activation(expT[:, slot * 1024:(slot + 1) * 1024],
                                                      PA[:, slot * 1024:(slot + 1) * 1024],
                                                      ACT.Exp, bias=negc[:, h:h + 1],
                                                      scale=SCL))
                            blk.scalar(_)
                            exp_s[slot] = sS.n
                            cur_exp = sS.n

                            mv_cur = 0
                            if p >= 2 * cc4:
                                moff = (p - 2 * cc4) * 1024
                                def _(v, cur_exp=cur_exp, h=h, cc4=cc4, slot=slot,
                                      moff=moff):
                                    if h == 0 and cc4 == 0 and moff == 0:
                                        v.wait_ge(dMSK.h, msk_done)
                                    v.wait_ge(sS.h, cur_exp)
                                    sl_ap = expT[:, slot * 1024:(slot + 1) * 1024]
                                    tick(sV, v.tensor_tensor(sl_ap, sl_ap,
                                                             msk[:, moff:moff + 1024],
                                                             op=ALU.mult))
                                blk.vector(_)
                                mv_cur = sV.n

                            def _(t, h=h, cc4=cc4, p=p, slot=slot, nk=nk,
                                  cur_exp=cur_exp, mv_cur=mv_cur,
                                  y_ev_v=y_ev_v, sum_rd_v=sum_rd_v):
                                t.wait_ge(sS.h, cur_exp)
                                if mv_cur:
                                    t.wait_ge(sV.h, mv_cur)
                                if p == 0:
                                    if y_ev_v:
                                        t.wait_ge(sV.h, y_ev_v)
                                    if sum_rd_v:
                                        t.wait_ge(sV.h, sum_rd_v)
                                for q in range(2):
                                    kt = 2 * p + q
                                    rhs = expT[:, slot * 1024 + q * 512: slot * 1024 + (q + 1) * 512]
                                    t.matmul(PB[:, 512:1024],
                                             v_all[:, kt * 1024 + h * 128: kt * 1024 + (h + 1) * 128],
                                             rhs, start=(kt == 0), stop=(kt == nk - 1))
                                    mm = t.matmul(PB[:, 1024:1536], ones_f[:, :], rhs,
                                                  start=(kt == 0), stop=(kt == nk - 1))
                                tick(sT, mm)
                            blk.tensor(_)
                            av_t[slot] = sT.n
                        av_fin = sT.n

                        def _(v, av_fin=av_fin):
                            v.wait_ge(sT.h, av_fin)
                            n1 = tick(sV, v.reciprocal(rcp[:, :], PB[:, 1024:1536]))
                            v.wait_ge(sV.h, n1)
                            tick(sV, v.tensor_copy(rcpb[:, :], rcp[:, :]))
                        blk.vector(_)
                        sum_rd_v = sV.n

                        def _(v, h=h, cc4=cc4, sum_rd_v=sum_rd_v):
                            v.wait_ge(sV.h, sum_rd_v)
                            tick(sV, v.tensor_tensor(
                                yT[:, h * 2048 + cc4 * 512: h * 2048 + (cc4 + 1) * 512],
                                PB[:, 512:1024], rcpb[:, :], op=ALU.mult))
                        blk.vector(_)
                        y_ev_v = sV.n
                    att_t_last = sT.n
                    exp_last_s = max(exp_s[0], exp_s[1])
                    paS = 0
                    paV = 0
                y_done_v = sV.n

                def _(g):
                    g.wait_ge(sV.h, y_done_v)
                    o_v = cc_in.rearrange("(hh p) t -> p hh t", p=128)
                    i_v = yT[:, :].rearrange("p (hh t) -> p hh t", hh=8)
                    tick(dCCI, g.dma_start(out=o_v, in_=i_v), 16)
                    g.wait_ge(dCCI.h, dCCI.n)
                    g.wait_ge(wS.h, wS.n)
                    tick(cS, g.collective_compute(
                        "AllGather", ALU.bypass,
                        replica_groups=[[0, 1, 2, 3, 4, 5, 6, 7]],
                        ins=[cc_in], outs=[cc_out]))
                blk.gpsimd(_)

        x1 = ctx0.enter_context(nc.sbuf_tensor("x1", [128, 8 * 2048], F32))
        with ExitStack() as big2:
            wpj = big2.enter_context(nc.sbuf_tensor("wpjs", [128, 16 * 2048], BF16))
            y_all = big2.enter_context(nc.sbuf_tensor("y_all", [128, 16 * 1024], BF16))

            def _(sy):
                sy.wait_ge(sV.h, y_done_v)
                sy.wait_ge(dCCI.h, 16)
                tick(dXO, sy.dma_start(
                    out=x1[:, :].rearrange("p (tt c) -> p tt c", tt=8),
                    in_=xo_h.rearrange("(tt p) c -> p tt c", p=128)), 16)
            blk.sync(_)
            xo_done = dXO.n

            def _(g):
                for yf in range(16):
                    tick(dWPJ, g.dma_start(out=wpj[:, yf * 2048:(yf + 1) * 2048],
                                           in_=wpj_h[:, yf, :]), 16)
            blk.gpsimd(_)
            wpj_done = dWPJ.n

            def _(g):
                g.wait_ge(cS.h, cS.n)
                pid = nc.partition_id(engines=[mybir.EngineType.Pool])
                j = pid % 2
                base = pid - j
                for half in range(2):
                    src = cc_out[bass.ts(base + half, 1024), :][:, bass.ts(j, 1024)]
                    src = src.rearrange("(hh p) t -> p hh t", p=128)
                    dst = y_all[:, half * 8192:(half + 1) * 8192].rearrange(
                        "p (hh t) -> p hh t", hh=8)
                    tick(dCCO, g.dma_start(out=dst, in_=src), 16)
            blk.gpsimd(_)
            cco_done = dCCO.n

            s_h = {}
            rc_vs2 = {}

            def emitA2(i):
                sl = i

                def _(v, i=i, sl=sl):
                    v.wait_ge(sV.h, pj_ev[i])
                    for cc in range(3):
                        v.bn_stats(bn6s[sl][:, cc * 6:(cc + 1) * 6],
                                   x1[:, i * 2048 + cc * 512: i * 2048 + (cc + 1) * 512])
                    n1 = tick(sV, v.bn_stats(bn6s[sl][:, 18:24],
                                             x1[:, i * 2048 + 1536: i * 2048 + 2048]))
                    v.wait_ge(sV.h, n1)
                    n2 = tick(sV, v.bn_aggr(mvs[sl][:, :], bn6s[sl][:, :]))
                    v.wait_ge(sV.h, n2)
                    n3 = tick(sV, v.tensor_scalar(tcs[sl][:, 0:1], mvs[sl][:, 1:2],
                                                  1.000001, 1e-11,
                                                  op0=ALU.mult, op1=ALU.add))
                    v.wait_ge(sV.h, n3)
                    tick(sV, v.reciprocal(tcs[sl][:, 1:2], tcs[sl][:, 0:1]))
                blk.vector(_)
                rc_vs2[i] = sV.n

            pj_mm = {}
            pj_ev = {}
            for tt in range(8):
                PX = PA if tt % 2 == 0 else PB

                def _(t, tt=tt, PX=PX):
                    if tt == 0:
                        t.wait_ge(dCCO.h, cco_done)
                        t.wait_ge(dWPJ.h, wpj_done)
                        t.wait_ge(dMISCG.h, misc_g)
                    if tt >= 2:
                        t.wait_ge(sV.h, pj_ev[tt - 2])
                    for dq in range(4):
                        t.matmul(PX[:, dq * 512:(dq + 1) * 512], ones_r[:, :],
                                 bpj[:, dq * 512:(dq + 1) * 512], start=True, stop=False)
                    for yf in range(16):
                        for dq in range(4):
                            mm = t.matmul(
                                PX[:, dq * 512:(dq + 1) * 512],
                                y_all[:, yf * 1024 + tt * 128: yf * 1024 + (tt + 1) * 128],
                                wpj[:, yf * 2048 + dq * 512: yf * 2048 + (dq + 1) * 512],
                                start=False, stop=(yf == 15))
                    tick(sT, mm)
                blk.tensor(_)
                pj_mm[tt] = sT.n

                def _(v, tt=tt, PX=PX):
                    v.wait_ge(sT.h, pj_mm[tt])
                    if tt == 0:
                        v.wait_ge(dXO.h, xo_done)
                    for dq in range(3):
                        sl = x1[:, tt * 2048 + dq * 512: tt * 2048 + (dq + 1) * 512]
                        v.tensor_tensor(sl, PX[:, dq * 512:(dq + 1) * 512], sl, op=ALU.add)
                    sl = x1[:, tt * 2048 + 1536: tt * 2048 + 2048]
                    tick(sV, v.tensor_tensor(sl, PX[:, 1536:2048], sl, op=ALU.add))
                blk.vector(_)
                pj_ev[tt] = sV.n
                emitA2(tt)
            proj_done_v = sV.n

        with ExitStack() as big3:
            hT = big3.enter_context(
                nc.sbuf_tensor("hT", [128, 16 * 1024], BF16, is_transpose_dest=True))
            su = [big3.enter_context(nc.sbuf_tensor(f"su{i}", [128, 8 * 1024], BF16))
                  for i in range(2)]
            guw = [big3.enter_context(nc.sbuf_tensor(f"guw{i}", [128, 4096], BF16))
                   for i in range(2)]
            dnw = [big3.enter_context(nc.sbuf_tensor(f"dnw{i}", [128, 4096], BF16))
                   for i in range(4)]
            h_t = [big3.enter_context(nc.sbuf_tensor(f"h_t{i}", [128, 2048], BF16))
                   for i in range(2)]
            silu_sb = big3.enter_context(nc.sbuf_tensor("silu_sb", [128, 1024], BF16))
            lg_sb = big3.enter_context(nc.sbuf_tensor("lg_sb", [8, 1024], F32))

            def emitB2(i):
                sl = i
                hsl = i % 2

                def _(s, i=i, sl=sl):
                    s.wait_ge(sV.h, rc_vs2[i])
                    if i == 0:
                        s.wait_ge(sT.h, pj_mm[7])
                    tick(sS, s.sqrt(acs[sl][:, :], tcs[sl][:, 1:2]))
                blk.scalar(_)
                sq_s = sS.n

                def _(v, i=i, sl=sl, sq_s=sq_s):
                    v.wait_ge(sS.h, sq_s)
                    tick(sV, v.scalar_tensor_tensor(bcs[sl][:, :], mvs[sl][:, 0:1], -1.0,
                                                    acs[sl][:, :], op0=ALU.mult, op1=ALU.mult))
                blk.vector(_)
                ab_v = sV.n

                def _(s, i=i, sl=sl, hsl=hsl, ab_v=ab_v):
                    s.wait_ge(sV.h, ab_v)
                    if i >= 2:
                        s.wait_ge(dHT[hsl].h, 16 * (i // 2))
                    tick(sS, s.activation(h_t[hsl][:, :], x1[:, i * 2048:(i + 1) * 2048],
                                          ACT.Identity, bias=bcs[sl][:, :], scale=acs[sl][:, :]))
                blk.scalar(_)
                s_h[i] = sS.n

                def _(sy, i=i, hsl=hsl):
                    sy.wait_ge(sS.h, s_h[i])
                    if i == 0:
                        sy.wait_ge(sT.h, pj_mm[7])
                    o_v = hT[:, :].rearrange("p (dt tt) -> p dt tt", dt=16)[:, :, i * 128:(i + 1) * 128]
                    i_v = h_t[hsl][:, :].rearrange("t (dt p) -> t dt p", dt=16)
                    tick(dHT[hsl], sy.dma_start_transpose(o_v, i_v), 16)
                blk.sync(_)

            for i in range(8):
                emitB2(i)
            ht_done = tuple(dHT[i].n for i in range(4))

            def _(v):
                for i in range(4):
                    v.wait_ge(dHT[i].h, ht_done[i])
                v.wait_ge(dMISC.h, misc_done)
                for dt in range(15):
                    v.tensor_scalar(hT[:, dt * 1024:(dt + 1) * 1024],
                                    hT[:, dt * 1024:(dt + 1) * 1024],
                                    rmsw[:, dt:dt + 1], None, op0=ALU.mult)
                tick(sV, v.tensor_scalar(hT[:, 15 * 1024:16 * 1024],
                                         hT[:, 15 * 1024:16 * 1024],
                                         rmsw[:, 15:16], None, op0=ALU.mult))
            blk.vector(_)
            hT_ready_v = sV.n

            def _(t):
                t.wait_ge(sV.h, hT_ready_v)
                t.wait_ge(dMISCG.h, misc_g)
                for cc in range(2):
                    for kt in range(16):
                        mm = t.matmul(PA[0:8, cc * 512:(cc + 1) * 512],
                                      rts[:, kt * 8:(kt + 1) * 8],
                                      hT[:, kt * 1024 + cc * 512: kt * 1024 + (cc + 1) * 512],
                                      start=(kt == 0), stop=(kt == 15))
                tick(sT, mm)
            blk.tensor(_)
            lg_t = sT.n

            def _(v):
                v.wait_ge(sT.h, lg_t)
                tick(sV, v.tensor_copy(lg_sb[:, :], PA[0:8, 0:1024]))
            blk.vector(_)
            lg_v = sV.n

            def _(t):
                t.wait_ge(sV.h, lg_v)
                t.wait_ge(dMISC.h, misc_done)
                for i in range(7):
                    t.transpose(PB[:, i * 8:(i + 1) * 8], lg_sb[:, i * 128:(i + 1) * 128],
                                id32[0:8, 0:8])
                tick(sT, t.transpose(PB[:, 56:64], lg_sb[:, 896:1024], id32[0:8, 0:8]))
            blk.tensor(_)
            lgT_t = sT.n

            def _(v):
                v.wait_ge(sT.h, lgT_t)
                n1 = tick(sV, v.tensor_copy(lgT[:, :], PB[:, 0:64]))
                v.wait_ge(sV.h, n1)
                for i in range(7):
                    v.reduce_max(mcol[:, i:i + 1], lgT[:, i * 8:(i + 1) * 8], axis=AX)
                n2 = tick(sV, v.reduce_max(mcol[:, 7:8], lgT[:, 56:64], axis=AX))
                v.wait_ge(sV.h, n2)
                tick(sV, v.tensor_scalar_mul(mneg[:, :], mcol[:, :], -1.0))
            blk.vector(_)
            m1_v = sV.n

            def _(s):
                s.wait_ge(sV.h, m1_v)
                for i in range(7):
                    s.activation(e_sb[:, i * 8:(i + 1) * 8], lgT[:, i * 8:(i + 1) * 8],
                                 ACT.Exp, bias=mneg[:, i:i + 1], scale=1.0)
                tick(sS, s.activation(e_sb[:, 56:64], lgT[:, 56:64],
                                      ACT.Exp, bias=mneg[:, 7:8], scale=1.0))
            blk.scalar(_)
            e_s = sS.n

            def _(v):
                v.wait_ge(sS.h, e_s)
                n1 = tick(sV, v.tensor_scalar(msel[:, :], e_sb[:, :], 1.0, -1.0,
                                              op0=ALU.is_ge, op1=ALU.mult))
                v.wait_ge(sV.h, n1)
                n2 = tick(sV, v.tensor_scalar_add(msel[:, :], msel[:, :], 1.0))
                v.wait_ge(sV.h, n2)
                n3 = tick(sV, v.tensor_tensor(msel[:, :], e_sb[:, :], msel[:, :], op=ALU.mult))
                v.wait_ge(sV.h, n3)
                for i in range(7):
                    v.reduce_max(mcol[:, i:i + 1], msel[:, i * 8:(i + 1) * 8], axis=AX)
                n4 = tick(sV, v.reduce_max(mcol[:, 7:8], msel[:, 56:64], axis=AX))
                v.wait_ge(sV.h, n4)
                for i in range(7):
                    v.tensor_scalar(wd_tok[:, i * 8:(i + 1) * 8], e_sb[:, i * 8:(i + 1) * 8],
                                    mcol[:, i:i + 1], None, op0=ALU.is_ge)
                n5 = tick(sV, v.tensor_scalar(wd_tok[:, 56:64], e_sb[:, 56:64],
                                              mcol[:, 7:8], None, op0=ALU.is_ge))
                v.wait_ge(sV.h, n5)
                n6 = tick(sV, v.tensor_tensor(wd_tok[:, :], e_sb[:, :], wd_tok[:, :], op=ALU.mult))
                v.wait_ge(sV.h, n6)
                n7 = tick(sV, v.tensor_scalar_add(mcol[:, :], mcol[:, :], 1.0))
                v.wait_ge(sV.h, n7)
                tick(sV, v.reciprocal(mcol[:, :], mcol[:, :]))
            blk.vector(_)
            w2_v = sV.n

            def _(s):
                s.wait_ge(sV.h, w2_v)
                for i in range(7):
                    s.activation(wd_tok[:, i * 8:(i + 1) * 8], wd_tok[:, i * 8:(i + 1) * 8],
                                 ACT.Identity, bias=0.0, scale=mcol[:, i:i + 1])
                tick(sS, s.activation(wd_tok[:, 56:64], wd_tok[:, 56:64],
                                      ACT.Identity, bias=0.0, scale=mcol[:, 7:8]))
            blk.scalar(_)
            wd_s = sS.n

            gu_mm = {}
            dn_mm = {}
            gu_tgt = {}
            dn_tgt = {}
            su_v = {}
            silu_s = {}
            ev_v = {}
            wdT_done = 0
            wdb_v = 0
            wdb_cp_v = 0

            def load_gu(idx, ft):
                e = EORD[idx]
                sl = ft % 2

                def _(g, idx=idx, e=e, ft=ft, sl=sl):
                    if (idx, ft) == (0, 0):
                        g.wait_ge(sT.h, pj_mm[7])
                    prev = (EORD[idx], ft - 2) if ft >= 2 else \
                        ((EORD[idx - 1], ft + 6) if idx >= 1 else None)
                    if prev in gu_mm:
                        g.wait_ge(sT.h, gu_mm[prev])
                    tick(dGU[sl], g.dma_start(out=guw[sl][:, 0:2048], in_=gu_h[e, ft, 0]), 16)
                    tick(dGU[sl], g.dma_start(out=guw[sl][:, 2048:4096], in_=gu_h[e, ft, 1]), 16)
                blk.gpsimd(_)
                gu_tgt[(e, ft)] = dGU[sl].n

            def load_dn(idx, dq):
                e = EORD[idx]

                def _(g, idx=idx, e=e, dq=dq):
                    if (idx, dq) == (0, 0):
                        g.wait_ge(sT.h, pj_mm[7])
                    if idx >= 1:
                        prev = (EORD[idx - 1], dq // 2)
                        if prev in dn_mm:
                            g.wait_ge(sT.h, dn_mm[prev])
                    tick(dDN[dq], g.dma_start(out=dnw[dq][:, :], in_=dn_h[e, dq]), 16)
                blk.gpsimd(_)
                dn_tgt[(e, dq)] = dDN[dq].n

            load_gu(0, 0)
            load_gu(0, 1)
            for dq in range(4):
                load_dn(0, dq)

            for idx in range(NE):
                e = EORD[idx]
                esl = idx % 2

                if e != 8:
                    def _(t, e=e, idx=idx, wdb_v=wdb_v):
                        t.wait_ge(sV.h, wdb_v)
                        t.wait_ge(sV.h, ev_v[(EORD[idx - 1], 1, 3)])
                        t.matmul(PB[:, 0:512], selb[:, e * 128:(e + 1) * 128],
                                 wd8[:, 0:512], start=True, stop=True)
                        tick(sT, t.matmul(PB[:, 512:1024], selb[:, e * 128:(e + 1) * 128],
                                          wd8[:, 512:1024], start=True, stop=True))
                    blk.tensor(_)
                    wdb_t = sT.n

                    def _(v, wdb_t=wdb_t):
                        v.wait_ge(sT.h, wdb_t)
                        tick(sV, v.tensor_copy(wdB[:, :], PB[:, 0:1024]))
                    blk.vector(_)
                    wdb_cp_v = sV.n

                for ft in range(FT):
                    sl = ft % 2

                    def _(t, idx=idx, e=e, ft=ft, sl=sl):
                        t.wait_ge(dGU[sl].h, gu_tgt[(e, ft)])
                        if (idx, ft) == (0, 0):
                            t.wait_ge(sV.h, hT_ready_v)
                            t.wait_ge(sV.h, lg_v)
                        prevk = (e, ft - 1) if ft >= 1 else \
                            ((EORD[idx - 1], 7) if idx >= 1 else None)
                        if prevk in su_v:
                            t.wait_ge(sV.h, su_v[prevk])
                        if ft == 0 and idx >= 1:
                            t.wait_ge(sV.h, ev_v[(EORD[idx - 1], 1, 3)])
                        for gui in range(2):
                            for kt in range(16):
                                for cc in range(2):
                                    bank = 2 * gui + cc
                                    mm = t.matmul(
                                        PA[:, bank * 512:(bank + 1) * 512],
                                        guw[sl][:, gui * 2048 + kt * 128: gui * 2048 + (kt + 1) * 128],
                                        hT[:, kt * 1024 + cc * 512: kt * 1024 + (cc + 1) * 512],
                                        start=(kt == 0), stop=(kt == 15))
                        tick(sT, mm)
                    blk.tensor(_)
                    gu_mm[(e, ft)] = sT.n

                    def _(s, e=e, ft=ft, idx=idx):
                        s.wait_ge(sT.h, gu_mm[(e, ft)])
                        prevk = (e, ft - 1) if ft >= 1 else \
                            ((EORD[idx - 1], 7) if idx >= 1 else None)
                        if prevk in su_v:
                            s.wait_ge(sV.h, su_v[prevk])
                        tick(sS, s.activation(silu_sb[:, :], PA[:, 0:1024], ACT.Silu,
                                              bias=0.0, scale=1.0))
                    blk.scalar(_)
                    silu_s[(e, ft)] = sS.n

                    def _(v, e=e, ft=ft, idx=idx, esl=esl, wdb_cp_v=wdb_cp_v):
                        v.wait_ge(sS.h, silu_s[(e, ft)])
                        if e != 8 and ft == 0:
                            v.wait_ge(sV.h, wdb_cp_v)
                        if ft == 0 and idx >= 2:
                            v.wait_ge(sT.h, dn_mm[(EORD[idx - 2], 1)])
                        for cc in range(2):
                            ap = su[esl][:, ft * 1024 + cc * 512: ft * 1024 + (cc + 1) * 512]
                            ii = v.tensor_tensor(ap, silu_sb[:, cc * 512:(cc + 1) * 512],
                                                 PA[:, (2 + cc) * 512:(3 + cc) * 512],
                                                 op=ALU.mult)
                        n1 = tick(sV, ii)
                        if e != 8:
                            v.wait_ge(sV.h, n1)
                            for cc in range(2):
                                ap = su[esl][:, ft * 1024 + cc * 512: ft * 1024 + (cc + 1) * 512]
                                ii = v.tensor_tensor(ap, ap, wdB[:, cc * 512:(cc + 1) * 512],
                                                     op=ALU.mult)
                            tick(sV, ii)
                    blk.vector(_)
                    su_v[(e, ft)] = sV.n

                    nxt = (idx, ft + 2) if ft + 2 < FT else (idx + 1, ft + 2 - FT)
                    if nxt[0] < NE:
                        load_gu(*nxt)

                if idx == 0:
                    def _(t):
                        t.wait_ge(sS.h, wd_s)
                        for i in range(7):
                            t.transpose(PB[0:8, 512 + i * 128: 512 + (i + 1) * 128],
                                        wd_tok[:, i * 8:(i + 1) * 8], id32[:, :])
                        tick(sT, t.transpose(PB[0:8, 1408:1536], wd_tok[:, 56:64], id32[:, :]))
                    blk.tensor(_)
                    wdT_done = sT.n

                    def _(v):
                        v.wait_ge(sT.h, wdT_done)
                        tick(sV, v.tensor_copy(wd8[:, :], PB[0:8, 512:1536]))
                    blk.vector(_)
                    wdb_v = sV.n

                for dqp in range(2):
                    for tq in range(4):
                        PX = PB if tq % 2 == 0 else PA

                        def _(t, idx=idx, e=e, dqp=dqp, tq=tq, PX=PX, wdb_v=wdb_v):
                            if tq == 0:
                                t.wait_ge(dDN[2 * dqp].h, dn_tgt[(e, 2 * dqp)])
                                t.wait_ge(dDN[2 * dqp + 1].h, dn_tgt[(e, 2 * dqp + 1)])
                            if idx == 0 and dqp == 0 and tq < 2:
                                t.wait_ge(sV.h, wdb_v)
                            if tq >= 2:
                                prev = (e, dqp, tq - 2)
                            elif dqp == 1:
                                prev = (e, 0, tq + 2)
                            elif idx >= 1:
                                prev = (EORD[idx - 1], 1, tq + 2)
                            else:
                                prev = None
                            if prev in ev_v:
                                t.wait_ge(sV.h, ev_v[prev])
                            for ft in range(FT):
                                if tq == 0 and dqp == 0:
                                    t.wait_ge(sV.h, su_v[(e, ft)])
                                for q in range(2):
                                    tt = tq * 2 + q
                                    for dqi in range(2):
                                        bank = q * 2 + dqi
                                        mm = t.matmul(
                                            PX[:, bank * 512:(bank + 1) * 512],
                                            su[idx % 2][:, ft * 1024 + tt * 128: ft * 1024 + (tt + 1) * 128],
                                            dnw[2 * dqp + dqi][:, ft * 512:(ft + 1) * 512],
                                            start=(ft == 0), stop=(ft == FT - 1))
                            tick(sT, mm)
                        blk.tensor(_)
                        dn_mm[(e, dqp, tq)] = sT.n
                        if tq == 3:
                            dn_mm[(e, dqp)] = sT.n

                        def _(v, idx=idx, e=e, dqp=dqp, tq=tq, PX=PX):
                            v.wait_ge(sT.h, dn_mm[(e, dqp, tq)])
                            if idx == 0 and dqp == 0 and tq == 0:
                                v.wait_ge(sV.h, proj_done_v)
                            for q in range(2):
                                tt = tq * 2 + q
                                for dqi in range(2):
                                    dq = 2 * dqp + dqi
                                    bank = q * 2 + dqi
                                    ap = x1[:, tt * 2048 + dq * 512: tt * 2048 + (dq + 1) * 512]
                                    ii = v.tensor_tensor(ap, PX[:, bank * 512:(bank + 1) * 512],
                                                         ap, op=ALU.add)
                            tick(sV, ii)
                        blk.vector(_)
                        ev_v[(e, dqp, tq)] = sV.n
                if idx + 1 < NE:
                    for dq in range(4):
                        load_dn(idx + 1, dq)
            ffn_done_v = sV.n

            def _(sy):
                sy.wait_ge(sV.h, ffn_done_v)
                tick(dOUT, sy.dma_start(
                    out=out_h.rearrange("(tt p) c -> p tt c", p=128),
                    in_=x1[:, :].rearrange("p (tt c) -> p tt c", tt=8)), 16)
                sy.wait_ge(dOUT.h, dOUT.n)
            blk.sync(_)

            def _(g):
                g.wait_ge(dOUT.h, dOUT.n)
            blk.gpsimd(_)

    return nc


def _pack_core(inp, c):
    b, j = c // 2, c % 2
    wa = np.asarray(inp["L0_attn_w_attn"])
    qb, kb, vb = 1024 * j, 2048 + 1024 * j, 4096 + 1024 * j
    d = {}
    x = np.asarray(inp["L0_attn_x"])
    d["x"] = np.ascontiguousarray(x[b])
    d["xo"] = np.ascontiguousarray(x[b][j * 1024:(j + 1) * 1024])
    qk = np.concatenate([wa[:, qb:qb + 1024], wa[:, kb:kb + 1024]], axis=1)
    d["wqk"] = np.ascontiguousarray(
        qk.reshape(16, 128, 16, 128).transpose(2, 1, 0, 3).reshape(16, 128, 2048))
    wv = wa[:, vb:vb + 1024]
    d["wv"] = np.ascontiguousarray(
        wv.reshape(16, 128, 2, 512).transpose(2, 1, 0, 3).reshape(2, 128, 8192))
    wp = np.asarray(inp["L0_attn_w_proj"])
    d["wpj"] = np.ascontiguousarray(wp.reshape(16, 128, 2048).transpose(1, 0, 2))
    ba = np.asarray(inp["L0_attn_b_attn"])
    d["bqk"] = np.ascontiguousarray(
        np.concatenate([ba[qb:qb + 1024], ba[kb:kb + 1024]]).reshape(16, 128).T)
    d["bv"] = np.ascontiguousarray(ba[vb:vb + 1024].reshape(1, 1024))
    d["bpj"] = np.ascontiguousarray(np.asarray(inp["L0_attn_b_proj"]).reshape(1, 2048))
    rw = np.asarray(inp["L0_ffn_router_w"])
    d["rt"] = np.ascontiguousarray(rw.reshape(16, 128, 8).transpose(1, 0, 2).reshape(128, 128))
    d["rmsw"] = np.ascontiguousarray(np.asarray(inp["L0_ffn_rms_weight"]).reshape(16, 128).T)
    wg = np.asarray(inp["L0_ffn_w_gate"])
    wu = np.asarray(inp["L0_ffn_w_up"])
    wd = np.asarray(inp["L0_ffn_w_down"])
    swg = np.asarray(inp["L0_ffn_sw_gate"])
    swu = np.asarray(inp["L0_ffn_sw_up"])
    swd = np.asarray(inp["L0_ffn_sw_down"])
    gu = np.empty((NE, FT, 2, 128, 2048), np.float32)
    for e in range(NE):
        g = wg[e] if e < 8 else swg
        u = wu[e] if e < 8 else swu
        gu[e, :, 0] = g.reshape(16, 128, 8, 128).transpose(2, 1, 0, 3).reshape(8, 128, 2048)
        gu[e, :, 1] = u.reshape(16, 128, 8, 128).transpose(2, 1, 0, 3).reshape(8, 128, 2048)
    d["gu"] = np.ascontiguousarray(gu)
    dn = np.empty((NE, 4, 128, 4096), np.float32)
    for e in range(NE):
        D = wd[e] if e < 8 else swd
        dn[e] = D.reshape(8, 128, 4, 512).transpose(2, 1, 0, 3).reshape(4, 128, 4096)
    d["dn"] = np.ascontiguousarray(dn)
    i_idx = np.arange(128)[:, None]
    u_idx = np.arange(512)[None, :]
    mskb = np.empty((128, 2048), BF)
    for s in range(4):
        mskb[:, s * 512:(s + 1) * 512] = (u_idx >= i_idx + 128 * s).astype(BF)
    d["msk"] = mskb
    d["id32"] = np.eye(128, dtype=np.float32)
    selb = np.zeros((8, 1024), BF)
    for e in range(8):
        selb[e, e * 128:(e + 1) * 128] = 1
    d["selb"] = selb
    return d


def kernel(**inputs):
    in_maps = [_pack_core(inputs, c) for c in range(8)]
    nc = _build()
    res = run_bass_kernel_spmd(nc, in_maps, core_ids=list(range(8)))
    out = np.empty((4, 2048, 2048), np.float32)
    for c in range(8):
        b, j = c // 2, c % 2
        out[b, j * 1024:(j + 1) * 1024, :] = res.results[c]["out"]
    return out


# version 09
# speedup vs baseline: 1.0207x; 1.0105x over previous
import math
import numpy as np
import ml_dtypes

import concourse.bass as bass
import concourse.mybir as mybir
import concourse.bass_utils as _bu
from concourse.bass_utils import run_bass_kernel_spmd
from contextlib import ExitStack

if not getattr(_bu, "_ldwopt_patched", False):
    _orig_run_command = _bu.run_command

    def _run_command_ldwopt(argv, **kwargs):
        argv = ["--enable-ldw-opt=true" if a == "--enable-ldw-opt=false" else a
                for a in argv]
        return _orig_run_command(argv, **kwargs)

    _bu.run_command = _run_command_ldwopt
    _bu._ldwopt_patched = True

F32 = mybir.dt.float32
BF16 = mybir.dt.bfloat16
AX = mybir.AxisListType.X
ALU = mybir.AluOpType
ACT = mybir.ActivationFunctionType
BF = ml_dtypes.bfloat16

NE = 9
FT = 8
SCL = 1.0 / math.sqrt(128.0)
EORD = [8, 0, 1, 2, 3, 4, 5, 6, 7]


class Ctr:
    def __init__(self, h):
        self.h = h
        self.n = 0


def tick(ctr, inst, k=1):
    inst.then_inc(ctr.h, k)
    ctr.n += k
    return ctr.n


def _build():
    nc = bass.Bass("TRN2", target_bir_lowering=False, debug=False)

    x_h = nc.dram_tensor("x", [2048, 2048], F32, kind="ExternalInput").ap()
    xo_h = nc.dram_tensor("xo", [1024, 2048], F32, kind="ExternalInput").ap()
    wqk_h = nc.dram_tensor("wqk", [16, 128, 2048], F32, kind="ExternalInput").ap()
    wv_h = nc.dram_tensor("wv", [2, 128, 8192], F32, kind="ExternalInput").ap()
    wpj_h = nc.dram_tensor("wpj", [128, 16, 2048], F32, kind="ExternalInput").ap()
    bqk_h = nc.dram_tensor("bqk", [128, 16], F32, kind="ExternalInput").ap()
    bv_h = nc.dram_tensor("bv", [1, 1024], F32, kind="ExternalInput").ap()
    bpj_h = nc.dram_tensor("bpj", [1, 2048], F32, kind="ExternalInput").ap()
    rt_h = nc.dram_tensor("rt", [128, 128], F32, kind="ExternalInput").ap()
    rmsw_h = nc.dram_tensor("rmsw", [128, 16], F32, kind="ExternalInput").ap()
    gu_h = nc.dram_tensor("gu", [NE, FT, 2, 128, 2048], F32, kind="ExternalInput").ap()
    dn_h = nc.dram_tensor("dn", [NE, 4, 128, 4096], F32, kind="ExternalInput").ap()
    msk_h = nc.dram_tensor("msk", [128, 2048], BF16, kind="ExternalInput").ap()
    selb_h = nc.dram_tensor("selb", [8, 1024], BF16, kind="ExternalInput").ap()
    id32_h = nc.dram_tensor("id32", [128, 128], F32, kind="ExternalInput").ap()
    out_h = nc.dram_tensor("out", [1024, 2048], F32, kind="ExternalOutput").ap()

    wu_in = nc.dram_tensor("wu_in", [8, 128], BF16).ap()
    wu_out = nc.dram_tensor("wu_out", [64, 128], BF16, addr_space="Shared").ap()
    cc_in = nc.dram_tensor("cc_in", [1024, 2048], BF16).ap()
    cc_out = nc.dram_tensor("cc_out", [8192, 2048], BF16, addr_space="Shared").ap()

    with ExitStack() as ctx0:
        def mksem(name):
            return Ctr(ctx0.enter_context(nc.semaphore(name)))

        sT, sV, sS = mksem("sT"), mksem("sV"), mksem("sS")
        dX = [mksem(f"dX{i}") for i in range(4)]
        dTR = [mksem(f"dTR{i}") for i in range(4)]
        dWQK = [mksem("dWQK0"), mksem("dWQK1")]
        dWV = mksem("dWV")
        dWPJ = mksem("dWPJ")
        dGU = [mksem("dGU0"), mksem("dGU1")]
        dDN = [mksem(f"dDN{i}") for i in range(4)]
        dMISC = mksem("dMISC")
        dMISCG = mksem("dMISCG")
        dMSK = mksem("dMSK")
        dXO = mksem("dXO")
        dCCI = mksem("dCCI")
        dCCO = mksem("dCCO")
        dHT = [mksem(f"dHT{i}") for i in range(4)]
        dOUT = mksem("dOUT")
        cS = mksem("cS")
        wS = mksem("wS")

        PA = ctx0.enter_context(nc.psum_tensor("PA", [128, 2048], F32))
        PB = ctx0.enter_context(nc.psum_tensor("PB", [128, 2048], F32))

        def sbt(name, shape, dtype, **kw):
            return ctx0.enter_context(nc.sbuf_tensor(name, shape, dtype, **kw))

        ones_c = sbt("ones_c", [128, 1], BF16)
        ones_f = sbt("ones_f", [128, 128], BF16)
        ones_r = sbt("ones_r", [1, 128], BF16)
        qkc = sbt("qkc", [128, 4], F32)
        id32 = sbt("id32s", [128, 128], F32)
        bqk = sbt("bqks", [128, 16], F32)
        bv = sbt("bvs", [1, 1024], BF16)
        bpj = sbt("bpjs", [1, 2048], BF16)
        rts = sbt("rts", [128, 128], BF16)
        rmsw = sbt("rmsws", [128, 16], F32)
        negc = sbt("negc", [128, 8], F32)
        bn6s = [sbt(f"bn6_{i}", [128, 24], F32) for i in range(8)]
        mvs = [sbt(f"mv_{i}", [128, 2], F32) for i in range(8)]
        tcs = [sbt(f"tc_{i}", [128, 2], F32) for i in range(8)]
        acs = [sbt(f"ac_{i}", [128, 1], F32) for i in range(8)]
        bcs = [sbt(f"bc_{i}", [128, 1], F32) for i in range(8)]
        sm1 = sbt("sm1", [1, 4], F32)
        sm1b = sbt("sm1b", [1, 1], BF16)
        lgT = sbt("lgT", [128, 64], F32)
        e_sb = sbt("e_sb", [128, 64], F32)
        msel = sbt("msel", [128, 64], F32)
        mcol = sbt("mcol", [128, 8], F32)
        mneg = sbt("mneg", [128, 8], F32)
        wd_tok = sbt("wd_tok", [128, 64], F32)
        wd8 = sbt("wd8", [8, 1024], BF16)
        selb = sbt("selbs", [8, 1024], BF16)
        wdB = sbt("wdB", [128, 1024], BF16)

        blk = ctx0.enter_context(nc.Block())

        def _(v):
            v.memset(ones_c[:, :], 1.0)
            v.memset(ones_f[:, :], 1.0)
            tick(sV, v.memset(ones_r[:, :], 1.0))
        blk.vector(_)
        v_ones = sV.n

        def _(sy):
            tick(dMISC, sy.dma_start(out=id32[:, :], in_=id32_h), 16)
            tick(dMISC, sy.dma_start(out=selb[:, :], in_=selb_h), 16)
            tick(dMISC, sy.dma_start(out=bqk[:, :], in_=bqk_h), 16)
            tick(dMISC, sy.dma_start(out=rmsw[:, :], in_=rmsw_h), 16)
        blk.sync(_)
        misc_done = dMISC.n

        def _(g):
            tick(dMISCG, g.dma_start(out=bv[:, :], in_=bv_h), 16)
            tick(dMISCG, g.dma_start(out=bpj[:, :], in_=bpj_h), 16)
            tick(dMISCG, g.dma_start(out=rts[:, :], in_=rt_h), 16)
            g.dma_start(out=wu_in, in_=bv_h[0:1, 0:1024].rearrange("o (a b) -> (o a) b", a=8)).then_inc(dMISCG.h, 16)
            g.wait_ge(dMISCG.h, 64)
            tick(wS, g.collective_compute(
                "AllGather", ALU.bypass,
                replica_groups=[[0, 1, 2, 3, 4, 5, 6, 7]],
                ins=[wu_in], outs=[wu_out]))
        blk.gpsimd(_)
        dMISCG.n += 16
        misc_g = 48

        with ExitStack() as big1:
            ln1T = big1.enter_context(
                nc.sbuf_tensor("ln1T", [128, 16 * 2048], BF16, is_transpose_dest=True))

            v_all = big1.enter_context(nc.sbuf_tensor("v_all", [128, 16 * 1024], BF16))
            with ExitStack() as sub1a:
                x_t = [sub1a.enter_context(nc.sbuf_tensor(f"x_t{i}", [128, 2048], F32))
                       for i in range(4)]
                l_t = [sub1a.enter_context(nc.sbuf_tensor(f"l_t{i}", [128, 2048], BF16))
                       for i in range(4)]
                wvw = [sub1a.enter_context(nc.sbuf_tensor(f"wvw{i}", [128, 8192], BF16))
                       for i in range(2)]

                def _(g):
                    tick(dWV, g.dma_start(out=wvw[0][:, :], in_=wv_h[0]), 16)
                    tick(dWV, g.dma_start(out=wvw[1][:, :], in_=wv_h[1]), 16)
                blk.gpsimd(_)
                wv_done = dWV.n

                s_app = {}
                tr_done = {}
                v_ev = {}
                rc_vs = {}

                def emitA(i):
                    sl = i % 4

                    def _(sy, i=i, sl=sl):
                        if i >= 4:
                            sy.wait_ge(sS.h, s_app[i - 4])
                        tick(dX[sl], sy.dma_start(
                            out=x_t[sl][:, :], in_=x_h[i * 128:(i + 1) * 128, :]), 16)
                    blk.sync(_)

                    def _(v, i=i, sl=sl):
                        v.wait_ge(dX[sl].h, dX[sl].n)
                        if i >= 4:
                            v.wait_ge(sS.h, s_app[i - 4])
                        for cc in range(3):
                            v.bn_stats(bn6s[sl][:, cc * 6:(cc + 1) * 6],
                                       x_t[sl][:, cc * 512:(cc + 1) * 512])
                        n1 = tick(sV, v.bn_stats(bn6s[sl][:, 18:24], x_t[sl][:, 1536:2048]))
                        v.wait_ge(sV.h, n1)
                        n2 = tick(sV, v.bn_aggr(mvs[sl][:, :], bn6s[sl][:, :]))
                        v.wait_ge(sV.h, n2)
                        n3 = tick(sV, v.tensor_scalar_add(tcs[sl][:, 0:1], mvs[sl][:, 1:2], 1e-5))
                        v.wait_ge(sV.h, n3)
                        tick(sV, v.reciprocal(tcs[sl][:, 1:2], tcs[sl][:, 0:1]))
                    blk.vector(_)
                    rc_vs[i] = sV.n

                def emitB(i):
                    sl = i % 4

                    def _(s, i=i, sl=sl):
                        s.wait_ge(sV.h, rc_vs[i])
                        tick(sS, s.sqrt(acs[sl][:, :], tcs[sl][:, 1:2]))
                    blk.scalar(_)
                    sq_s = sS.n

                    def _(v, i=i, sl=sl, sq_s=sq_s):
                        v.wait_ge(sS.h, sq_s)
                        tick(sV, v.scalar_tensor_tensor(bcs[sl][:, :], mvs[sl][:, 0:1], -1.0,
                                                        acs[sl][:, :], op0=ALU.mult,
                                                        op1=ALU.mult))
                    blk.vector(_)
                    ab_v = sV.n

                    def _(s, i=i, sl=sl, ab_v=ab_v):
                        s.wait_ge(sV.h, ab_v)
                        if i >= 4:
                            s.wait_ge(dTR[sl].h, 16 * (i // 4))
                        tick(sS, s.activation(l_t[sl][:, :], x_t[sl][:, :], ACT.Identity,
                                              bias=bcs[sl][:, :], scale=acs[sl][:, :]))
                    blk.scalar(_)
                    s_app[i] = sS.n

                    def _(sy, i=i, sl=sl):
                        sy.wait_ge(sS.h, s_app[i])
                        o_v = ln1T[:, :].rearrange("p (ct tt) -> p ct tt", ct=16)[:, :, i * 128:(i + 1) * 128]
                        i_v = l_t[sl][:, :].rearrange("t (ct p) -> t ct p", ct=16)
                        tick(dTR[sl], sy.dma_start_transpose(o_v, i_v), 16)
                    blk.sync(_)
                    tr_done[i] = (sl, dTR[sl].n)

                    def _(t, i=i):
                        sl_, tgt = tr_done[i]
                        t.wait_ge(dTR[sl_].h, tgt)
                        if i == 0:
                            t.wait_ge(dMISCG.h, misc_g)
                            t.wait_ge(dWV.h, wv_done)
                            t.wait_ge(sV.h, v_ones)
                        if i >= 2:
                            t.wait_ge(sV.h, v_ev[i - 2])
                        base = (i % 2) * 2
                        for vch in range(2):
                            po = PA[:, (base + vch) * 512:(base + vch + 1) * 512]
                            t.matmul(po, ones_r[:, :], bv[:, vch * 512:(vch + 1) * 512],
                                     start=True, stop=False)
                            for kt in range(16):
                                mm = t.matmul(
                                    po,
                                    ln1T[:, kt * 2048 + i * 128: kt * 2048 + (i + 1) * 128],
                                    wvw[vch][:, kt * 512:(kt + 1) * 512],
                                    start=False, stop=(kt == 15))
                        tick(sT, mm)
                    blk.tensor(_)
                    v_mms[i] = sT.n

                def emitEv(i):
                    def _(v, i=i):
                        v.wait_ge(sT.h, v_mms[i])
                        base = (i % 2) * 2
                        tick(sV, v.tensor_copy(v_all[:, i * 1024:(i + 1) * 1024],
                                               PA[:, base * 512:(base + 2) * 512]))
                    blk.vector(_)
                    v_ev[i] = sV.n

                v_mms = {}
                for i in range(17):
                    if i < 16:
                        emitA(i)
                    if i >= 1:
                        emitB(i - 1)
                    if i >= 2:
                        emitEv(i - 2)
                emitEv(15)
                v_done_v = sV.n
                v_mm_last = sT.n

            with ExitStack() as sub3:
                yT = sub3.enter_context(nc.sbuf_tensor("yT", [128, 8 * 2048], BF16))
                expT = sub3.enter_context(nc.sbuf_tensor("expT", [128, 2048], BF16))
                msk = sub3.enter_context(nc.sbuf_tensor("msks", [128, 2048], BF16))
                qT = sub3.enter_context(nc.sbuf_tensor("qT", [128, 2048], BF16))
                kTl = sub3.enter_context(nc.sbuf_tensor("kTl", [128, 2048], BF16))
                rcp = sub3.enter_context(nc.sbuf_tensor("rcp", [128, 512], F32))
                rcpb = sub3.enter_context(nc.sbuf_tensor("rcpb", [128, 512], BF16))
                wqkw = [sub3.enter_context(nc.sbuf_tensor(f"wqkw{i}", [128, 2048], BF16))
                        for i in range(2)]

                def _(sy):
                    sy.wait_ge(sT.h, v_mm_last)
                    tick(dMSK, sy.dma_start(out=msk[:, :], in_=msk_h), 16)
                blk.sync(_)
                msk_done = dMSK.n

                qk_mm = {}
                wqk_tgt = {}

                def load_wqk(o):
                    sl = o % 2
                    m = (o >> 1) + 8 * (o & 1)

                    def _(g, o=o, sl=sl, m=m):
                        if o >= 2:
                            g.wait_ge(sT.h, qk_mm[o - 2])
                        else:
                            g.wait_ge(sT.h, v_mm_last)
                        tick(dWQK[sl], g.dma_start(out=wqkw[sl][:, :], in_=wqk_h[m]), 16)
                    blk.gpsimd(_)
                    wqk_tgt[o] = dWQK[sl].n

                load_wqk(0)
                load_wqk(1)

                paS = 0
                paV = v_done_v
                att_t_last = 0
                exp_last_s = 0
                y_ev_v = 0
                sum_rd_v = 0
                bc_ev_s = 0
                negc_prev_v = 0
                km_v = 0

                for h in range(8):
                    for which in range(2):
                        o = 2 * h + which
                        m = h if which == 0 else 8 + h
                        sl = o % 2
                        dst = qT if which == 0 else kTl

                        def _(t, o=o, sl=sl, paS=paS, paV=paV, exp_last_s=exp_last_s):
                            t.wait_ge(dWQK[sl].h, wqk_tgt[o])
                            if paS:
                                t.wait_ge(sS.h, paS)
                            if exp_last_s:
                                t.wait_ge(sS.h, exp_last_s)
                            if paV:
                                t.wait_ge(sV.h, paV)
                            for kt in range(16):
                                for cc in range(4):
                                    mm = t.matmul(
                                        PA[:, cc * 512:(cc + 1) * 512],
                                        wqkw[sl][:, kt * 128:(kt + 1) * 128],
                                        ln1T[:, kt * 2048 + cc * 512: kt * 2048 + (cc + 1) * 512],
                                        start=(kt == 0), stop=(kt == 15))
                            tick(sT, mm)
                        blk.tensor(_)
                        qk_mm[o] = sT.n

                        def _(s, o=o, m=m, dst=dst):
                            s.wait_ge(sT.h, qk_mm[o])
                            if o == 0:
                                s.wait_ge(dMISC.h, misc_done)
                            tick(sS, s.activation(dst[:, :], PA[:, :], ACT.Identity,
                                                  bias=bqk[:, m:m + 1], scale=1.0))
                        blk.scalar(_)
                        paS = sS.n
                        paV = 0
                        exp_last_s = 0
                        if o + 2 < 16:
                            load_wqk(o + 2)

                        if which == 0:
                            q_ev = sS.n

                            def _(v, h=h, q_ev=q_ev, att_t_last=att_t_last):
                                v.wait_ge(sS.h, q_ev)
                                if att_t_last:
                                    v.wait_ge(sT.h, att_t_last)
                                tick(sV, v.tensor_tensor(expT[:, :], qT[:, :], qT[:, :],
                                                         op=ALU.mult))
                            blk.vector(_)
                            sqq_v = sV.n

                            def _(t, h=h, sqq_v=sqq_v, y_ev_v=y_ev_v, sum_rd_v=sum_rd_v):
                                t.wait_ge(sV.h, sqq_v)
                                if y_ev_v:
                                    t.wait_ge(sV.h, y_ev_v)
                                if sum_rd_v:
                                    t.wait_ge(sV.h, sum_rd_v)
                                for cc in range(3):
                                    t.matmul(PB[:, cc * 512:(cc + 1) * 512], ones_f[:, :],
                                             expT[:, cc * 512:(cc + 1) * 512],
                                             start=True, stop=True)
                                tick(sT, t.matmul(PB[:, 1536:2048], ones_f[:, :],
                                                  expT[:, 1536:2048], start=True, stop=True))
                            blk.tensor(_)
                            nq_t = sT.n

                            def _(v, nq_t=nq_t):
                                v.wait_ge(sT.h, nq_t)
                                tick(sV, v.reduce_max(qkc[:, 0:1], PB[:, 0:2048], axis=AX))
                            blk.vector(_)
                            redq_v = sV.n
                    qk_ev = sS.n

                    def _(v, h=h, qk_ev=qk_ev, nq_t=nq_t):
                        v.wait_ge(sS.h, qk_ev)
                        v.wait_ge(sT.h, nq_t)
                        tick(sV, v.tensor_tensor(expT[:, :], kTl[:, :], kTl[:, :],
                                                 op=ALU.mult))
                    blk.vector(_)
                    sqk_v = sV.n

                    def _(t, h=h, sqk_v=sqk_v, redq_v=redq_v):
                        t.wait_ge(sV.h, sqk_v)
                        t.wait_ge(sV.h, redq_v)
                        for cc in range(3):
                            t.matmul(PB[:, cc * 512:(cc + 1) * 512], ones_f[:, :],
                                     expT[:, cc * 512:(cc + 1) * 512], start=True, stop=True)
                        tick(sT, t.matmul(PB[:, 1536:2048], ones_f[:, :],
                                          expT[:, 1536:2048], start=True, stop=True))
                    blk.tensor(_)
                    nk_t = sT.n

                    def _(v, h=h, nk_t=nk_t):
                        v.wait_ge(sT.h, nk_t)
                        n1 = tick(sV, v.reduce_max(qkc[:, 1:2], PB[:, 0:2048], axis=AX))
                        v.wait_ge(sV.h, n1)
                        n2 = tick(sV, v.tensor_tensor(qkc[:, 2:3], qkc[:, 0:1], qkc[:, 1:2],
                                                      op=ALU.mult))
                        v.wait_ge(sV.h, n2)
                        tick(sV, v.tensor_scalar_mul(qkc[:, 2:3], qkc[:, 2:3], 1.0 / 128.0))
                    blk.vector(_)
                    km_v = sV.n

                    def _(s, h=h, km_v=km_v):
                        s.wait_ge(sV.h, km_v)
                        tick(sS, s.sqrt(qkc[:, 3:4], qkc[:, 2:3]))
                    blk.scalar(_)
                    cqs_s = sS.n

                    def _(v, h=h, cqs_s=cqs_s):
                        v.wait_ge(sS.h, cqs_s)
                        tick(sV, v.tensor_scalar_mul(negc[:, h:h + 1], qkc[:, 3:4], -1.0))
                    blk.vector(_)
                    negc_s = sV.n

                    exp_s = {0: 0, 1: 0}
                    av_t = {0: att_t_last, 1: att_t_last}
                    for cc4 in range(4):
                        nk = 4 * (cc4 + 1)
                        npair = nk // 2
                        for p in range(npair):
                            slot = p % 2

                            def _(t, h=h, cc4=cc4, p=p, slot=slot, km_v=km_v):
                                if exp_s[slot]:
                                    t.wait_ge(sS.h, exp_s[slot])
                                if cc4 == 0 and p < 2:
                                    t.wait_ge(sV.h, km_v)
                                for q in range(2):
                                    kt = 2 * p + q
                                    bank = 2 * slot + q
                                    mm = t.matmul(PA[:, bank * 512:(bank + 1) * 512],
                                                  kTl[:, kt * 128:(kt + 1) * 128],
                                                  qT[:, cc4 * 512:(cc4 + 1) * 512],
                                                  start=True, stop=True)
                                tick(sT, mm)
                            blk.tensor(_)
                            sc_t = sT.n

                            def _(s, h=h, slot=slot, sc_t=sc_t, negc_s=negc_s):
                                s.wait_ge(sT.h, sc_t)
                                s.wait_ge(sV.h, negc_s)
                                if av_t[slot]:
                                    s.wait_ge(sT.h, av_t[slot])
                                tick(sS, s.activation(expT[:, slot * 1024:(slot + 1) * 1024],
                                                      PA[:, slot * 1024:(slot + 1) * 1024],
                                                      ACT.Exp, bias=negc[:, h:h + 1],
                                                      scale=SCL))
                            blk.scalar(_)
                            exp_s[slot] = sS.n
                            cur_exp = sS.n

                            mv_cur = 0
                            if p >= 2 * cc4:
                                moff = (p - 2 * cc4) * 1024
                                def _(v, cur_exp=cur_exp, h=h, cc4=cc4, slot=slot,
                                      moff=moff):
                                    if h == 0 and cc4 == 0 and moff == 0:
                                        v.wait_ge(dMSK.h, msk_done)
                                    v.wait_ge(sS.h, cur_exp)
                                    sl_ap = expT[:, slot * 1024:(slot + 1) * 1024]
                                    tick(sV, v.tensor_tensor(sl_ap, sl_ap,
                                                             msk[:, moff:moff + 1024],
                                                             op=ALU.mult))
                                blk.vector(_)
                                mv_cur = sV.n

                            def _(t, h=h, cc4=cc4, p=p, slot=slot, nk=nk,
                                  cur_exp=cur_exp, mv_cur=mv_cur,
                                  y_ev_v=y_ev_v, sum_rd_v=sum_rd_v):
                                t.wait_ge(sS.h, cur_exp)
                                if mv_cur:
                                    t.wait_ge(sV.h, mv_cur)
                                if p == 0:
                                    if y_ev_v:
                                        t.wait_ge(sV.h, y_ev_v)
                                    if sum_rd_v:
                                        t.wait_ge(sV.h, sum_rd_v)
                                for q in range(2):
                                    kt = 2 * p + q
                                    rhs = expT[:, slot * 1024 + q * 512: slot * 1024 + (q + 1) * 512]
                                    t.matmul(PB[:, 512:1024],
                                             v_all[:, kt * 1024 + h * 128: kt * 1024 + (h + 1) * 128],
                                             rhs, start=(kt == 0), stop=(kt == nk - 1))
                                    mm = t.matmul(PB[:, 1024:1536], ones_f[:, :], rhs,
                                                  start=(kt == 0), stop=(kt == nk - 1))
                                tick(sT, mm)
                            blk.tensor(_)
                            av_t[slot] = sT.n
                        av_fin = sT.n

                        def _(v, av_fin=av_fin):
                            v.wait_ge(sT.h, av_fin)
                            n1 = tick(sV, v.reciprocal(rcp[:, :], PB[:, 1024:1536]))
                            v.wait_ge(sV.h, n1)
                            tick(sV, v.tensor_copy(rcpb[:, :], rcp[:, :]))
                        blk.vector(_)
                        sum_rd_v = sV.n

                        def _(v, h=h, cc4=cc4, sum_rd_v=sum_rd_v):
                            v.wait_ge(sV.h, sum_rd_v)
                            tick(sV, v.tensor_tensor(
                                yT[:, h * 2048 + cc4 * 512: h * 2048 + (cc4 + 1) * 512],
                                PB[:, 512:1024], rcpb[:, :], op=ALU.mult))
                        blk.vector(_)
                        y_ev_v = sV.n
                    att_t_last = sT.n
                    exp_last_s = max(exp_s[0], exp_s[1])
                    paS = 0
                    paV = 0
                y_done_v = sV.n

                def _(g):
                    g.wait_ge(sV.h, y_done_v)
                    o_v = cc_in.rearrange("(hh p) t -> p hh t", p=128)
                    i_v = yT[:, :].rearrange("p (hh t) -> p hh t", hh=8)
                    tick(dCCI, g.dma_start(out=o_v, in_=i_v), 16)
                    g.wait_ge(dCCI.h, dCCI.n)
                    g.wait_ge(wS.h, wS.n)
                    tick(cS, g.collective_compute(
                        "AllGather", ALU.bypass,
                        replica_groups=[[0, 1, 2, 3, 4, 5, 6, 7]],
                        ins=[cc_in], outs=[cc_out]))
                blk.gpsimd(_)

        x1 = ctx0.enter_context(nc.sbuf_tensor("x1", [128, 8 * 2048], F32))
        with ExitStack() as big2:
            wpj = big2.enter_context(nc.sbuf_tensor("wpjs", [128, 16 * 2048], BF16))
            y_all = big2.enter_context(nc.sbuf_tensor("y_all", [128, 16 * 1024], BF16))

            def _(sy):
                sy.wait_ge(sV.h, y_done_v)
                sy.wait_ge(dCCI.h, 16)
                tick(dXO, sy.dma_start(
                    out=x1[:, :].rearrange("p (tt c) -> p tt c", tt=8),
                    in_=xo_h.rearrange("(tt p) c -> p tt c", p=128)), 16)
            blk.sync(_)
            xo_done = dXO.n

            def _(g):
                for yf in range(16):
                    tick(dWPJ, g.dma_start(out=wpj[:, yf * 2048:(yf + 1) * 2048],
                                           in_=wpj_h[:, yf, :]), 16)
            blk.gpsimd(_)
            wpj_done = dWPJ.n

            def _(g):
                g.wait_ge(cS.h, cS.n)
                pid = nc.partition_id(engines=[mybir.EngineType.Pool])
                j = pid % 2
                base = pid - j
                for half in range(2):
                    src = cc_out[bass.ts(base + half, 1024), :][:, bass.ts(j, 1024)]
                    src = src.rearrange("(hh p) t -> p hh t", p=128)
                    dst = y_all[:, half * 8192:(half + 1) * 8192].rearrange(
                        "p (hh t) -> p hh t", hh=8)
                    tick(dCCO, g.dma_start(out=dst, in_=src), 16)
            blk.gpsimd(_)
            cco_done = dCCO.n

            s_h = {}
            rc_vs2 = {}

            def emitA2(i):
                sl = i

                def _(v, i=i, sl=sl):
                    v.wait_ge(sV.h, pj_ev[i])
                    for cc in range(3):
                        v.bn_stats(bn6s[sl][:, cc * 6:(cc + 1) * 6],
                                   x1[:, i * 2048 + cc * 512: i * 2048 + (cc + 1) * 512])
                    n1 = tick(sV, v.bn_stats(bn6s[sl][:, 18:24],
                                             x1[:, i * 2048 + 1536: i * 2048 + 2048]))
                    v.wait_ge(sV.h, n1)
                    n2 = tick(sV, v.bn_aggr(mvs[sl][:, :], bn6s[sl][:, :]))
                    v.wait_ge(sV.h, n2)
                    n3 = tick(sV, v.tensor_scalar(tcs[sl][:, 0:1], mvs[sl][:, 1:2],
                                                  1.000001, 1e-11,
                                                  op0=ALU.mult, op1=ALU.add))
                    v.wait_ge(sV.h, n3)
                    tick(sV, v.reciprocal(tcs[sl][:, 1:2], tcs[sl][:, 0:1]))
                blk.vector(_)
                rc_vs2[i] = sV.n

            pj_mm = {}
            pj_ev = {}
            for tt in range(8):
                PX = PA if tt % 2 == 0 else PB

                def _(t, tt=tt, PX=PX):
                    if tt == 0:
                        t.wait_ge(dCCO.h, cco_done)
                        t.wait_ge(dWPJ.h, wpj_done)
                        t.wait_ge(dMISCG.h, misc_g)
                    if tt >= 2:
                        t.wait_ge(sV.h, pj_ev[tt - 2])
                    for dq in range(4):
                        t.matmul(PX[:, dq * 512:(dq + 1) * 512], ones_r[:, :],
                                 bpj[:, dq * 512:(dq + 1) * 512], start=True, stop=False)
                    for yf in range(16):
                        for dq in range(4):
                            mm = t.matmul(
                                PX[:, dq * 512:(dq + 1) * 512],
                                y_all[:, yf * 1024 + tt * 128: yf * 1024 + (tt + 1) * 128],
                                wpj[:, yf * 2048 + dq * 512: yf * 2048 + (dq + 1) * 512],
                                start=False, stop=(yf == 15))
                    tick(sT, mm)
                blk.tensor(_)
                pj_mm[tt] = sT.n

                def _(v, tt=tt, PX=PX):
                    v.wait_ge(sT.h, pj_mm[tt])
                    if tt == 0:
                        v.wait_ge(dXO.h, xo_done)
                    for dq in range(3):
                        sl = x1[:, tt * 2048 + dq * 512: tt * 2048 + (dq + 1) * 512]
                        v.tensor_tensor(sl, PX[:, dq * 512:(dq + 1) * 512], sl, op=ALU.add)
                    sl = x1[:, tt * 2048 + 1536: tt * 2048 + 2048]
                    tick(sV, v.tensor_tensor(sl, PX[:, 1536:2048], sl, op=ALU.add))
                blk.vector(_)
                pj_ev[tt] = sV.n
                emitA2(tt)
            proj_done_v = sV.n

        with ExitStack() as big3:
            hT = big3.enter_context(
                nc.sbuf_tensor("hT", [128, 16 * 1024], BF16, is_transpose_dest=True))
            su = [big3.enter_context(nc.sbuf_tensor(f"su{i}", [128, 8 * 1024], BF16))
                  for i in range(2)]
            guw = [big3.enter_context(nc.sbuf_tensor(f"guw{i}", [128, 4096], BF16))
                   for i in range(2)]
            dnw = [big3.enter_context(nc.sbuf_tensor(f"dnw{i}", [128, 4096], BF16))
                   for i in range(4)]
            h_t = [big3.enter_context(nc.sbuf_tensor(f"h_t{i}", [128, 2048], BF16))
                   for i in range(2)]
            silu_sb = big3.enter_context(nc.sbuf_tensor("silu_sb", [128, 1024], BF16))
            lg_sb = big3.enter_context(nc.sbuf_tensor("lg_sb", [8, 1024], F32))

            def emitB2(i):
                sl = i
                hsl = i % 2

                def _(s, i=i, sl=sl):
                    s.wait_ge(sV.h, rc_vs2[i])
                    if i == 0:
                        s.wait_ge(sT.h, pj_mm[7])
                    tick(sS, s.sqrt(acs[sl][:, :], tcs[sl][:, 1:2]))
                blk.scalar(_)
                sq_s = sS.n

                def _(v, i=i, sl=sl, sq_s=sq_s):
                    v.wait_ge(sS.h, sq_s)
                    tick(sV, v.scalar_tensor_tensor(bcs[sl][:, :], mvs[sl][:, 0:1], -1.0,
                                                    acs[sl][:, :], op0=ALU.mult, op1=ALU.mult))
                blk.vector(_)
                ab_v = sV.n

                def _(s, i=i, sl=sl, hsl=hsl, ab_v=ab_v):
                    s.wait_ge(sV.h, ab_v)
                    if i >= 2:
                        s.wait_ge(dHT[hsl].h, 16 * (i // 2))
                    tick(sS, s.activation(h_t[hsl][:, :], x1[:, i * 2048:(i + 1) * 2048],
                                          ACT.Identity, bias=bcs[sl][:, :], scale=acs[sl][:, :]))
                blk.scalar(_)
                s_h[i] = sS.n

                def _(sy, i=i, hsl=hsl):
                    sy.wait_ge(sS.h, s_h[i])
                    if i == 0:
                        sy.wait_ge(sT.h, pj_mm[7])
                    o_v = hT[:, :].rearrange("p (dt tt) -> p dt tt", dt=16)[:, :, i * 128:(i + 1) * 128]
                    i_v = h_t[hsl][:, :].rearrange("t (dt p) -> t dt p", dt=16)
                    tick(dHT[hsl], sy.dma_start_transpose(o_v, i_v), 16)
                blk.sync(_)

            for i in range(8):
                emitB2(i)
            ht_done = tuple(dHT[i].n for i in range(4))

            def _(v):
                for i in range(4):
                    v.wait_ge(dHT[i].h, ht_done[i])
                v.wait_ge(dMISC.h, misc_done)
                for dt in range(15):
                    v.tensor_scalar(hT[:, dt * 1024:(dt + 1) * 1024],
                                    hT[:, dt * 1024:(dt + 1) * 1024],
                                    rmsw[:, dt:dt + 1], None, op0=ALU.mult)
                tick(sV, v.tensor_scalar(hT[:, 15 * 1024:16 * 1024],
                                         hT[:, 15 * 1024:16 * 1024],
                                         rmsw[:, 15:16], None, op0=ALU.mult))
            blk.vector(_)
            hT_ready_v = sV.n

            def _(t):
                t.wait_ge(sV.h, hT_ready_v)
                t.wait_ge(dMISCG.h, misc_g)
                for cc in range(2):
                    for kt in range(16):
                        mm = t.matmul(PA[0:8, cc * 512:(cc + 1) * 512],
                                      rts[:, kt * 8:(kt + 1) * 8],
                                      hT[:, kt * 1024 + cc * 512: kt * 1024 + (cc + 1) * 512],
                                      start=(kt == 0), stop=(kt == 15))
                tick(sT, mm)
            blk.tensor(_)
            lg_t = sT.n

            def _(v):
                v.wait_ge(sT.h, lg_t)
                tick(sV, v.tensor_copy(lg_sb[:, :], PA[0:8, 0:1024]))
            blk.vector(_)
            lg_v = sV.n

            def _(t):
                t.wait_ge(sV.h, lg_v)
                t.wait_ge(dMISC.h, misc_done)
                for i in range(7):
                    t.transpose(PB[:, i * 8:(i + 1) * 8], lg_sb[:, i * 128:(i + 1) * 128],
                                id32[0:8, 0:8])
                tick(sT, t.transpose(PB[:, 56:64], lg_sb[:, 896:1024], id32[0:8, 0:8]))
            blk.tensor(_)
            lgT_t = sT.n

            def _(v):
                v.wait_ge(sT.h, lgT_t)
                n1 = tick(sV, v.tensor_copy(lgT[:, :], PB[:, 0:64]))
                v.wait_ge(sV.h, n1)
                for i in range(7):
                    v.reduce_max(mcol[:, i:i + 1], lgT[:, i * 8:(i + 1) * 8], axis=AX)
                n2 = tick(sV, v.reduce_max(mcol[:, 7:8], lgT[:, 56:64], axis=AX))
                v.wait_ge(sV.h, n2)
                tick(sV, v.tensor_scalar_mul(mneg[:, :], mcol[:, :], -1.0))
            blk.vector(_)
            m1_v = sV.n

            def _(s):
                s.wait_ge(sV.h, m1_v)
                for i in range(7):
                    s.activation(e_sb[:, i * 8:(i + 1) * 8], lgT[:, i * 8:(i + 1) * 8],
                                 ACT.Exp, bias=mneg[:, i:i + 1], scale=1.0)
                tick(sS, s.activation(e_sb[:, 56:64], lgT[:, 56:64],
                                      ACT.Exp, bias=mneg[:, 7:8], scale=1.0))
            blk.scalar(_)
            e_s = sS.n

            def _(v):
                v.wait_ge(sS.h, e_s)
                n1 = tick(sV, v.tensor_scalar(msel[:, :], e_sb[:, :], 1.0, -1.0,
                                              op0=ALU.is_ge, op1=ALU.mult))
                v.wait_ge(sV.h, n1)
                n2 = tick(sV, v.tensor_scalar_add(msel[:, :], msel[:, :], 1.0))
                v.wait_ge(sV.h, n2)
                n3 = tick(sV, v.tensor_tensor(msel[:, :], e_sb[:, :], msel[:, :], op=ALU.mult))
                v.wait_ge(sV.h, n3)
                for i in range(7):
                    v.reduce_max(mcol[:, i:i + 1], msel[:, i * 8:(i + 1) * 8], axis=AX)
                n4 = tick(sV, v.reduce_max(mcol[:, 7:8], msel[:, 56:64], axis=AX))
                v.wait_ge(sV.h, n4)
                for i in range(7):
                    v.tensor_scalar(wd_tok[:, i * 8:(i + 1) * 8], e_sb[:, i * 8:(i + 1) * 8],
                                    mcol[:, i:i + 1], None, op0=ALU.is_ge)
                n5 = tick(sV, v.tensor_scalar(wd_tok[:, 56:64], e_sb[:, 56:64],
                                              mcol[:, 7:8], None, op0=ALU.is_ge))
                v.wait_ge(sV.h, n5)
                n6 = tick(sV, v.tensor_tensor(wd_tok[:, :], e_sb[:, :], wd_tok[:, :], op=ALU.mult))
                v.wait_ge(sV.h, n6)
                n7 = tick(sV, v.tensor_scalar_add(mcol[:, :], mcol[:, :], 1.0))
                v.wait_ge(sV.h, n7)
                tick(sV, v.reciprocal(mcol[:, :], mcol[:, :]))
            blk.vector(_)
            w2_v = sV.n

            def _(s):
                s.wait_ge(sV.h, w2_v)
                for i in range(7):
                    s.activation(wd_tok[:, i * 8:(i + 1) * 8], wd_tok[:, i * 8:(i + 1) * 8],
                                 ACT.Identity, bias=0.0, scale=mcol[:, i:i + 1])
                tick(sS, s.activation(wd_tok[:, 56:64], wd_tok[:, 56:64],
                                      ACT.Identity, bias=0.0, scale=mcol[:, 7:8]))
            blk.scalar(_)
            wd_s = sS.n

            gu_mm = {}
            dn_mm = {}
            gu_tgt = {}
            dn_tgt = {}
            su_v = {}
            silu_s = {}
            ev_v = {}
            wdT_done = 0
            wdb_v = 0
            wdb_cp_v = 0

            def load_gu(idx, ft):
                e = EORD[idx]
                sl = ft % 2

                def _(g, idx=idx, e=e, ft=ft, sl=sl):
                    if (idx, ft) == (0, 0):
                        g.wait_ge(sT.h, pj_mm[7])
                    prev = (EORD[idx], ft - 2) if ft >= 2 else \
                        ((EORD[idx - 1], ft + 6) if idx >= 1 else None)
                    if prev in gu_mm:
                        g.wait_ge(sT.h, gu_mm[prev])
                    tick(dGU[sl], g.dma_start(out=guw[sl][:, 0:2048], in_=gu_h[e, ft, 0]), 16)
                    tick(dGU[sl], g.dma_start(out=guw[sl][:, 2048:4096], in_=gu_h[e, ft, 1]), 16)
                blk.gpsimd(_)
                gu_tgt[(e, ft)] = dGU[sl].n

            def load_dn(idx, dq):
                e = EORD[idx]

                def _(g, idx=idx, e=e, dq=dq):
                    if (idx, dq) == (0, 0):
                        g.wait_ge(sT.h, pj_mm[7])
                    if idx >= 1:
                        prev = (EORD[idx - 1], dq // 2)
                        if prev in dn_mm:
                            g.wait_ge(sT.h, dn_mm[prev])
                    tick(dDN[dq], g.dma_start(out=dnw[dq][:, :], in_=dn_h[e, dq]), 16)
                blk.gpsimd(_)
                dn_tgt[(e, dq)] = dDN[dq].n

            load_gu(0, 0)
            load_gu(0, 1)
            for dq in range(4):
                load_dn(0, dq)

            for idx in range(NE):
                e = EORD[idx]
                esl = idx % 2

                if e != 8:
                    def _(t, e=e, idx=idx, wdb_v=wdb_v):
                        t.wait_ge(sV.h, wdb_v)
                        t.wait_ge(sV.h, ev_v[(EORD[idx - 1], 1, 3)])
                        t.matmul(PB[:, 0:512], selb[:, e * 128:(e + 1) * 128],
                                 wd8[:, 0:512], start=True, stop=True)
                        tick(sT, t.matmul(PB[:, 512:1024], selb[:, e * 128:(e + 1) * 128],
                                          wd8[:, 512:1024], start=True, stop=True))
                    blk.tensor(_)
                    wdb_t = sT.n

                    def _(v, wdb_t=wdb_t):
                        v.wait_ge(sT.h, wdb_t)
                        tick(sV, v.tensor_copy(wdB[:, :], PB[:, 0:1024]))
                    blk.vector(_)
                    wdb_cp_v = sV.n

                for ft in range(FT):
                    sl = ft % 2

                    def _(t, idx=idx, e=e, ft=ft, sl=sl):
                        t.wait_ge(dGU[sl].h, gu_tgt[(e, ft)])
                        if (idx, ft) == (0, 0):
                            t.wait_ge(sV.h, hT_ready_v)
                            t.wait_ge(sV.h, lg_v)
                        prevk = (e, ft - 1) if ft >= 1 else \
                            ((EORD[idx - 1], 7) if idx >= 1 else None)
                        if prevk in su_v:
                            t.wait_ge(sV.h, su_v[prevk])
                        if ft == 0 and idx >= 1:
                            t.wait_ge(sV.h, ev_v[(EORD[idx - 1], 1, 3)])
                        for gui in range(2):
                            for kt in range(16):
                                for cc in range(2):
                                    bank = 2 * gui + cc
                                    mm = t.matmul(
                                        PA[:, bank * 512:(bank + 1) * 512],
                                        guw[sl][:, gui * 2048 + kt * 128: gui * 2048 + (kt + 1) * 128],
                                        hT[:, kt * 1024 + cc * 512: kt * 1024 + (cc + 1) * 512],
                                        start=(kt == 0), stop=(kt == 15))
                        tick(sT, mm)
                    blk.tensor(_)
                    gu_mm[(e, ft)] = sT.n

                    def _(s, e=e, ft=ft, idx=idx):
                        s.wait_ge(sT.h, gu_mm[(e, ft)])
                        prevk = (e, ft - 1) if ft >= 1 else \
                            ((EORD[idx - 1], 7) if idx >= 1 else None)
                        if prevk in su_v:
                            s.wait_ge(sV.h, su_v[prevk])
                        tick(sS, s.activation(silu_sb[:, :], PA[:, 0:1024], ACT.Silu,
                                              bias=0.0, scale=1.0))
                    blk.scalar(_)
                    silu_s[(e, ft)] = sS.n

                    def _(v, e=e, ft=ft, idx=idx, esl=esl, wdb_cp_v=wdb_cp_v):
                        v.wait_ge(sS.h, silu_s[(e, ft)])
                        if e != 8 and ft == 0:
                            v.wait_ge(sV.h, wdb_cp_v)
                        if ft == 0 and idx >= 2:
                            v.wait_ge(sT.h, dn_mm[(EORD[idx - 2], 1)])
                        for cc in range(2):
                            ap = su[esl][:, ft * 1024 + cc * 512: ft * 1024 + (cc + 1) * 512]
                            ii = v.tensor_tensor(ap, silu_sb[:, cc * 512:(cc + 1) * 512],
                                                 PA[:, (2 + cc) * 512:(3 + cc) * 512],
                                                 op=ALU.mult)
                        n1 = tick(sV, ii)
                        if e != 8:
                            v.wait_ge(sV.h, n1)
                            for cc in range(2):
                                ap = su[esl][:, ft * 1024 + cc * 512: ft * 1024 + (cc + 1) * 512]
                                ii = v.tensor_tensor(ap, ap, wdB[:, cc * 512:(cc + 1) * 512],
                                                     op=ALU.mult)
                            tick(sV, ii)
                    blk.vector(_)
                    su_v[(e, ft)] = sV.n

                    nxt = (idx, ft + 2) if ft + 2 < FT else (idx + 1, ft + 2 - FT)
                    if nxt[0] < NE:
                        load_gu(*nxt)

                if idx == 0:
                    def _(t):
                        t.wait_ge(sS.h, wd_s)
                        for i in range(7):
                            t.transpose(PB[0:8, 512 + i * 128: 512 + (i + 1) * 128],
                                        wd_tok[:, i * 8:(i + 1) * 8], id32[:, :])
                        tick(sT, t.transpose(PB[0:8, 1408:1536], wd_tok[:, 56:64], id32[:, :]))
                    blk.tensor(_)
                    wdT_done = sT.n

                    def _(v):
                        v.wait_ge(sT.h, wdT_done)
                        tick(sV, v.tensor_copy(wd8[:, :], PB[0:8, 512:1536]))
                    blk.vector(_)
                    wdb_v = sV.n

                for dqp in range(2):
                    for tq in range(4):
                        PX = PB if tq % 2 == 0 else PA

                        def _(t, idx=idx, e=e, dqp=dqp, tq=tq, PX=PX, wdb_v=wdb_v):
                            if tq == 0:
                                t.wait_ge(dDN[2 * dqp].h, dn_tgt[(e, 2 * dqp)])
                                t.wait_ge(dDN[2 * dqp + 1].h, dn_tgt[(e, 2 * dqp + 1)])
                            if idx == 0 and dqp == 0 and tq < 2:
                                t.wait_ge(sV.h, wdb_v)
                            if tq >= 2:
                                prev = (e, dqp, tq - 2)
                            elif dqp == 1:
                                prev = (e, 0, tq + 2)
                            elif idx >= 1:
                                prev = (EORD[idx - 1], 1, tq + 2)
                            else:
                                prev = None
                            if prev in ev_v:
                                t.wait_ge(sV.h, ev_v[prev])
                            for ft in range(FT):
                                if tq == 0 and dqp == 0:
                                    t.wait_ge(sV.h, su_v[(e, ft)])
                                for q in range(2):
                                    tt = tq * 2 + q
                                    for dqi in range(2):
                                        bank = q * 2 + dqi
                                        mm = t.matmul(
                                            PX[:, bank * 512:(bank + 1) * 512],
                                            su[idx % 2][:, ft * 1024 + tt * 128: ft * 1024 + (tt + 1) * 128],
                                            dnw[2 * dqp + dqi][:, ft * 512:(ft + 1) * 512],
                                            start=(ft == 0), stop=(ft == FT - 1))
                            tick(sT, mm)
                        blk.tensor(_)
                        dn_mm[(e, dqp, tq)] = sT.n
                        if tq == 3:
                            dn_mm[(e, dqp)] = sT.n

                        def _(v, idx=idx, e=e, dqp=dqp, tq=tq, PX=PX):
                            v.wait_ge(sT.h, dn_mm[(e, dqp, tq)])
                            if idx == 0 and dqp == 0 and tq == 0:
                                v.wait_ge(sV.h, proj_done_v)
                            for q in range(2):
                                tt = tq * 2 + q
                                for dqi in range(2):
                                    dq = 2 * dqp + dqi
                                    bank = q * 2 + dqi
                                    ap = x1[:, tt * 2048 + dq * 512: tt * 2048 + (dq + 1) * 512]
                                    ii = v.tensor_tensor(ap, PX[:, bank * 512:(bank + 1) * 512],
                                                         ap, op=ALU.add)
                            tick(sV, ii)
                        blk.vector(_)
                        ev_v[(e, dqp, tq)] = sV.n
                if idx + 1 < NE:
                    for dq in range(4):
                        load_dn(idx + 1, dq)
            ffn_done_v = sV.n

            def _(sy):
                sy.wait_ge(sV.h, ffn_done_v)
                tick(dOUT, sy.dma_start(
                    out=out_h.rearrange("(tt p) c -> p tt c", p=128),
                    in_=x1[:, :].rearrange("p (tt c) -> p tt c", tt=8)), 16)
                sy.wait_ge(dOUT.h, dOUT.n)
            blk.sync(_)

            def _(g):
                g.wait_ge(dOUT.h, dOUT.n)
            blk.gpsimd(_)

    return nc


def _pack_core(inp, c):
    b, j = c // 2, c % 2
    wa = np.asarray(inp["L0_attn_w_attn"])
    qb, kb, vb = 1024 * j, 2048 + 1024 * j, 4096 + 1024 * j
    d = {}
    x = np.asarray(inp["L0_attn_x"])
    d["x"] = np.ascontiguousarray(x[b])
    d["xo"] = np.ascontiguousarray(x[b][j * 1024:(j + 1) * 1024])
    qk = np.concatenate([wa[:, qb:qb + 1024], wa[:, kb:kb + 1024]], axis=1)
    d["wqk"] = np.ascontiguousarray(
        qk.reshape(16, 128, 16, 128).transpose(2, 1, 0, 3).reshape(16, 128, 2048))
    wv = wa[:, vb:vb + 1024]
    d["wv"] = np.ascontiguousarray(
        wv.reshape(16, 128, 2, 512).transpose(2, 1, 0, 3).reshape(2, 128, 8192))
    wp = np.asarray(inp["L0_attn_w_proj"])
    d["wpj"] = np.ascontiguousarray(wp.reshape(16, 128, 2048).transpose(1, 0, 2))
    ba = np.asarray(inp["L0_attn_b_attn"])
    d["bqk"] = np.ascontiguousarray(
        np.concatenate([ba[qb:qb + 1024], ba[kb:kb + 1024]]).reshape(16, 128).T)
    d["bv"] = np.ascontiguousarray(ba[vb:vb + 1024].reshape(1, 1024))
    d["bpj"] = np.ascontiguousarray(np.asarray(inp["L0_attn_b_proj"]).reshape(1, 2048))
    rw = np.asarray(inp["L0_ffn_router_w"])
    d["rt"] = np.ascontiguousarray(rw.reshape(16, 128, 8).transpose(1, 0, 2).reshape(128, 128))
    d["rmsw"] = np.ascontiguousarray(np.asarray(inp["L0_ffn_rms_weight"]).reshape(16, 128).T)
    wg = np.asarray(inp["L0_ffn_w_gate"])
    wu = np.asarray(inp["L0_ffn_w_up"])
    wd = np.asarray(inp["L0_ffn_w_down"])
    swg = np.asarray(inp["L0_ffn_sw_gate"])
    swu = np.asarray(inp["L0_ffn_sw_up"])
    swd = np.asarray(inp["L0_ffn_sw_down"])
    gu = np.empty((NE, FT, 2, 128, 2048), np.float32)
    for e in range(NE):
        g = wg[e] if e < 8 else swg
        u = wu[e] if e < 8 else swu
        gu[e, :, 0] = g.reshape(16, 128, 8, 128).transpose(2, 1, 0, 3).reshape(8, 128, 2048)
        gu[e, :, 1] = u.reshape(16, 128, 8, 128).transpose(2, 1, 0, 3).reshape(8, 128, 2048)
    d["gu"] = np.ascontiguousarray(gu)
    dn = np.empty((NE, 4, 128, 4096), np.float32)
    for e in range(NE):
        D = wd[e] if e < 8 else swd
        dn[e] = D.reshape(8, 128, 4, 512).transpose(2, 1, 0, 3).reshape(4, 128, 4096)
    d["dn"] = np.ascontiguousarray(dn)
    i_idx = np.arange(128)[:, None]
    u_idx = np.arange(512)[None, :]
    mskb = np.empty((128, 2048), BF)
    for s in range(4):
        mskb[:, s * 512:(s + 1) * 512] = (u_idx >= i_idx + 128 * s).astype(BF)
    d["msk"] = mskb
    d["id32"] = np.eye(128, dtype=np.float32)
    selb = np.zeros((8, 1024), BF)
    for e in range(8):
        selb[e, e * 128:(e + 1) * 128] = 1
    d["selb"] = selb
    return d


def kernel(**inputs):
    in_maps = [_pack_core(inputs, c) for c in range(8)]
    nc = _build()
    res = run_bass_kernel_spmd(nc, in_maps, core_ids=list(range(8)))
    out = np.empty((4, 2048, 2048), np.float32)
    for c in range(8):
        b, j = c // 2, c % 2
        out[b, j * 1024:(j + 1) * 1024, :] = res.results[c]["out"]
    return out


# version 10
# speedup vs baseline: 1.0220x; 1.0012x over previous
import math
import numpy as np
import ml_dtypes

import concourse.bass as bass
import concourse.mybir as mybir
import concourse.bass_utils as _bu
from concourse.bass_utils import run_bass_kernel_spmd
from contextlib import ExitStack

if not getattr(_bu, "_ldwopt_patched", False):
    _orig_run_command = _bu.run_command

    def _run_command_ldwopt(argv, **kwargs):
        argv = ["--enable-ldw-opt=true" if a == "--enable-ldw-opt=false" else a
                for a in argv]
        return _orig_run_command(argv, **kwargs)

    _bu.run_command = _run_command_ldwopt
    _bu._ldwopt_patched = True

F32 = mybir.dt.float32
BF16 = mybir.dt.bfloat16
AX = mybir.AxisListType.X
ALU = mybir.AluOpType
ACT = mybir.ActivationFunctionType
BF = ml_dtypes.bfloat16

NE = 9
FT = 8
SCL = 1.0 / math.sqrt(128.0)
EORD = [8, 0, 1, 2, 3, 4, 5, 6, 7]


class Ctr:
    def __init__(self, h):
        self.h = h
        self.n = 0


def tick(ctr, inst, k=1):
    inst.then_inc(ctr.h, k)
    ctr.n += k
    return ctr.n


def _build():
    nc = bass.Bass("TRN2", target_bir_lowering=False, debug=False)

    x_h = nc.dram_tensor("x", [2048, 2048], F32, kind="ExternalInput").ap()
    xo_h = nc.dram_tensor("xo", [1024, 2048], F32, kind="ExternalInput").ap()
    wqk_h = nc.dram_tensor("wqk", [16, 128, 2048], F32, kind="ExternalInput").ap()
    wv_h = nc.dram_tensor("wv", [2, 128, 8192], F32, kind="ExternalInput").ap()
    wpj_h = nc.dram_tensor("wpj", [128, 16, 2048], F32, kind="ExternalInput").ap()
    bqk_h = nc.dram_tensor("bqk", [128, 16], F32, kind="ExternalInput").ap()
    bv_h = nc.dram_tensor("bv", [1, 1024], F32, kind="ExternalInput").ap()
    bpj_h = nc.dram_tensor("bpj", [1, 2048], F32, kind="ExternalInput").ap()
    rt_h = nc.dram_tensor("rt", [128, 128], F32, kind="ExternalInput").ap()
    rmsw_h = nc.dram_tensor("rmsw", [128, 16], F32, kind="ExternalInput").ap()
    gu_h = nc.dram_tensor("gu", [NE, FT, 2, 128, 2048], F32, kind="ExternalInput").ap()
    dn_h = nc.dram_tensor("dn", [NE, 4, 128, 4096], F32, kind="ExternalInput").ap()
    msk_h = nc.dram_tensor("msk", [128, 2048], BF16, kind="ExternalInput").ap()
    selb_h = nc.dram_tensor("selb", [8, 1024], BF16, kind="ExternalInput").ap()
    id32_h = nc.dram_tensor("id32", [128, 128], F32, kind="ExternalInput").ap()
    out_h = nc.dram_tensor("out", [1024, 2048], F32, kind="ExternalOutput").ap()

    wu_in = nc.dram_tensor("wu_in", [8, 128], BF16).ap()
    wu_out = nc.dram_tensor("wu_out", [64, 128], BF16, addr_space="Shared").ap()
    ydr = nc.dram_tensor("ydr", [1024, 2048], BF16).ap()
    cc_in = nc.dram_tensor("cc_in", [1024, 1024], BF16).ap()
    cc_out = nc.dram_tensor("cc_out", [8192, 1024], BF16, addr_space="Shared").ap()

    with ExitStack() as ctx0:
        def mksem(name):
            return Ctr(ctx0.enter_context(nc.semaphore(name)))

        sT, sV, sS = mksem("sT"), mksem("sV"), mksem("sS")
        dX = [mksem(f"dX{i}") for i in range(4)]
        dTR = [mksem(f"dTR{i}") for i in range(4)]
        dWQK = [mksem("dWQK0"), mksem("dWQK1")]
        dWV = mksem("dWV")
        dWPJ = mksem("dWPJ")
        dGU = [mksem("dGU0"), mksem("dGU1")]
        dDN = [mksem(f"dDN{i}") for i in range(4)]
        dMISC = mksem("dMISC")
        dMISCG = mksem("dMISCG")
        dMSK = mksem("dMSK")
        dXO = mksem("dXO")
        dYD = mksem("dYD")
        dCCI = mksem("dCCI")
        dCCL = mksem("dCCL")
        dCCO = mksem("dCCO")
        dHT = [mksem(f"dHT{i}") for i in range(4)]
        dOUT = mksem("dOUT")
        cS = mksem("cS")
        wS = mksem("wS")
        dWU = mksem("dWU")

        PA = ctx0.enter_context(nc.psum_tensor("PA", [128, 2048], F32))
        PB = ctx0.enter_context(nc.psum_tensor("PB", [128, 2048], F32))

        def sbt(name, shape, dtype, **kw):
            return ctx0.enter_context(nc.sbuf_tensor(name, shape, dtype, **kw))

        ones_c = sbt("ones_c", [128, 1], BF16)
        ones_f = sbt("ones_f", [128, 128], BF16)
        ones_r = sbt("ones_r", [1, 128], BF16)
        qkc = sbt("qkc", [128, 4], F32)
        id32 = sbt("id32s", [128, 128], F32)
        bqk = sbt("bqks", [128, 16], F32)
        bv = sbt("bvs", [1, 1024], BF16)
        bpj = sbt("bpjs", [1, 2048], BF16)
        rts = sbt("rts", [128, 128], BF16)
        rmsw = sbt("rmsws", [128, 16], F32)
        negc = sbt("negc", [128, 8], F32)
        bn6s = [sbt(f"bn6_{i}", [128, 24], F32) for i in range(8)]
        mvs = [sbt(f"mv_{i}", [128, 2], F32) for i in range(8)]
        tcs = [sbt(f"tc_{i}", [128, 2], F32) for i in range(8)]
        acs = [sbt(f"ac_{i}", [128, 1], F32) for i in range(8)]
        bcs = [sbt(f"bc_{i}", [128, 1], F32) for i in range(8)]
        sm1 = sbt("sm1", [1, 4], F32)
        sm1b = sbt("sm1b", [1, 1], BF16)
        lgT = sbt("lgT", [128, 64], F32)
        e_sb = sbt("e_sb", [128, 64], F32)
        msel = sbt("msel", [128, 64], F32)
        mcol = sbt("mcol", [128, 8], F32)
        mneg = sbt("mneg", [128, 8], F32)
        wd_tok = sbt("wd_tok", [128, 64], F32)
        wd8 = sbt("wd8", [8, 1024], BF16)
        selb = sbt("selbs", [8, 1024], BF16)
        wdB = sbt("wdB", [128, 1024], BF16)

        blk = ctx0.enter_context(nc.Block())

        def _(v):
            v.memset(ones_c[:, :], 1.0)
            v.memset(ones_f[:, :], 1.0)
            tick(sV, v.memset(ones_r[:, :], 1.0))
        blk.vector(_)
        v_ones = sV.n

        def _(sy):
            tick(dMISC, sy.dma_start(out=id32[:, :], in_=id32_h), 16)
            tick(dMISC, sy.dma_start(out=selb[:, :], in_=selb_h), 16)
            tick(dMISC, sy.dma_start(out=bqk[:, :], in_=bqk_h), 16)
            tick(dMISC, sy.dma_start(out=rmsw[:, :], in_=rmsw_h), 16)
        blk.sync(_)
        misc_done = dMISC.n

        def _(g):
            tick(dMISCG, g.dma_start(out=bv[:, :], in_=bv_h), 16)
            tick(dMISCG, g.dma_start(out=bpj[:, :], in_=bpj_h), 16)
            tick(dMISCG, g.dma_start(out=rts[:, :], in_=rt_h), 16)
            tick(dWU, g.dma_start(
                out=wu_in,
                in_=bv_h[0:1, 0:1024].rearrange("o (a b) -> (o a) b", a=8)), 16)
            g.wait_ge(dWU.h, 16)
            tick(wS, g.collective_compute(
                "AllGather", ALU.bypass,
                replica_groups=[[0, 1, 2, 3, 4, 5, 6, 7]],
                ins=[wu_in], outs=[wu_out]))
        blk.gpsimd(_)
        misc_g = dMISCG.n

        vsc = ExitStack()
        v_all = vsc.enter_context(nc.sbuf_tensor("v_all", [128, 16 * 1024], BF16,
                                                 side="right"))

        with ExitStack() as big1:
            ln1T = big1.enter_context(
                nc.sbuf_tensor("ln1T", [128, 16 * 2048], BF16, is_transpose_dest=True))
            with ExitStack() as sub1a:
                x_t = [sub1a.enter_context(nc.sbuf_tensor(f"x_t{i}", [128, 2048], F32))
                       for i in range(4)]
                l_t = [sub1a.enter_context(nc.sbuf_tensor(f"l_t{i}", [128, 2048], BF16))
                       for i in range(4)]
                wvw = [sub1a.enter_context(nc.sbuf_tensor(f"wvw{i}", [128, 8192], BF16))
                       for i in range(2)]

                def _(g):
                    tick(dWV, g.dma_start(out=wvw[0][:, :], in_=wv_h[0]), 16)
                    tick(dWV, g.dma_start(out=wvw[1][:, :], in_=wv_h[1]), 16)
                blk.gpsimd(_)
                wv_done = dWV.n

                s_app = {}
                tr_done = {}
                v_ev = {}
                rc_vs = {}

                def emitA(i):
                    sl = i % 4

                    def _(sy, i=i, sl=sl):
                        if i >= 4:
                            sy.wait_ge(sS.h, s_app[i - 4])
                        tick(dX[sl], sy.dma_start(
                            out=x_t[sl][:, :], in_=x_h[i * 128:(i + 1) * 128, :]), 16)
                    blk.sync(_)

                    def _(v, i=i, sl=sl):
                        v.wait_ge(dX[sl].h, dX[sl].n)
                        if i >= 4:
                            v.wait_ge(sS.h, s_app[i - 4])
                        for cc in range(3):
                            v.bn_stats(bn6s[sl][:, cc * 6:(cc + 1) * 6],
                                       x_t[sl][:, cc * 512:(cc + 1) * 512])
                        n1 = tick(sV, v.bn_stats(bn6s[sl][:, 18:24], x_t[sl][:, 1536:2048]))
                        v.wait_ge(sV.h, n1)
                        n2 = tick(sV, v.bn_aggr(mvs[sl][:, :], bn6s[sl][:, :]))
                        v.wait_ge(sV.h, n2)
                        n3 = tick(sV, v.tensor_scalar_add(tcs[sl][:, 0:1], mvs[sl][:, 1:2], 1e-5))
                        v.wait_ge(sV.h, n3)
                        tick(sV, v.reciprocal(tcs[sl][:, 1:2], tcs[sl][:, 0:1]))
                    blk.vector(_)
                    rc_vs[i] = sV.n

                def emitB(i):
                    sl = i % 4

                    def _(s, i=i, sl=sl):
                        s.wait_ge(sV.h, rc_vs[i])
                        tick(sS, s.sqrt(acs[sl][:, :], tcs[sl][:, 1:2]))
                    blk.scalar(_)
                    sq_s = sS.n

                    def _(v, i=i, sl=sl, sq_s=sq_s):
                        v.wait_ge(sS.h, sq_s)
                        tick(sV, v.scalar_tensor_tensor(bcs[sl][:, :], mvs[sl][:, 0:1], -1.0,
                                                        acs[sl][:, :], op0=ALU.mult,
                                                        op1=ALU.mult))
                    blk.vector(_)
                    ab_v = sV.n

                    def _(s, i=i, sl=sl, ab_v=ab_v):
                        s.wait_ge(sV.h, ab_v)
                        if i >= 4:
                            s.wait_ge(dTR[sl].h, 16 * (i // 4))
                        tick(sS, s.activation(l_t[sl][:, :], x_t[sl][:, :], ACT.Identity,
                                              bias=bcs[sl][:, :], scale=acs[sl][:, :]))
                    blk.scalar(_)
                    s_app[i] = sS.n

                    def _(sy, i=i, sl=sl):
                        sy.wait_ge(sS.h, s_app[i])
                        o_v = ln1T[:, :].rearrange("p (ct tt) -> p ct tt", ct=16)[:, :, i * 128:(i + 1) * 128]
                        i_v = l_t[sl][:, :].rearrange("t (ct p) -> t ct p", ct=16)
                        tick(dTR[sl], sy.dma_start_transpose(o_v, i_v), 16)
                    blk.sync(_)
                    tr_done[i] = (sl, dTR[sl].n)

                    def _(t, i=i):
                        sl_, tgt = tr_done[i]
                        t.wait_ge(dTR[sl_].h, tgt)
                        if i == 0:
                            t.wait_ge(dMISCG.h, misc_g)
                            t.wait_ge(dWV.h, wv_done)
                            t.wait_ge(sV.h, v_ones)
                        if i >= 2:
                            t.wait_ge(sV.h, v_ev[i - 2])
                        base = (i % 2) * 2
                        for vch in range(2):
                            po = PA[:, (base + vch) * 512:(base + vch + 1) * 512]
                            t.matmul(po, ones_r[:, :], bv[:, vch * 512:(vch + 1) * 512],
                                     start=True, stop=False)
                            for kt in range(16):
                                mm = t.matmul(
                                    po,
                                    ln1T[:, kt * 2048 + i * 128: kt * 2048 + (i + 1) * 128],
                                    wvw[vch][:, kt * 512:(kt + 1) * 512],
                                    start=False, stop=(kt == 15))
                        tick(sT, mm)
                    blk.tensor(_)
                    v_mms[i] = sT.n

                def emitEv(i):
                    def _(v, i=i):
                        v.wait_ge(sT.h, v_mms[i])
                        base = (i % 2) * 2
                        tick(sV, v.tensor_copy(v_all[:, i * 1024:(i + 1) * 1024],
                                               PA[:, base * 512:(base + 2) * 512]))
                    blk.vector(_)
                    v_ev[i] = sV.n

                v_mms = {}
                for i in range(17):
                    if i < 16:
                        emitA(i)
                    if i >= 1:
                        emitB(i - 1)
                    if i >= 2:
                        emitEv(i - 2)
                emitEv(15)
                v_done_v = sV.n
                v_mm_last = sT.n

            with ExitStack() as sub3:
                yT = sub3.enter_context(nc.sbuf_tensor("yT", [128, 8 * 2048], BF16))
                expT = sub3.enter_context(nc.sbuf_tensor("expT", [128, 2048], BF16))
                msk = sub3.enter_context(nc.sbuf_tensor("msks", [128, 2048], BF16))
                qT = sub3.enter_context(nc.sbuf_tensor("qT", [128, 2048], BF16))
                kTl = sub3.enter_context(nc.sbuf_tensor("kTl", [128, 2048], BF16))
                rcp = sub3.enter_context(nc.sbuf_tensor("rcp", [128, 512], F32))
                rcpb = sub3.enter_context(nc.sbuf_tensor("rcpb", [128, 512], BF16))
                wqkw = [sub3.enter_context(nc.sbuf_tensor(f"wqkw{i}", [128, 2048], BF16))
                        for i in range(2)]

                def _(sy):
                    sy.wait_ge(sT.h, v_mm_last)
                    tick(dMSK, sy.dma_start(out=msk[:, :], in_=msk_h), 16)
                blk.sync(_)
                msk_done = dMSK.n

                qk_mm = {}
                wqk_tgt = {}

                def load_wqk(o):
                    sl = o % 2
                    m = (o >> 1) + 8 * (o & 1)

                    def _(g, o=o, sl=sl, m=m):
                        if o >= 2:
                            g.wait_ge(sT.h, qk_mm[o - 2])
                        else:
                            g.wait_ge(sT.h, v_mm_last)
                        tick(dWQK[sl], g.dma_start(out=wqkw[sl][:, :], in_=wqk_h[m]), 16)
                    blk.gpsimd(_)
                    wqk_tgt[o] = dWQK[sl].n

                load_wqk(0)
                load_wqk(1)

                paS = 0
                paV = v_done_v
                att_t_last = 0
                exp_last_s = 0
                y_ev_v = 0
                sum_rd_v = 0
                bc_ev_s = 0
                negc_prev_v = 0
                km_v = 0

                for h in range(8):
                    for which in range(2):
                        o = 2 * h + which
                        m = h if which == 0 else 8 + h
                        sl = o % 2
                        dst = qT if which == 0 else kTl

                        def _(t, o=o, sl=sl, paS=paS, paV=paV, exp_last_s=exp_last_s):
                            t.wait_ge(dWQK[sl].h, wqk_tgt[o])
                            if paS:
                                t.wait_ge(sS.h, paS)
                            if exp_last_s:
                                t.wait_ge(sS.h, exp_last_s)
                            if paV:
                                t.wait_ge(sV.h, paV)
                            for kt in range(16):
                                for cc in range(4):
                                    mm = t.matmul(
                                        PA[:, cc * 512:(cc + 1) * 512],
                                        wqkw[sl][:, kt * 128:(kt + 1) * 128],
                                        ln1T[:, kt * 2048 + cc * 512: kt * 2048 + (cc + 1) * 512],
                                        start=(kt == 0), stop=(kt == 15))
                            tick(sT, mm)
                        blk.tensor(_)
                        qk_mm[o] = sT.n

                        def _(s, o=o, m=m, dst=dst):
                            s.wait_ge(sT.h, qk_mm[o])
                            if o == 0:
                                s.wait_ge(dMISC.h, misc_done)
                            tick(sS, s.activation(dst[:, :], PA[:, :], ACT.Identity,
                                                  bias=bqk[:, m:m + 1], scale=1.0))
                        blk.scalar(_)
                        paS = sS.n
                        paV = 0
                        exp_last_s = 0
                        if o + 2 < 16:
                            load_wqk(o + 2)

                        if which == 0:
                            q_ev = sS.n

                            def _(v, h=h, q_ev=q_ev, att_t_last=att_t_last):
                                v.wait_ge(sS.h, q_ev)
                                if att_t_last:
                                    v.wait_ge(sT.h, att_t_last)
                                tick(sV, v.tensor_tensor(expT[:, :], qT[:, :], qT[:, :],
                                                         op=ALU.mult))
                            blk.vector(_)
                            sqq_v = sV.n

                            def _(t, h=h, sqq_v=sqq_v, y_ev_v=y_ev_v, sum_rd_v=sum_rd_v):
                                t.wait_ge(sV.h, sqq_v)
                                if y_ev_v:
                                    t.wait_ge(sV.h, y_ev_v)
                                if sum_rd_v:
                                    t.wait_ge(sV.h, sum_rd_v)
                                for cc in range(3):
                                    t.matmul(PB[:, cc * 512:(cc + 1) * 512], ones_f[:, :],
                                             expT[:, cc * 512:(cc + 1) * 512],
                                             start=True, stop=True)
                                tick(sT, t.matmul(PB[:, 1536:2048], ones_f[:, :],
                                                  expT[:, 1536:2048], start=True, stop=True))
                            blk.tensor(_)
                            nq_t = sT.n

                            def _(v, nq_t=nq_t):
                                v.wait_ge(sT.h, nq_t)
                                tick(sV, v.reduce_max(qkc[:, 0:1], PB[:, 0:2048], axis=AX))
                            blk.vector(_)
                            redq_v = sV.n
                    qk_ev = sS.n

                    def _(v, h=h, qk_ev=qk_ev, nq_t=nq_t):
                        v.wait_ge(sS.h, qk_ev)
                        v.wait_ge(sT.h, nq_t)
                        tick(sV, v.tensor_tensor(expT[:, :], kTl[:, :], kTl[:, :],
                                                 op=ALU.mult))
                    blk.vector(_)
                    sqk_v = sV.n

                    def _(t, h=h, sqk_v=sqk_v, redq_v=redq_v):
                        t.wait_ge(sV.h, sqk_v)
                        t.wait_ge(sV.h, redq_v)
                        for cc in range(3):
                            t.matmul(PB[:, cc * 512:(cc + 1) * 512], ones_f[:, :],
                                     expT[:, cc * 512:(cc + 1) * 512], start=True, stop=True)
                        tick(sT, t.matmul(PB[:, 1536:2048], ones_f[:, :],
                                          expT[:, 1536:2048], start=True, stop=True))
                    blk.tensor(_)
                    nk_t = sT.n

                    def _(v, h=h, nk_t=nk_t):
                        v.wait_ge(sT.h, nk_t)
                        n1 = tick(sV, v.reduce_max(qkc[:, 1:2], PB[:, 0:2048], axis=AX))
                        v.wait_ge(sV.h, n1)
                        n2 = tick(sV, v.tensor_tensor(qkc[:, 2:3], qkc[:, 0:1], qkc[:, 1:2],
                                                      op=ALU.mult))
                        v.wait_ge(sV.h, n2)
                        tick(sV, v.tensor_scalar_mul(qkc[:, 2:3], qkc[:, 2:3], 1.0 / 128.0))
                    blk.vector(_)
                    km_v = sV.n

                    def _(s, h=h, km_v=km_v):
                        s.wait_ge(sV.h, km_v)
                        tick(sS, s.sqrt(qkc[:, 3:4], qkc[:, 2:3]))
                    blk.scalar(_)
                    cqs_s = sS.n

                    def _(v, h=h, cqs_s=cqs_s):
                        v.wait_ge(sS.h, cqs_s)
                        tick(sV, v.tensor_scalar_mul(negc[:, h:h + 1], qkc[:, 3:4], -1.0))
                    blk.vector(_)
                    negc_s = sV.n

                    exp_s = {0: 0, 1: 0}
                    av_t = {0: att_t_last, 1: att_t_last}
                    for cc4 in range(4):
                        nk = 4 * (cc4 + 1)
                        npair = nk // 2
                        for p in range(npair):
                            slot = p % 2

                            def _(t, h=h, cc4=cc4, p=p, slot=slot, km_v=km_v):
                                if exp_s[slot]:
                                    t.wait_ge(sS.h, exp_s[slot])
                                if cc4 == 0 and p < 2:
                                    t.wait_ge(sV.h, km_v)
                                for q in range(2):
                                    kt = 2 * p + q
                                    bank = 2 * slot + q
                                    mm = t.matmul(PA[:, bank * 512:(bank + 1) * 512],
                                                  kTl[:, kt * 128:(kt + 1) * 128],
                                                  qT[:, cc4 * 512:(cc4 + 1) * 512],
                                                  start=True, stop=True)
                                tick(sT, mm)
                            blk.tensor(_)
                            sc_t = sT.n

                            def _(s, h=h, slot=slot, sc_t=sc_t, negc_s=negc_s):
                                s.wait_ge(sT.h, sc_t)
                                s.wait_ge(sV.h, negc_s)
                                if av_t[slot]:
                                    s.wait_ge(sT.h, av_t[slot])
                                tick(sS, s.activation(expT[:, slot * 1024:(slot + 1) * 1024],
                                                      PA[:, slot * 1024:(slot + 1) * 1024],
                                                      ACT.Exp, bias=negc[:, h:h + 1],
                                                      scale=SCL))
                            blk.scalar(_)
                            exp_s[slot] = sS.n
                            cur_exp = sS.n

                            mv_cur = 0
                            if p >= 2 * cc4:
                                moff = (p - 2 * cc4) * 1024
                                def _(v, cur_exp=cur_exp, h=h, cc4=cc4, slot=slot,
                                      moff=moff):
                                    if h == 0 and cc4 == 0 and moff == 0:
                                        v.wait_ge(dMSK.h, msk_done)
                                    v.wait_ge(sS.h, cur_exp)
                                    sl_ap = expT[:, slot * 1024:(slot + 1) * 1024]
                                    tick(sV, v.tensor_tensor(sl_ap, sl_ap,
                                                             msk[:, moff:moff + 1024],
                                                             op=ALU.mult))
                                blk.vector(_)
                                mv_cur = sV.n

                            def _(t, h=h, cc4=cc4, p=p, slot=slot, nk=nk,
                                  cur_exp=cur_exp, mv_cur=mv_cur,
                                  y_ev_v=y_ev_v, sum_rd_v=sum_rd_v):
                                t.wait_ge(sS.h, cur_exp)
                                if mv_cur:
                                    t.wait_ge(sV.h, mv_cur)
                                if p == 0:
                                    if y_ev_v:
                                        t.wait_ge(sV.h, y_ev_v)
                                    if sum_rd_v:
                                        t.wait_ge(sV.h, sum_rd_v)
                                for q in range(2):
                                    kt = 2 * p + q
                                    rhs = expT[:, slot * 1024 + q * 512: slot * 1024 + (q + 1) * 512]
                                    t.matmul(PB[:, 512:1024],
                                             v_all[:, kt * 1024 + h * 128: kt * 1024 + (h + 1) * 128],
                                             rhs, start=(kt == 0), stop=(kt == nk - 1))
                                    mm = t.matmul(PB[:, 1024:1536], ones_f[:, :], rhs,
                                                  start=(kt == 0), stop=(kt == nk - 1))
                                tick(sT, mm)
                            blk.tensor(_)
                            av_t[slot] = sT.n
                        av_fin = sT.n

                        def _(v, av_fin=av_fin):
                            v.wait_ge(sT.h, av_fin)
                            n1 = tick(sV, v.reciprocal(rcp[:, :], PB[:, 1024:1536]))
                            v.wait_ge(sV.h, n1)
                            tick(sV, v.tensor_copy(rcpb[:, :], rcp[:, :]))
                        blk.vector(_)
                        sum_rd_v = sV.n

                        def _(v, h=h, cc4=cc4, sum_rd_v=sum_rd_v):
                            v.wait_ge(sV.h, sum_rd_v)
                            tick(sV, v.tensor_tensor(
                                yT[:, h * 2048 + cc4 * 512: h * 2048 + (cc4 + 1) * 512],
                                PB[:, 512:1024], rcpb[:, :], op=ALU.mult))
                        blk.vector(_)
                        y_ev_v = sV.n
                    att_t_last = sT.n
                    exp_last_s = max(exp_s[0], exp_s[1])
                    paS = 0
                    paV = 0
                y_done_v = sV.n

                def _(g):
                    g.wait_ge(sV.h, y_done_v)
                    pid = nc.partition_id(engines=[mybir.EngineType.Pool])
                    j = pid % 2
                    comp = 1 - j
                    tick(dYD, g.dma_start(
                        out=ydr.rearrange("(hh p) t -> p hh t", p=128),
                        in_=yT[:, :].rearrange("p (hh t) -> p hh t", hh=8)), 16)
                    g.wait_ge(dYD.h, dYD.n)
                    tick(dCCI, g.dma_start(
                        out=cc_in.rearrange("(hh p) t -> p hh t", p=128),
                        in_=ydr.rearrange("(hh p) t -> p hh t", p=128)[:, :, bass.ts(comp, 1024)]), 16)
                    g.wait_ge(dCCI.h, dCCI.n)
                    g.wait_ge(wS.h, wS.n)
                    tick(cS, g.collective_compute(
                        "AllGather", ALU.bypass,
                        replica_groups=[[0, 1, 2, 3, 4, 5, 6, 7]],
                        ins=[cc_in], outs=[cc_out]))
                blk.gpsimd(_)

        x1 = ctx0.enter_context(nc.sbuf_tensor("x1", [128, 8 * 2048], F32))
        with ExitStack() as big2:
            wpj = big2.enter_context(nc.sbuf_tensor("wpjs", [128, 16 * 2048], BF16))
            y_all = v_all

            def _(g):
                pid = nc.partition_id(engines=[mybir.EngineType.Pool])
                j = pid % 2
                dst0 = y_all[:, 0:8192].rearrange("p (hh t) -> p hh t", hh=8)
                src0 = ydr.rearrange("(hh p) t -> p hh t", p=128)[:, :, bass.ts(j, 1024)]
                tick(dCCL, g.dma_start(out=dst0, in_=src0), 16)
                g.wait_ge(cS.h, cS.n)
                p1 = pid + 1 - 2 * j
                src1 = cc_out[bass.ts(p1, 1024), :].rearrange("(hh p) t -> p hh t", p=128)
                dst1 = y_all[:, 8192:16384].rearrange("p (hh t) -> p hh t", hh=8)
                tick(dCCO, g.dma_start(out=dst1, in_=src1), 16)
            blk.gpsimd(_)
            ccl_done = dCCL.n
            cco_done = dCCO.n

            def _(sy):
                sy.wait_ge(sV.h, y_done_v)
                sy.wait_ge(dYD.h, 16)
                tick(dXO, sy.dma_start(
                    out=x1[:, :].rearrange("p (tt c) -> p tt c", tt=8),
                    in_=xo_h.rearrange("(tt p) c -> p tt c", p=128)), 16)
            blk.sync(_)
            xo_done = dXO.n

            def _(g):
                g.wait_ge(dYD.h, 16)
                for yf in range(16):
                    tick(dWPJ, g.dma_start(out=wpj[:, yf * 2048:(yf + 1) * 2048],
                                           in_=wpj_h[:, yf, :]), 16)
            blk.gpsimd(_)
            wpj_done = dWPJ.n

            s_h = {}
            rc_vs2 = {}

            def emitA2(i):
                sl = i

                def _(v, i=i, sl=sl):
                    v.wait_ge(sV.h, pj_ev[i])
                    for cc in range(3):
                        v.bn_stats(bn6s[sl][:, cc * 6:(cc + 1) * 6],
                                   x1[:, i * 2048 + cc * 512: i * 2048 + (cc + 1) * 512])
                    n1 = tick(sV, v.bn_stats(bn6s[sl][:, 18:24],
                                             x1[:, i * 2048 + 1536: i * 2048 + 2048]))
                    v.wait_ge(sV.h, n1)
                    n2 = tick(sV, v.bn_aggr(mvs[sl][:, :], bn6s[sl][:, :]))
                    v.wait_ge(sV.h, n2)
                    n3 = tick(sV, v.tensor_scalar(tcs[sl][:, 0:1], mvs[sl][:, 1:2],
                                                  1.000001, 1e-11,
                                                  op0=ALU.mult, op1=ALU.add))
                    v.wait_ge(sV.h, n3)
                    tick(sV, v.reciprocal(tcs[sl][:, 1:2], tcs[sl][:, 0:1]))
                blk.vector(_)
                rc_vs2[i] = sV.n

            pj_mm = {}
            pj_ev = {}
            for tt in range(8):
                PX = PA if tt % 2 == 0 else PB

                def _(t, tt=tt, PX=PX):
                    if tt == 0:
                        t.wait_ge(dCCL.h, ccl_done)
                        t.wait_ge(dWPJ.h, wpj_done)
                        t.wait_ge(dMISCG.h, misc_g)
                    if tt >= 2:
                        t.wait_ge(sV.h, pj_ev[tt - 2])
                    for dq in range(4):
                        t.matmul(PX[:, dq * 512:(dq + 1) * 512], ones_r[:, :],
                                 bpj[:, dq * 512:(dq + 1) * 512], start=True, stop=False)
                    for yf in range(16):
                        if yf == 8 and tt == 0:
                            t.wait_ge(dCCO.h, cco_done)
                        for dq in range(4):
                            mm = t.matmul(
                                PX[:, dq * 512:(dq + 1) * 512],
                                y_all[:, yf * 1024 + tt * 128: yf * 1024 + (tt + 1) * 128],
                                wpj[:, yf * 2048 + dq * 512: yf * 2048 + (dq + 1) * 512],
                                start=False, stop=(yf == 15))
                    tick(sT, mm)
                blk.tensor(_)
                pj_mm[tt] = sT.n

                def _(v, tt=tt, PX=PX):
                    v.wait_ge(sT.h, pj_mm[tt])
                    if tt == 0:
                        v.wait_ge(dXO.h, xo_done)
                    for dq in range(3):
                        sl = x1[:, tt * 2048 + dq * 512: tt * 2048 + (dq + 1) * 512]
                        v.tensor_tensor(sl, PX[:, dq * 512:(dq + 1) * 512], sl, op=ALU.add)
                    sl = x1[:, tt * 2048 + 1536: tt * 2048 + 2048]
                    tick(sV, v.tensor_tensor(sl, PX[:, 1536:2048], sl, op=ALU.add))
                blk.vector(_)
                pj_ev[tt] = sV.n
                emitA2(tt)
            proj_done_v = sV.n
        vsc.close()

        with ExitStack() as big3:
            hT = big3.enter_context(
                nc.sbuf_tensor("hT", [128, 16 * 1024], BF16, is_transpose_dest=True))
            su = [big3.enter_context(nc.sbuf_tensor(f"su{i}", [128, 8 * 1024], BF16))
                  for i in range(2)]
            guw = [big3.enter_context(nc.sbuf_tensor(f"guw{i}", [128, 4096], BF16))
                   for i in range(2)]
            dnw = [big3.enter_context(nc.sbuf_tensor(f"dnw{i}", [128, 4096], BF16))
                   for i in range(4)]
            h_t = [big3.enter_context(nc.sbuf_tensor(f"h_t{i}", [128, 2048], BF16))
                   for i in range(2)]
            silu_sb = big3.enter_context(nc.sbuf_tensor("silu_sb", [128, 1024], BF16))
            lg_sb = big3.enter_context(nc.sbuf_tensor("lg_sb", [8, 1024], F32))

            def emitB2(i):
                sl = i
                hsl = i % 2

                def _(s, i=i, sl=sl):
                    s.wait_ge(sV.h, rc_vs2[i])
                    if i == 0:
                        s.wait_ge(sT.h, pj_mm[7])
                    tick(sS, s.sqrt(acs[sl][:, :], tcs[sl][:, 1:2]))
                blk.scalar(_)
                sq_s = sS.n

                def _(v, i=i, sl=sl, sq_s=sq_s):
                    v.wait_ge(sS.h, sq_s)
                    tick(sV, v.scalar_tensor_tensor(bcs[sl][:, :], mvs[sl][:, 0:1], -1.0,
                                                    acs[sl][:, :], op0=ALU.mult, op1=ALU.mult))
                blk.vector(_)
                ab_v = sV.n

                def _(s, i=i, sl=sl, hsl=hsl, ab_v=ab_v):
                    s.wait_ge(sV.h, ab_v)
                    if i >= 2:
                        s.wait_ge(dHT[hsl].h, 16 * (i // 2))
                    tick(sS, s.activation(h_t[hsl][:, :], x1[:, i * 2048:(i + 1) * 2048],
                                          ACT.Identity, bias=bcs[sl][:, :], scale=acs[sl][:, :]))
                blk.scalar(_)
                s_h[i] = sS.n

                def _(sy, i=i, hsl=hsl):
                    sy.wait_ge(sS.h, s_h[i])
                    if i == 0:
                        sy.wait_ge(sT.h, pj_mm[7])
                    o_v = hT[:, :].rearrange("p (dt tt) -> p dt tt", dt=16)[:, :, i * 128:(i + 1) * 128]
                    i_v = h_t[hsl][:, :].rearrange("t (dt p) -> t dt p", dt=16)
                    tick(dHT[hsl], sy.dma_start_transpose(o_v, i_v), 16)
                blk.sync(_)

            for i in range(8):
                emitB2(i)
            ht_done = tuple(dHT[i].n for i in range(4))

            def _(v):
                for i in range(4):
                    v.wait_ge(dHT[i].h, ht_done[i])
                v.wait_ge(dMISC.h, misc_done)
                for dt in range(15):
                    v.tensor_scalar(hT[:, dt * 1024:(dt + 1) * 1024],
                                    hT[:, dt * 1024:(dt + 1) * 1024],
                                    rmsw[:, dt:dt + 1], None, op0=ALU.mult)
                tick(sV, v.tensor_scalar(hT[:, 15 * 1024:16 * 1024],
                                         hT[:, 15 * 1024:16 * 1024],
                                         rmsw[:, 15:16], None, op0=ALU.mult))
            blk.vector(_)
            hT_ready_v = sV.n

            def _(t):
                t.wait_ge(sV.h, hT_ready_v)
                t.wait_ge(dMISCG.h, misc_g)
                for cc in range(2):
                    for kt in range(16):
                        mm = t.matmul(PA[0:8, cc * 512:(cc + 1) * 512],
                                      rts[:, kt * 8:(kt + 1) * 8],
                                      hT[:, kt * 1024 + cc * 512: kt * 1024 + (cc + 1) * 512],
                                      start=(kt == 0), stop=(kt == 15))
                tick(sT, mm)
            blk.tensor(_)
            lg_t = sT.n

            def _(v):
                v.wait_ge(sT.h, lg_t)
                tick(sV, v.tensor_copy(lg_sb[:, :], PA[0:8, 0:1024]))
            blk.vector(_)
            lg_v = sV.n

            def _(t):
                t.wait_ge(sV.h, lg_v)
                t.wait_ge(dMISC.h, misc_done)
                for i in range(7):
                    t.transpose(PB[:, i * 8:(i + 1) * 8], lg_sb[:, i * 128:(i + 1) * 128],
                                id32[0:8, 0:8])
                tick(sT, t.transpose(PB[:, 56:64], lg_sb[:, 896:1024], id32[0:8, 0:8]))
            blk.tensor(_)
            lgT_t = sT.n

            def _(v):
                v.wait_ge(sT.h, lgT_t)
                n1 = tick(sV, v.tensor_copy(lgT[:, :], PB[:, 0:64]))
                v.wait_ge(sV.h, n1)
                for i in range(7):
                    v.reduce_max(mcol[:, i:i + 1], lgT[:, i * 8:(i + 1) * 8], axis=AX)
                n2 = tick(sV, v.reduce_max(mcol[:, 7:8], lgT[:, 56:64], axis=AX))
                v.wait_ge(sV.h, n2)
                tick(sV, v.tensor_scalar_mul(mneg[:, :], mcol[:, :], -1.0))
            blk.vector(_)
            m1_v = sV.n

            def _(s):
                s.wait_ge(sV.h, m1_v)
                for i in range(7):
                    s.activation(e_sb[:, i * 8:(i + 1) * 8], lgT[:, i * 8:(i + 1) * 8],
                                 ACT.Exp, bias=mneg[:, i:i + 1], scale=1.0)
                tick(sS, s.activation(e_sb[:, 56:64], lgT[:, 56:64],
                                      ACT.Exp, bias=mneg[:, 7:8], scale=1.0))
            blk.scalar(_)
            e_s = sS.n

            def _(v):
                v.wait_ge(sS.h, e_s)
                n1 = tick(sV, v.tensor_scalar(msel[:, :], e_sb[:, :], 1.0, -1.0,
                                              op0=ALU.is_ge, op1=ALU.mult))
                v.wait_ge(sV.h, n1)
                n2 = tick(sV, v.tensor_scalar_add(msel[:, :], msel[:, :], 1.0))
                v.wait_ge(sV.h, n2)
                n3 = tick(sV, v.tensor_tensor(msel[:, :], e_sb[:, :], msel[:, :], op=ALU.mult))
                v.wait_ge(sV.h, n3)
                for i in range(7):
                    v.reduce_max(mcol[:, i:i + 1], msel[:, i * 8:(i + 1) * 8], axis=AX)
                n4 = tick(sV, v.reduce_max(mcol[:, 7:8], msel[:, 56:64], axis=AX))
                v.wait_ge(sV.h, n4)
                for i in range(7):
                    v.tensor_scalar(wd_tok[:, i * 8:(i + 1) * 8], e_sb[:, i * 8:(i + 1) * 8],
                                    mcol[:, i:i + 1], None, op0=ALU.is_ge)
                n5 = tick(sV, v.tensor_scalar(wd_tok[:, 56:64], e_sb[:, 56:64],
                                              mcol[:, 7:8], None, op0=ALU.is_ge))
                v.wait_ge(sV.h, n5)
                n6 = tick(sV, v.tensor_tensor(wd_tok[:, :], e_sb[:, :], wd_tok[:, :], op=ALU.mult))
                v.wait_ge(sV.h, n6)
                n7 = tick(sV, v.tensor_scalar_add(mcol[:, :], mcol[:, :], 1.0))
                v.wait_ge(sV.h, n7)
                tick(sV, v.reciprocal(mcol[:, :], mcol[:, :]))
            blk.vector(_)
            w2_v = sV.n

            def _(s):
                s.wait_ge(sV.h, w2_v)
                for i in range(7):
                    s.activation(wd_tok[:, i * 8:(i + 1) * 8], wd_tok[:, i * 8:(i + 1) * 8],
                                 ACT.Identity, bias=0.0, scale=mcol[:, i:i + 1])
                tick(sS, s.activation(wd_tok[:, 56:64], wd_tok[:, 56:64],
                                      ACT.Identity, bias=0.0, scale=mcol[:, 7:8]))
            blk.scalar(_)
            wd_s = sS.n

            gu_mm = {}
            dn_mm = {}
            gu_tgt = {}
            dn_tgt = {}
            su_v = {}
            silu_s = {}
            ev_v = {}
            wdT_done = 0
            wdb_v = 0
            wdb_cp_v = 0

            def load_gu(idx, ft):
                e = EORD[idx]
                sl = ft % 2

                def _(g, idx=idx, e=e, ft=ft, sl=sl):
                    if (idx, ft) == (0, 0):
                        g.wait_ge(sT.h, pj_mm[7])
                    prev = (EORD[idx], ft - 2) if ft >= 2 else \
                        ((EORD[idx - 1], ft + 6) if idx >= 1 else None)
                    if prev in gu_mm:
                        g.wait_ge(sT.h, gu_mm[prev])
                    tick(dGU[sl], g.dma_start(out=guw[sl][:, 0:2048], in_=gu_h[e, ft, 0]), 16)
                    tick(dGU[sl], g.dma_start(out=guw[sl][:, 2048:4096], in_=gu_h[e, ft, 1]), 16)
                blk.gpsimd(_)
                gu_tgt[(e, ft)] = dGU[sl].n

            def load_dn(idx, dq):
                e = EORD[idx]

                def _(g, idx=idx, e=e, dq=dq):
                    if (idx, dq) == (0, 0):
                        g.wait_ge(sT.h, pj_mm[7])
                    if idx >= 1:
                        prev = (EORD[idx - 1], dq // 2)
                        if prev in dn_mm:
                            g.wait_ge(sT.h, dn_mm[prev])
                    tick(dDN[dq], g.dma_start(out=dnw[dq][:, :], in_=dn_h[e, dq]), 16)
                blk.gpsimd(_)
                dn_tgt[(e, dq)] = dDN[dq].n

            load_gu(0, 0)
            load_gu(0, 1)
            for dq in range(4):
                load_dn(0, dq)

            for idx in range(NE):
                e = EORD[idx]
                esl = idx % 2

                if e != 8:
                    def _(t, e=e, idx=idx, wdb_v=wdb_v):
                        t.wait_ge(sV.h, wdb_v)
                        t.wait_ge(sV.h, ev_v[(EORD[idx - 1], 1, 3)])
                        t.matmul(PB[:, 0:512], selb[:, e * 128:(e + 1) * 128],
                                 wd8[:, 0:512], start=True, stop=True)
                        tick(sT, t.matmul(PB[:, 512:1024], selb[:, e * 128:(e + 1) * 128],
                                          wd8[:, 512:1024], start=True, stop=True))
                    blk.tensor(_)
                    wdb_t = sT.n

                    def _(v, wdb_t=wdb_t):
                        v.wait_ge(sT.h, wdb_t)
                        tick(sV, v.tensor_copy(wdB[:, :], PB[:, 0:1024]))
                    blk.vector(_)
                    wdb_cp_v = sV.n

                for ft in range(FT):
                    sl = ft % 2

                    def _(t, idx=idx, e=e, ft=ft, sl=sl):
                        t.wait_ge(dGU[sl].h, gu_tgt[(e, ft)])
                        if (idx, ft) == (0, 0):
                            t.wait_ge(sV.h, hT_ready_v)
                            t.wait_ge(sV.h, lg_v)
                        prevk = (e, ft - 1) if ft >= 1 else \
                            ((EORD[idx - 1], 7) if idx >= 1 else None)
                        if prevk in su_v:
                            t.wait_ge(sV.h, su_v[prevk])
                        if ft == 0 and idx >= 1:
                            t.wait_ge(sV.h, ev_v[(EORD[idx - 1], 1, 3)])
                        for gui in range(2):
                            for kt in range(16):
                                for cc in range(2):
                                    bank = 2 * gui + cc
                                    mm = t.matmul(
                                        PA[:, bank * 512:(bank + 1) * 512],
                                        guw[sl][:, gui * 2048 + kt * 128: gui * 2048 + (kt + 1) * 128],
                                        hT[:, kt * 1024 + cc * 512: kt * 1024 + (cc + 1) * 512],
                                        start=(kt == 0), stop=(kt == 15))
                        tick(sT, mm)
                    blk.tensor(_)
                    gu_mm[(e, ft)] = sT.n

                    def _(s, e=e, ft=ft, idx=idx):
                        s.wait_ge(sT.h, gu_mm[(e, ft)])
                        prevk = (e, ft - 1) if ft >= 1 else \
                            ((EORD[idx - 1], 7) if idx >= 1 else None)
                        if prevk in su_v:
                            s.wait_ge(sV.h, su_v[prevk])
                        tick(sS, s.activation(silu_sb[:, :], PA[:, 0:1024], ACT.Silu,
                                              bias=0.0, scale=1.0))
                    blk.scalar(_)
                    silu_s[(e, ft)] = sS.n

                    def _(v, e=e, ft=ft, idx=idx, esl=esl, wdb_cp_v=wdb_cp_v):
                        v.wait_ge(sS.h, silu_s[(e, ft)])
                        if e != 8 and ft == 0:
                            v.wait_ge(sV.h, wdb_cp_v)
                        if ft == 0 and idx >= 2:
                            v.wait_ge(sT.h, dn_mm[(EORD[idx - 2], 1)])
                        for cc in range(2):
                            ap = su[esl][:, ft * 1024 + cc * 512: ft * 1024 + (cc + 1) * 512]
                            ii = v.tensor_tensor(ap, silu_sb[:, cc * 512:(cc + 1) * 512],
                                                 PA[:, (2 + cc) * 512:(3 + cc) * 512],
                                                 op=ALU.mult)
                        n1 = tick(sV, ii)
                        if e != 8:
                            v.wait_ge(sV.h, n1)
                            for cc in range(2):
                                ap = su[esl][:, ft * 1024 + cc * 512: ft * 1024 + (cc + 1) * 512]
                                ii = v.tensor_tensor(ap, ap, wdB[:, cc * 512:(cc + 1) * 512],
                                                     op=ALU.mult)
                            tick(sV, ii)
                    blk.vector(_)
                    su_v[(e, ft)] = sV.n

                    nxt = (idx, ft + 2) if ft + 2 < FT else (idx + 1, ft + 2 - FT)
                    if nxt[0] < NE:
                        load_gu(*nxt)

                if idx == 0:
                    def _(t):
                        t.wait_ge(sS.h, wd_s)
                        for i in range(7):
                            t.transpose(PB[0:8, 512 + i * 128: 512 + (i + 1) * 128],
                                        wd_tok[:, i * 8:(i + 1) * 8], id32[:, :])
                        tick(sT, t.transpose(PB[0:8, 1408:1536], wd_tok[:, 56:64], id32[:, :]))
                    blk.tensor(_)
                    wdT_done = sT.n

                    def _(v):
                        v.wait_ge(sT.h, wdT_done)
                        tick(sV, v.tensor_copy(wd8[:, :], PB[0:8, 512:1536]))
                    blk.vector(_)
                    wdb_v = sV.n

                for dqp in range(2):
                    for tq in range(4):
                        PX = PB if tq % 2 == 0 else PA

                        def _(t, idx=idx, e=e, dqp=dqp, tq=tq, PX=PX, wdb_v=wdb_v):
                            if tq == 0:
                                t.wait_ge(dDN[2 * dqp].h, dn_tgt[(e, 2 * dqp)])
                                t.wait_ge(dDN[2 * dqp + 1].h, dn_tgt[(e, 2 * dqp + 1)])
                            if idx == 0 and dqp == 0 and tq < 2:
                                t.wait_ge(sV.h, wdb_v)
                            if tq >= 2:
                                prev = (e, dqp, tq - 2)
                            elif dqp == 1:
                                prev = (e, 0, tq + 2)
                            elif idx >= 1:
                                prev = (EORD[idx - 1], 1, tq + 2)
                            else:
                                prev = None
                            if prev in ev_v:
                                t.wait_ge(sV.h, ev_v[prev])
                            for ft in range(FT):
                                if tq == 0 and dqp == 0:
                                    t.wait_ge(sV.h, su_v[(e, ft)])
                                for q in range(2):
                                    tt = tq * 2 + q
                                    for dqi in range(2):
                                        bank = q * 2 + dqi
                                        mm = t.matmul(
                                            PX[:, bank * 512:(bank + 1) * 512],
                                            su[idx % 2][:, ft * 1024 + tt * 128: ft * 1024 + (tt + 1) * 128],
                                            dnw[2 * dqp + dqi][:, ft * 512:(ft + 1) * 512],
                                            start=(ft == 0), stop=(ft == FT - 1))
                            tick(sT, mm)
                        blk.tensor(_)
                        dn_mm[(e, dqp, tq)] = sT.n
                        if tq == 3:
                            dn_mm[(e, dqp)] = sT.n

                        def _(v, idx=idx, e=e, dqp=dqp, tq=tq, PX=PX):
                            v.wait_ge(sT.h, dn_mm[(e, dqp, tq)])
                            if idx == 0 and dqp == 0 and tq == 0:
                                v.wait_ge(sV.h, proj_done_v)
                            for q in range(2):
                                tt = tq * 2 + q
                                for dqi in range(2):
                                    dq = 2 * dqp + dqi
                                    bank = q * 2 + dqi
                                    ap = x1[:, tt * 2048 + dq * 512: tt * 2048 + (dq + 1) * 512]
                                    ii = v.tensor_tensor(ap, PX[:, bank * 512:(bank + 1) * 512],
                                                         ap, op=ALU.add)
                            tick(sV, ii)
                        blk.vector(_)
                        ev_v[(e, dqp, tq)] = sV.n
                if idx + 1 < NE:
                    for dq in range(4):
                        load_dn(idx + 1, dq)
            ffn_done_v = sV.n

            def _(sy):
                sy.wait_ge(sV.h, ffn_done_v)
                tick(dOUT, sy.dma_start(
                    out=out_h.rearrange("(tt p) c -> p tt c", p=128),
                    in_=x1[:, :].rearrange("p (tt c) -> p tt c", tt=8)), 16)
                sy.wait_ge(dOUT.h, dOUT.n)
            blk.sync(_)

            def _(g):
                g.wait_ge(dOUT.h, dOUT.n)
            blk.gpsimd(_)

    return nc


def _pack_core(inp, c):
    b, j = c // 2, c % 2
    wa = np.asarray(inp["L0_attn_w_attn"])
    qb, kb, vb = 1024 * j, 2048 + 1024 * j, 4096 + 1024 * j
    d = {}
    x = np.asarray(inp["L0_attn_x"])
    d["x"] = np.ascontiguousarray(x[b])
    d["xo"] = np.ascontiguousarray(x[b][j * 1024:(j + 1) * 1024])
    qk = np.concatenate([wa[:, qb:qb + 1024], wa[:, kb:kb + 1024]], axis=1)
    d["wqk"] = np.ascontiguousarray(
        qk.reshape(16, 128, 16, 128).transpose(2, 1, 0, 3).reshape(16, 128, 2048))
    wv = wa[:, vb:vb + 1024]
    d["wv"] = np.ascontiguousarray(
        wv.reshape(16, 128, 2, 512).transpose(2, 1, 0, 3).reshape(2, 128, 8192))
    wp = np.asarray(inp["L0_attn_w_proj"])
    perm = list(range(8 * j, 8 * j + 8)) + list(range(8 * (1 - j), 8 * (1 - j) + 8))
    d["wpj"] = np.ascontiguousarray(wp.reshape(16, 128, 2048)[perm].transpose(1, 0, 2))
    ba = np.asarray(inp["L0_attn_b_attn"])
    d["bqk"] = np.ascontiguousarray(
        np.concatenate([ba[qb:qb + 1024], ba[kb:kb + 1024]]).reshape(16, 128).T)
    d["bv"] = np.ascontiguousarray(ba[vb:vb + 1024].reshape(1, 1024))
    d["bpj"] = np.ascontiguousarray(np.asarray(inp["L0_attn_b_proj"]).reshape(1, 2048))
    rw = np.asarray(inp["L0_ffn_router_w"])
    d["rt"] = np.ascontiguousarray(rw.reshape(16, 128, 8).transpose(1, 0, 2).reshape(128, 128))
    d["rmsw"] = np.ascontiguousarray(np.asarray(inp["L0_ffn_rms_weight"]).reshape(16, 128).T)
    wg = np.asarray(inp["L0_ffn_w_gate"])
    wu = np.asarray(inp["L0_ffn_w_up"])
    wd = np.asarray(inp["L0_ffn_w_down"])
    swg = np.asarray(inp["L0_ffn_sw_gate"])
    swu = np.asarray(inp["L0_ffn_sw_up"])
    swd = np.asarray(inp["L0_ffn_sw_down"])
    gu = np.empty((NE, FT, 2, 128, 2048), np.float32)
    for e in range(NE):
        g = wg[e] if e < 8 else swg
        u = wu[e] if e < 8 else swu
        gu[e, :, 0] = g.reshape(16, 128, 8, 128).transpose(2, 1, 0, 3).reshape(8, 128, 2048)
        gu[e, :, 1] = u.reshape(16, 128, 8, 128).transpose(2, 1, 0, 3).reshape(8, 128, 2048)
    d["gu"] = np.ascontiguousarray(gu)
    dn = np.empty((NE, 4, 128, 4096), np.float32)
    for e in range(NE):
        D = wd[e] if e < 8 else swd
        dn[e] = D.reshape(8, 128, 4, 512).transpose(2, 1, 0, 3).reshape(4, 128, 4096)
    d["dn"] = np.ascontiguousarray(dn)
    i_idx = np.arange(128)[:, None]
    u_idx = np.arange(512)[None, :]
    mskb = np.empty((128, 2048), BF)
    for s in range(4):
        mskb[:, s * 512:(s + 1) * 512] = (u_idx >= i_idx + 128 * s).astype(BF)
    d["msk"] = mskb
    d["id32"] = np.eye(128, dtype=np.float32)
    selb = np.zeros((8, 1024), BF)
    for e in range(8):
        selb[e, e * 128:(e + 1) * 128] = 1
    d["selb"] = selb
    return d


def kernel(**inputs):
    in_maps = [_pack_core(inputs, c) for c in range(8)]
    nc = _build()
    res = run_bass_kernel_spmd(nc, in_maps, core_ids=list(range(8)))
    out = np.empty((4, 2048, 2048), np.float32)
    for c in range(8):
        b, j = c // 2, c % 2
        out[b, j * 1024:(j + 1) * 1024, :] = res.results[c]["out"]
    return out


# version 13
# speedup vs baseline: 1.4164x; 1.3859x over previous
import math
import numpy as np
import ml_dtypes

import concourse.bass as bass
import concourse.mybir as mybir
import concourse.bass_utils as _bu
from concourse.bass_utils import run_bass_kernel_spmd
from contextlib import ExitStack

if not getattr(_bu, "_ldwopt_patched", False):
    _orig_run_command = _bu.run_command

    def _run_command_ldwopt(argv, **kwargs):
        argv = ["--enable-ldw-opt=true" if a == "--enable-ldw-opt=false" else a
                for a in argv]
        return _orig_run_command(argv, **kwargs)

    _bu.run_command = _run_command_ldwopt
    _bu._ldwopt_patched = True

F32 = mybir.dt.float32
BF16 = mybir.dt.bfloat16
AX = mybir.AxisListType.X
ALU = mybir.AluOpType
ACT = mybir.ActivationFunctionType
BF = ml_dtypes.bfloat16

NE = 9
FT = 8
SCL = 1.0 / math.sqrt(128.0)
EORD = [8, 0, 1, 2, 3, 4, 5, 6, 7]


class Ctr:
    def __init__(self, h):
        self.h = h
        self.n = 0


def tick(ctr, inst, k=1):
    inst.then_inc(ctr.h, k)
    ctr.n += k
    return ctr.n


def _build():
    nc = bass.Bass("TRN2", target_bir_lowering=False, debug=False)

    x_h = nc.dram_tensor("x", [2048, 2048], F32, kind="ExternalInput").ap()
    xo_h = nc.dram_tensor("xo", [1024, 2048], F32, kind="ExternalInput").ap()
    wqk_h = nc.dram_tensor("wqk", [16, 128, 2048], F32, kind="ExternalInput").ap()
    wv_h = nc.dram_tensor("wv", [2, 128, 8192], F32, kind="ExternalInput").ap()
    wpj_h = nc.dram_tensor("wpj", [128, 16, 2048], F32, kind="ExternalInput").ap()
    bqk_h = nc.dram_tensor("bqk", [128, 16], F32, kind="ExternalInput").ap()
    bv_h = nc.dram_tensor("bv", [1, 1024], F32, kind="ExternalInput").ap()
    bpj_h = nc.dram_tensor("bpj", [1, 2048], F32, kind="ExternalInput").ap()
    rt_h = nc.dram_tensor("rt", [128, 128], F32, kind="ExternalInput").ap()
    rmsw_h = nc.dram_tensor("rmsw", [128, 16], F32, kind="ExternalInput").ap()
    rmswr_h = nc.dram_tensor("rmswr", [1, 2048], F32, kind="ExternalInput").ap()
    gu_h = nc.dram_tensor("gu", [NE, FT, 2, 128, 2048], F32, kind="ExternalInput").ap()
    dn_h = nc.dram_tensor("dn", [NE, 4, 128, 4096], F32, kind="ExternalInput").ap()
    msk_h = nc.dram_tensor("msk", [128, 2048], BF16, kind="ExternalInput").ap()
    selb_h = nc.dram_tensor("selb", [8, 1024], BF16, kind="ExternalInput").ap()
    id32_h = nc.dram_tensor("id32", [128, 128], F32, kind="ExternalInput").ap()
    out_h = nc.dram_tensor("out", [1024, 2048], F32, kind="ExternalOutput").ap()

    wu_in = nc.dram_tensor("wu_in", [8, 128], BF16).ap()
    wu_out = nc.dram_tensor("wu_out", [64, 128], BF16, addr_space="Shared").ap()
    ydr = nc.dram_tensor("ydr", [1024, 2048], BF16).ap()
    cc_in = [nc.dram_tensor(f"cc_in{c}", [256, 1024], BF16).ap() for c in range(4)]
    cc_out = [nc.dram_tensor(f"cc_out{c}", [2048, 1024], BF16,
                             addr_space="Shared").ap() for c in range(4)]

    with ExitStack() as ctx0:
        def mksem(name):
            return Ctr(ctx0.enter_context(nc.semaphore(name)))

        sT, sV, sS = mksem("sT"), mksem("sV"), mksem("sS")
        dX = [mksem(f"dX{i}") for i in range(4)]
        dTR = [mksem(f"dTR{i}") for i in range(4)]
        dWQK = [mksem("dWQK0"), mksem("dWQK1")]
        dWV = mksem("dWV")
        dWPJ = mksem("dWPJ")
        dGU = [mksem("dGU0"), mksem("dGU1")]
        dDN = [mksem(f"dDN{i}") for i in range(4)]
        dMISC = mksem("dMISC")
        dMISCG = mksem("dMISCG")
        dMSK = mksem("dMSK")
        dXO = mksem("dXO")
        dYD = mksem("dYD")
        dCCI = mksem("dCCI")
        dCCL = mksem("dCCL")
        dCCO = mksem("dCCO")
        dHT = [mksem(f"dHT{i}") for i in range(4)]
        dOUT = mksem("dOUT")
        cS = mksem("cS")
        wS = mksem("wS")
        dWU = mksem("dWU")

        PA = ctx0.enter_context(nc.psum_tensor("PA", [128, 2048], F32))
        PB = ctx0.enter_context(nc.psum_tensor("PB", [128, 2048], F32))

        def sbt(name, shape, dtype, **kw):
            return ctx0.enter_context(nc.sbuf_tensor(name, shape, dtype, **kw))

        ones_c = sbt("ones_c", [128, 1], BF16)
        ones_f = sbt("ones_f", [128, 128], BF16)
        ones_r = sbt("ones_r", [1, 128], BF16)
        qkc = sbt("qkc", [128, 4], F32)
        id32 = sbt("id32s", [128, 128], F32)
        bqk = sbt("bqks", [128, 16], F32)
        bv = sbt("bvs", [1, 1024], BF16)
        bpj = sbt("bpjs", [1, 2048], BF16)
        rts = sbt("rts", [128, 128], BF16)
        rmsw = sbt("rmsws", [128, 16], F32)
        negc = sbt("negc", [128, 8], F32)
        bn6s = [sbt(f"bn6_{i}", [128, 24], F32) for i in range(8)]
        mvs = [sbt(f"mv_{i}", [128, 2], F32) for i in range(8)]
        tcs = [sbt(f"tc_{i}", [128, 2], F32) for i in range(8)]
        acs = [sbt(f"ac_{i}", [128, 1], F32) for i in range(8)]
        bcs = [sbt(f"bc_{i}", [128, 1], F32) for i in range(8)]
        sm1 = sbt("sm1", [1, 4], F32)
        sm1b = sbt("sm1b", [1, 1], BF16)
        lgT = sbt("lgT", [128, 64], F32)
        e_sb = sbt("e_sb", [128, 64], F32)
        msel = sbt("msel", [128, 64], F32)
        mcol = sbt("mcol", [128, 8], F32)
        mneg = sbt("mneg", [128, 8], F32)
        wd_tok = sbt("wd_tok", [128, 64], F32)
        wd8 = sbt("wd8", [8, 1024], BF16)
        selb = sbt("selbs", [8, 1024], BF16)
        rmsw_b = sbt("rmsw_b", [128, 2048], BF16)

        blk = ctx0.enter_context(nc.Block())

        def _(v):
            v.memset(ones_c[:, :], 1.0)
            v.memset(ones_f[:, :], 1.0)
            tick(sV, v.memset(ones_r[:, :], 1.0))
        blk.vector(_)
        v_ones = sV.n

        def _(sy):
            tick(dMISC, sy.dma_start(out=id32[:, :], in_=id32_h), 16)
            tick(dMISC, sy.dma_start(out=selb[:, :], in_=selb_h), 16)
            tick(dMISC, sy.dma_start(out=bqk[:, :], in_=bqk_h), 16)
            tick(dMISC, sy.dma_start(out=rmsw[:, :], in_=rmsw_h), 16)
        blk.sync(_)
        misc_done = dMISC.n

        def _(g):
            tick(dMISCG, g.dma_start(out=bv[:, :], in_=bv_h), 16)
            tick(dMISCG, g.dma_start(out=bpj[:, :], in_=bpj_h), 16)
            tick(dMISCG, g.dma_start(out=rts[:, :], in_=rt_h), 16)
            tick(dMISCG, g.dma_start(out=rmsw_b[0:1, :], in_=rmswr_h), 16)
            tick(dWU, g.dma_start(
                out=wu_in,
                in_=bv_h[0:1, 0:1024].rearrange("o (a b) -> (o a) b", a=8)), 16)
            g.wait_ge(dWU.h, 16)
            tick(wS, g.collective_compute(
                "AllGather", ALU.bypass,
                replica_groups=[[0, 1, 2, 3, 4, 5, 6, 7]],
                ins=[wu_in], outs=[wu_out]))
        blk.gpsimd(_)
        misc_g = dMISCG.n

        def _(t):
            t.wait_ge(dMISCG.h, misc_g)
            t.wait_ge(sV.h, v_ones)
            for cc in range(4):
                mm = t.matmul(PB[:, cc * 512:(cc + 1) * 512], ones_r[:, :],
                              rmsw_b[0:1, cc * 512:(cc + 1) * 512],
                              start=True, stop=True)
            tick(sT, mm)
        blk.tensor(_)
        rb_t = sT.n

        def _(v):
            v.wait_ge(sT.h, rb_t)
            tick(sV, v.tensor_copy(rmsw_b[:, :], PB[:, 0:2048]))
        blk.vector(_)

        vsc = ExitStack()
        v_all = vsc.enter_context(nc.sbuf_tensor("v_all", [128, 16 * 1024], BF16,
                                                 side="right"))

        with ExitStack() as big1:
            ln1T = big1.enter_context(
                nc.sbuf_tensor("ln1T", [128, 16 * 2048], BF16, is_transpose_dest=True))
            with ExitStack() as sub1a:
                x_t = [sub1a.enter_context(nc.sbuf_tensor(f"x_t{i}", [128, 2048], F32))
                       for i in range(4)]
                l_t = [sub1a.enter_context(nc.sbuf_tensor(f"l_t{i}", [128, 2048], BF16))
                       for i in range(4)]
                wvw = [sub1a.enter_context(nc.sbuf_tensor(f"wvw{i}", [128, 8192], BF16))
                       for i in range(2)]

                def _(g):
                    tick(dWV, g.dma_start(out=wvw[0][:, :], in_=wv_h[0]), 16)
                    tick(dWV, g.dma_start(out=wvw[1][:, :], in_=wv_h[1]), 16)
                blk.gpsimd(_)
                wv_done = dWV.n

                s_app = {}
                tr_done = {}
                v_ev = {}
                rc_vs = {}

                def emitA(i):
                    sl = i % 4

                    def _(sy, i=i, sl=sl):
                        if i >= 4:
                            sy.wait_ge(sS.h, s_app[i - 4])
                        tick(dX[sl], sy.dma_start(
                            out=x_t[sl][:, :], in_=x_h[i * 128:(i + 1) * 128, :]), 16)
                    blk.sync(_)

                    def _(v, i=i, sl=sl):
                        v.wait_ge(dX[sl].h, dX[sl].n)
                        if i >= 4:
                            v.wait_ge(sS.h, s_app[i - 4])
                        for cc in range(3):
                            v.bn_stats(bn6s[sl][:, cc * 6:(cc + 1) * 6],
                                       x_t[sl][:, cc * 512:(cc + 1) * 512])
                        n1 = tick(sV, v.bn_stats(bn6s[sl][:, 18:24], x_t[sl][:, 1536:2048]))
                        v.wait_ge(sV.h, n1)
                        n2 = tick(sV, v.bn_aggr(mvs[sl][:, :], bn6s[sl][:, :]))
                        v.wait_ge(sV.h, n2)
                        n3 = tick(sV, v.tensor_scalar_add(tcs[sl][:, 0:1], mvs[sl][:, 1:2], 1e-5))
                        v.wait_ge(sV.h, n3)
                        tick(sV, v.reciprocal(tcs[sl][:, 1:2], tcs[sl][:, 0:1]))
                    blk.vector(_)
                    rc_vs[i] = sV.n

                def emitB(i):
                    sl = i % 4

                    def _(s, i=i, sl=sl):
                        s.wait_ge(sV.h, rc_vs[i])
                        tick(sS, s.sqrt(acs[sl][:, :], tcs[sl][:, 1:2]))
                    blk.scalar(_)
                    sq_s = sS.n

                    def _(v, i=i, sl=sl, sq_s=sq_s):
                        v.wait_ge(sS.h, sq_s)
                        tick(sV, v.scalar_tensor_tensor(bcs[sl][:, :], mvs[sl][:, 0:1], -1.0,
                                                        acs[sl][:, :], op0=ALU.mult,
                                                        op1=ALU.mult))
                    blk.vector(_)
                    ab_v = sV.n

                    def _(s, i=i, sl=sl, ab_v=ab_v):
                        s.wait_ge(sV.h, ab_v)
                        if i >= 4:
                            s.wait_ge(dTR[sl].h, 16 * (i // 4))
                        tick(sS, s.activation(l_t[sl][:, :], x_t[sl][:, :], ACT.Identity,
                                              bias=bcs[sl][:, :], scale=acs[sl][:, :]))
                    blk.scalar(_)
                    s_app[i] = sS.n

                    def _(sy, i=i, sl=sl):
                        sy.wait_ge(sS.h, s_app[i])
                        o_v = ln1T[:, :].rearrange("p (ct tt) -> p ct tt", ct=16)[:, :, i * 128:(i + 1) * 128]
                        i_v = l_t[sl][:, :].rearrange("t (ct p) -> t ct p", ct=16)
                        tick(dTR[sl], sy.dma_start_transpose(o_v, i_v), 16)
                    blk.sync(_)
                    tr_done[i] = (sl, dTR[sl].n)

                    def _(t, i=i):
                        sl_, tgt = tr_done[i]
                        t.wait_ge(dTR[sl_].h, tgt)
                        if i == 0:
                            t.wait_ge(dMISCG.h, misc_g)
                            t.wait_ge(dWV.h, wv_done)
                            t.wait_ge(sV.h, v_ones)
                        if i >= 2:
                            t.wait_ge(sV.h, v_ev[i - 2])
                        base = (i % 2) * 2
                        for vch in range(2):
                            po = PA[:, (base + vch) * 512:(base + vch + 1) * 512]
                            t.matmul(po, ones_r[:, :], bv[:, vch * 512:(vch + 1) * 512],
                                     start=True, stop=False)
                            for kt in range(16):
                                mm = t.matmul(
                                    po,
                                    ln1T[:, kt * 2048 + i * 128: kt * 2048 + (i + 1) * 128],
                                    wvw[vch][:, kt * 512:(kt + 1) * 512],
                                    start=False, stop=(kt == 15))
                        tick(sT, mm)
                    blk.tensor(_)
                    v_mms[i] = sT.n

                def emitEv(i):
                    def _(v, i=i):
                        v.wait_ge(sT.h, v_mms[i])
                        base = (i % 2) * 2
                        tick(sV, v.tensor_copy(v_all[:, i * 1024:(i + 1) * 1024],
                                               PA[:, base * 512:(base + 2) * 512]))
                    blk.vector(_)
                    v_ev[i] = sV.n

                v_mms = {}
                for i in range(17):
                    if i < 16:
                        emitA(i)
                    if i >= 1:
                        emitB(i - 1)
                    if i >= 2:
                        emitEv(i - 2)
                emitEv(15)
                v_done_v = sV.n
                v_mm_last = sT.n

            with ExitStack() as sub3:
                yTb = sub3.enter_context(nc.sbuf_tensor("yTb", [128, 4 * 2048], BF16))
                expT = sub3.enter_context(nc.sbuf_tensor("expT", [128, 4096], BF16))
                msk = sub3.enter_context(nc.sbuf_tensor("msks", [128, 2048], BF16))
                qT = sub3.enter_context(nc.sbuf_tensor("qT", [128, 2048], BF16))
                kTl = sub3.enter_context(nc.sbuf_tensor("kTl", [128, 2048], BF16))
                rcp = sub3.enter_context(nc.sbuf_tensor("rcp", [128, 512], F32))
                rcpb = sub3.enter_context(nc.sbuf_tensor("rcpb", [128, 512], BF16))
                wqkw = [sub3.enter_context(nc.sbuf_tensor(f"wqkw{i}", [128, 2048], BF16))
                        for i in range(2)]

                def _(sy):
                    sy.wait_ge(sT.h, v_mm_last)
                    tick(dMSK, sy.dma_start(out=msk[:, :], in_=msk_h), 16)
                blk.sync(_)
                msk_done = dMSK.n

                qk_mm = {}
                wqk_tgt = {}

                def load_wqk(o):
                    sl = o % 2
                    m = (o >> 1) + 8 * (o & 1)

                    def _(g, o=o, sl=sl, m=m):
                        if o >= 2:
                            g.wait_ge(sT.h, qk_mm[o - 2])
                        else:
                            g.wait_ge(sT.h, v_mm_last)
                        tick(dWQK[sl], g.dma_start(out=wqkw[sl][:, :], in_=wqk_h[m]), 16)
                    blk.gpsimd(_)
                    wqk_tgt[o] = dWQK[sl].n

                load_wqk(0)
                load_wqk(1)

                wpjA = vsc.enter_context(
                    nc.sbuf_tensor("wpjA", [128, 8 * 2048], BF16, side="right"))

                def _(g):
                    for yf in range(8):
                        tick(dWPJ, g.dma_start(out=wpjA[:, yf * 2048:(yf + 1) * 2048],
                                               in_=wpj_h[:, yf, :]), 16)
                blk.gpsimd(_)
                wpja_done = dWPJ.n

                dynx = {}

                def _(g):
                    pid_e = nc.partition_id(engines=[mybir.EngineType.Pool])
                    j_e = pid_e % 2
                    dynx["ts_comp"] = bass.ts(1 - j_e, 1024)
                    dynx["ts_j"] = bass.ts(j_e, 1024)
                    dynx["ts_p1"] = bass.ts(pid_e + 1 - 2 * j_e, 256)
                blk.gpsimd(_)
                ts_comp = dynx["ts_comp"]
                ts_j = dynx["ts_j"]
                ts_p1 = dynx["ts_p1"]

                paS = 0
                paV = v_done_v
                att_t_last = 0
                exp_last_s = 0
                tail_prev = 0
                km_v = 0
                yd_pair_tgt = {}
                att_tail_v = 0

                for h in range(8):
                    for which in range(2):
                        o = 2 * h + which
                        m = h if which == 0 else 8 + h
                        sl = o % 2
                        dst = qT if which == 0 else kTl

                        def _(t, o=o, sl=sl, paS=paS, paV=paV, exp_last_s=exp_last_s):
                            t.wait_ge(dWQK[sl].h, wqk_tgt[o])
                            if paS:
                                t.wait_ge(sS.h, paS)
                            if exp_last_s:
                                t.wait_ge(sS.h, exp_last_s)
                            if paV:
                                t.wait_ge(sV.h, paV)
                            for kt in range(16):
                                for cc in range(4):
                                    mm = t.matmul(
                                        PA[:, cc * 512:(cc + 1) * 512],
                                        wqkw[sl][:, kt * 128:(kt + 1) * 128],
                                        ln1T[:, kt * 2048 + cc * 512: kt * 2048 + (cc + 1) * 512],
                                        start=(kt == 0), stop=(kt == 15))
                            tick(sT, mm)
                        blk.tensor(_)
                        qk_mm[o] = sT.n

                        def _(s, o=o, m=m, dst=dst):
                            s.wait_ge(sT.h, qk_mm[o])
                            if o == 0:
                                s.wait_ge(dMISC.h, misc_done)
                            tick(sS, s.activation(dst[:, :], PA[:, :], ACT.Identity,
                                                  bias=bqk[:, m:m + 1], scale=1.0))
                        blk.scalar(_)
                        paS = sS.n
                        paV = 0
                        exp_last_s = 0
                        if o + 2 < 16:
                            load_wqk(o + 2)

                        if which == 0:
                            q_ev = sS.n

                            def _(v, h=h, q_ev=q_ev, att_t_last=att_t_last):
                                v.wait_ge(sS.h, q_ev)
                                if att_t_last:
                                    v.wait_ge(sT.h, att_t_last)
                                tick(sV, v.tensor_tensor(expT[:, 0:2048], qT[:, :],
                                                         qT[:, :], op=ALU.mult))
                            blk.vector(_)
                            sqq_v = sV.n

                            def _(t, h=h, sqq_v=sqq_v, tail_prev=tail_prev):
                                t.wait_ge(sV.h, sqq_v)
                                if tail_prev:
                                    t.wait_ge(sV.h, tail_prev)
                                for cc in range(3):
                                    t.matmul(PB[:, cc * 512:(cc + 1) * 512], ones_f[:, :],
                                             expT[:, cc * 512:(cc + 1) * 512],
                                             start=True, stop=True)
                                tick(sT, t.matmul(PB[:, 1536:2048], ones_f[:, :],
                                                  expT[:, 1536:2048], start=True, stop=True))
                            blk.tensor(_)
                            nq_t = sT.n

                            def _(v, nq_t=nq_t):
                                v.wait_ge(sT.h, nq_t)
                                tick(sV, v.reduce_max(qkc[:, 0:1], PB[:, 0:2048], axis=AX))
                            blk.vector(_)
                            redq_v = sV.n
                    qk_ev = sS.n

                    def _(v, h=h, qk_ev=qk_ev, nq_t=nq_t):
                        v.wait_ge(sS.h, qk_ev)
                        v.wait_ge(sT.h, nq_t)
                        tick(sV, v.tensor_tensor(expT[:, 0:2048], kTl[:, :], kTl[:, :],
                                                 op=ALU.mult))
                    blk.vector(_)
                    sqk_v = sV.n

                    def _(t, h=h, sqk_v=sqk_v, redq_v=redq_v):
                        t.wait_ge(sV.h, sqk_v)
                        t.wait_ge(sV.h, redq_v)
                        for cc in range(3):
                            t.matmul(PB[:, cc * 512:(cc + 1) * 512], ones_f[:, :],
                                     expT[:, cc * 512:(cc + 1) * 512], start=True, stop=True)
                        tick(sT, t.matmul(PB[:, 1536:2048], ones_f[:, :],
                                          expT[:, 1536:2048], start=True, stop=True))
                    blk.tensor(_)
                    nk_t = sT.n

                    def _(v, h=h, nk_t=nk_t):
                        v.wait_ge(sT.h, nk_t)
                        n1 = tick(sV, v.reduce_max(qkc[:, 1:2], PB[:, 0:2048], axis=AX))
                        v.wait_ge(sV.h, n1)
                        n2 = tick(sV, v.tensor_tensor(qkc[:, 2:3], qkc[:, 0:1], qkc[:, 1:2],
                                                      op=ALU.mult))
                        v.wait_ge(sV.h, n2)
                        tick(sV, v.tensor_scalar_mul(qkc[:, 2:3], qkc[:, 2:3], 1.0 / 128.0))
                    blk.vector(_)
                    km_v = sV.n

                    def _(s, h=h, km_v=km_v):
                        s.wait_ge(sV.h, km_v)
                        tick(sS, s.sqrt(qkc[:, 3:4], qkc[:, 2:3]))
                    blk.scalar(_)
                    cqs_s = sS.n

                    def _(v, h=h, cqs_s=cqs_s):
                        v.wait_ge(sS.h, cqs_s)
                        tick(sV, v.tensor_scalar_mul(negc[:, h:h + 1], qkc[:, 3:4], -1.0))
                    blk.vector(_)
                    negc_s = sV.n

                    pair_list = [(cc4, p) for cc4 in range(4)
                                 for p in range(2 * (cc4 + 1))]
                    NP = len(pair_list)
                    ch_first = {c: sum(2 * (i + 1) for i in range(c))
                                for c in range(4)}
                    ch_last = {c: ch_first[c] + 2 * (c + 1) - 1 for c in range(4)}
                    AVB = {0: 1, 1: 0, 2: 1, 3: 0}
                    SMB = {0: 2, 1: 3, 2: 2, 3: 3}
                    exp_s_l = [0] * NP
                    sc_t_l = [0] * NP
                    mv_l = [0] * NP
                    av_t_l = [att_t_last] * NP
                    tail_vl = {}

                    for k in range(NP + 2):
                        if k < NP:
                            cc4, p = pair_list[k]

                            def _(t, h=h, k=k, cc4=cc4, p=p, km_v=km_v):
                                if k >= 2 and exp_s_l[k - 2]:
                                    t.wait_ge(sS.h, exp_s_l[k - 2])
                                if k < 2:
                                    t.wait_ge(sV.h, km_v)
                                for q in range(2):
                                    kt = 2 * p + q
                                    bank = 2 * (k % 2) + q
                                    mm = t.matmul(PA[:, bank * 512:(bank + 1) * 512],
                                                  kTl[:, kt * 128:(kt + 1) * 128],
                                                  qT[:, cc4 * 512:(cc4 + 1) * 512],
                                                  start=True, stop=True)
                                tick(sT, mm)
                            blk.tensor(_)
                            sc_t_l[k] = sT.n

                            def _(s, h=h, k=k, negc_s=negc_s):
                                s.wait_ge(sT.h, sc_t_l[k])
                                if k == 0:
                                    s.wait_ge(sV.h, negc_s)
                                if k >= 4 and av_t_l[k - 4]:
                                    s.wait_ge(sT.h, av_t_l[k - 4])
                                tick(sS, s.activation(
                                    expT[:, (k % 4) * 1024:(k % 4 + 1) * 1024],
                                    PA[:, (k % 2) * 1024:(k % 2 + 1) * 1024],
                                    ACT.Exp, bias=negc[:, h:h + 1], scale=SCL))
                            blk.scalar(_)
                            exp_s_l[k] = sS.n

                            if p >= 2 * cc4:
                                moff = (p - 2 * cc4) * 1024

                                def _(v, h=h, k=k, moff=moff):
                                    if h == 0 and k == 0:
                                        v.wait_ge(dMSK.h, msk_done)
                                    v.wait_ge(sS.h, exp_s_l[k])
                                    sl_ap = expT[:, (k % 4) * 1024:(k % 4 + 1) * 1024]
                                    tick(sV, v.tensor_tensor(sl_ap, sl_ap,
                                                             msk[:, moff:moff + 1024],
                                                             op=ALU.mult))
                                blk.vector(_)
                                mv_l[k] = sV.n

                        if k >= 2:
                            m = k - 2
                            cc4, p = pair_list[m]
                            first = (p == 0)
                            last = (m == ch_last[cc4])
                            ab, sb = AVB[cc4], SMB[cc4]

                            def _(t, h=h, m=m, cc4=cc4, p=p, first=first,
                                  last=last, ab=ab, sb=sb, km_v=km_v):
                                t.wait_ge(sS.h, exp_s_l[m])
                                if mv_l[m]:
                                    t.wait_ge(sV.h, mv_l[m])
                                if first:
                                    if cc4 >= 2:
                                        t.wait_ge(sV.h, tail_vl[cc4 - 2])
                                    else:
                                        t.wait_ge(sV.h, km_v)
                                for q in range(2):
                                    kt = 2 * p + q
                                    rhs = expT[:, (m % 4) * 1024 + q * 512:
                                               (m % 4) * 1024 + (q + 1) * 512]
                                    t.matmul(PB[:, ab * 512:(ab + 1) * 512],
                                             v_all[:, kt * 1024 + h * 128:
                                                   kt * 1024 + (h + 1) * 128],
                                             rhs, start=(first and q == 0),
                                             stop=(last and q == 1))
                                    mm = t.matmul(PB[:, sb * 512:(sb + 1) * 512],
                                                  ones_f[:, :], rhs,
                                                  start=(first and q == 0),
                                                  stop=(last and q == 1))
                                tick(sT, mm)
                            blk.tensor(_)
                            av_t_l[m] = sT.n

                            if last:
                                def _(v, h=h, m=m, cc4=cc4, ab=ab, sb=sb):
                                    v.wait_ge(sT.h, av_t_l[m])
                                    if cc4 == 0 and h >= 4:
                                        v.wait_ge(dYD.h,
                                                  yd_pair_tgt[(h - 4) // 2])
                                    n1 = tick(sV, v.reciprocal(
                                        rcp[:, :], PB[:, sb * 512:(sb + 1) * 512]))
                                    v.wait_ge(sV.h, n1)
                                    n2 = tick(sV, v.tensor_copy(rcpb[:, :], rcp[:, :]))
                                    v.wait_ge(sV.h, n2)
                                    tick(sV, v.tensor_tensor(
                                        yTb[:, (h % 4) * 2048 + cc4 * 512:
                                            (h % 4) * 2048 + (cc4 + 1) * 512],
                                        PB[:, ab * 512:(ab + 1) * 512],
                                        rcpb[:, :], op=ALU.mult))
                                blk.vector(_)
                                tail_vl[cc4] = sV.n

                    att_t_last = av_t_l[NP - 1]
                    exp_last_s = exp_s_l[NP - 1]
                    tail_prev = tail_vl[3]
                    paS = 0
                    paV = 0

                    if h % 2 == 1:
                        c = h // 2
                        s0 = (2 * c) % 4

                        def _(g, c=c, s0=s0, tp=tail_vl[3]):
                            g.wait_ge(sV.h, tp)
                            ydv = ydr.rearrange("(hh p) t -> p hh t", p=128)
                            tick(dYD, g.dma_start(
                                out=ydv[:, 2 * c:2 * c + 2, :],
                                in_=yTb[:, s0 * 2048:(s0 + 2) * 2048]
                                .rearrange("p (hh t) -> p hh t", hh=2)), 16)
                            g.wait_ge(dYD.h, dYD.n)
                            tick(dCCI, g.dma_start(
                                out=cc_in[c].rearrange("(hh p) t -> p hh t", p=128),
                                in_=ydv[:, 2 * c:2 * c + 2, ts_comp]), 16)
                            if c == 3:
                                for cc in range(4):
                                    tick(dCCL, g.dma_start(
                                        out=v_all[:, 2 * cc * 1024:(2 * cc + 2) * 1024]
                                        .rearrange("p (hh t) -> p hh t", hh=2),
                                        in_=ydv[:, 2 * cc:2 * cc + 2, ts_j]), 16)
                            g.wait_ge(dCCI.h, dCCI.n)
                            if c == 0:
                                g.wait_ge(wS.h, 1)
                            tick(cS, g.collective_compute(
                                "AllGather", ALU.bypass,
                                replica_groups=[[0, 1, 2, 3, 4, 5, 6, 7]],
                                ins=[cc_in[c]], outs=[cc_out[c]]))
                            if c == 3:
                                g.wait_ge(cS.h, cS.n)
                                for cc in range(4):
                                    tick(dCCO, g.dma_start(
                                        out=v_all[:, 8192 + 2 * cc * 1024:
                                                  8192 + (2 * cc + 2) * 1024]
                                        .rearrange("p (hh t) -> p hh t", hh=2),
                                        in_=cc_out[cc][ts_p1, :]
                                        .rearrange("(hh p) t -> p hh t", p=128)), 16)
                        blk.gpsimd(_)
                        yd_pair_tgt[c] = dYD.n
                ccl_done = dCCL.n
                cco_done = dCCO.n
                att_tail_v = tail_prev

        x1 = ctx0.enter_context(nc.sbuf_tensor("x1", [128, 8 * 2048], F32))
        hT = ctx0.enter_context(
            nc.sbuf_tensor("hT", [128, 16 * 1024], BF16, is_transpose_dest=True))
        with ExitStack() as big2:
            h_t = [big2.enter_context(nc.sbuf_tensor(f"h_t{i}", [128, 2048], BF16))
                   for i in range(2)]
            y_all = v_all
            wpjB = wpjA

            def _(sy):
                sy.wait_ge(sT.h, qk_mm[15])
                tick(dXO, sy.dma_start(
                    out=x1[:, :].rearrange("p (tt c) -> p tt c", tt=8),
                    in_=xo_h.rearrange("(tt p) c -> p tt c", p=128)), 16)
            blk.sync(_)
            xo_done = dXO.n

            s_h = {}
            rc_vs2 = {}
            sq2_s = {}
            ab2_v = {}
            rms_v = {}
            ht_cnt = {k: 0 for k in range(4)}

            def emitA2(i):
                sl = i

                def _(v, i=i, sl=sl):
                    v.wait_ge(sV.h, pj_ev[i])
                    for cc in range(3):
                        v.bn_stats(bn6s[sl][:, cc * 6:(cc + 1) * 6],
                                   x1[:, i * 2048 + cc * 512: i * 2048 + (cc + 1) * 512])
                    n1 = tick(sV, v.bn_stats(bn6s[sl][:, 18:24],
                                             x1[:, i * 2048 + 1536: i * 2048 + 2048]))
                    v.wait_ge(sV.h, n1)
                    n2 = tick(sV, v.bn_aggr(mvs[sl][:, :], bn6s[sl][:, :]))
                    v.wait_ge(sV.h, n2)
                    n3 = tick(sV, v.tensor_scalar(tcs[sl][:, 0:1], mvs[sl][:, 1:2],
                                                  1.000001, 1e-11,
                                                  op0=ALU.mult, op1=ALU.add))
                    v.wait_ge(sV.h, n3)
                    tick(sV, v.reciprocal(tcs[sl][:, 1:2], tcs[sl][:, 0:1]))
                blk.vector(_)
                rc_vs2[i] = sV.n

                def _(s, i=i, sl=sl):
                    s.wait_ge(sV.h, rc_vs2[i])
                    tick(sS, s.sqrt(acs[sl][:, :], tcs[sl][:, 1:2]))
                blk.scalar(_)
                sq2_s[i] = sS.n

                def _(v, i=i, sl=sl):
                    v.wait_ge(sS.h, sq2_s[i])
                    tick(sV, v.scalar_tensor_tensor(bcs[sl][:, :], mvs[sl][:, 0:1], -1.0,
                                                    acs[sl][:, :], op0=ALU.mult,
                                                    op1=ALU.mult))
                blk.vector(_)
                ab2_v[i] = sV.n

            def emitB2(i):
                sl = i
                hsl = i % 2

                def _(s, i=i, sl=sl, hsl=hsl):
                    s.wait_ge(sV.h, ab2_v[i])
                    if i == 0:
                        s.wait_ge(sT.h, att_t_last)
                        s.wait_ge(sV.h, att_tail_v)
                    if i >= 2:
                        s.wait_ge(dHT[(i - 2) % 4].h, ht_cnt[(i - 2) % 4])
                    tick(sS, s.activation(h_t[hsl][:, :],
                                          x1[:, i * 2048:(i + 1) * 2048],
                                          ACT.Identity, bias=bcs[sl][:, :],
                                          scale=acs[sl][:, :]))
                blk.scalar(_)
                s_h[i] = sS.n

                def _(v, i=i, hsl=hsl):
                    v.wait_ge(sS.h, s_h[i])
                    tick(sV, v.tensor_tensor(h_t[hsl][:, :], h_t[hsl][:, :],
                                             rmsw_b[:, :], op=ALU.mult))
                blk.vector(_)
                rms_v[i] = sV.n

                def _(sy, i=i, hsl=hsl):
                    sy.wait_ge(sV.h, rms_v[i])
                    if i < 2:
                        sy.wait_ge(dYD.h, 64)
                        sy.wait_ge(sT.h, att_t_last)
                    o_v = hT[:, :].rearrange("p (dt tt) -> p dt tt", dt=16)[:, :, i * 128:(i + 1) * 128]
                    i_v = h_t[hsl][:, :].rearrange("t (dt p) -> t dt p", dt=16)
                    tick(dHT[i % 4], sy.dma_start_transpose(o_v, i_v), 16)
                blk.sync(_)
                ht_cnt[i % 4] = dHT[i % 4].n

            pj_mm = {}
            pj_ev = {}
            pj1_mm = {}
            pj1_ev = {}
            for tt in range(8):
                PX = PA if tt % 2 == 0 else PB

                def _(t, tt=tt, PX=PX):
                    if tt == 0:
                        t.wait_ge(dCCL.h, ccl_done)
                        t.wait_ge(dWPJ.h, wpja_done)
                        t.wait_ge(dMISCG.h, misc_g)
                        t.wait_ge(sS.h, exp_last_s)
                    if tt == 1:
                        t.wait_ge(sV.h, att_tail_v)
                    if tt >= 2:
                        t.wait_ge(sV.h, pj1_ev[tt - 2])
                    for dq in range(4):
                        t.matmul(PX[:, dq * 512:(dq + 1) * 512], ones_r[:, :],
                                 bpj[:, dq * 512:(dq + 1) * 512], start=True, stop=False)
                    for yf in range(8):
                        for dq in range(4):
                            mm = t.matmul(
                                PX[:, dq * 512:(dq + 1) * 512],
                                y_all[:, yf * 1024 + tt * 128: yf * 1024 + (tt + 1) * 128],
                                wpjA[:, yf * 2048 + dq * 512: yf * 2048 + (dq + 1) * 512],
                                start=False, stop=(yf == 7))
                    tick(sT, mm)
                blk.tensor(_)
                pj1_mm[tt] = sT.n

                def _(v, tt=tt, PX=PX):
                    v.wait_ge(sT.h, pj1_mm[tt])
                    if tt == 0:
                        v.wait_ge(dXO.h, xo_done)
                    for dq in range(3):
                        sl = x1[:, tt * 2048 + dq * 512: tt * 2048 + (dq + 1) * 512]
                        v.tensor_tensor(sl, PX[:, dq * 512:(dq + 1) * 512], sl, op=ALU.add)
                    sl = x1[:, tt * 2048 + 1536: tt * 2048 + 2048]
                    tick(sV, v.tensor_tensor(sl, PX[:, 1536:2048], sl, op=ALU.add))
                blk.vector(_)
                pj1_ev[tt] = sV.n

            def _(g):
                g.wait_ge(dWPJ.h, wpja_done)
                g.wait_ge(sT.h, pj1_mm[7])
                for yf in range(8):
                    tick(dWPJ, g.dma_start(out=wpjB[:, yf * 2048:(yf + 1) * 2048],
                                           in_=wpj_h[:, 8 + yf, :]), 16)
            blk.gpsimd(_)
            wpjb_done = dWPJ.n

            for tt in range(8):
                PX = PA if tt % 2 == 0 else PB

                def _(t, tt=tt, PX=PX):
                    if tt == 0:
                        t.wait_ge(dCCO.h, cco_done)
                        t.wait_ge(dWPJ.h, wpjb_done)
                    if tt < 2:
                        t.wait_ge(sV.h, pj1_ev[tt + 6])
                    else:
                        t.wait_ge(sV.h, pj_ev[tt - 2])
                    for yf in range(8):
                        for dq in range(4):
                            mm = t.matmul(
                                PX[:, dq * 512:(dq + 1) * 512],
                                y_all[:, 8192 + yf * 1024 + tt * 128:
                                      8192 + yf * 1024 + (tt + 1) * 128],
                                wpjB[:, yf * 2048 + dq * 512: yf * 2048 + (dq + 1) * 512],
                                start=(yf == 0), stop=(yf == 7))
                    tick(sT, mm)
                blk.tensor(_)
                pj_mm[tt] = sT.n

                def _(v, tt=tt, PX=PX):
                    v.wait_ge(sT.h, pj_mm[tt])
                    for dq in range(3):
                        sl = x1[:, tt * 2048 + dq * 512: tt * 2048 + (dq + 1) * 512]
                        v.tensor_tensor(sl, PX[:, dq * 512:(dq + 1) * 512], sl, op=ALU.add)
                    sl = x1[:, tt * 2048 + 1536: tt * 2048 + 2048]
                    tick(sV, v.tensor_tensor(sl, PX[:, 1536:2048], sl, op=ALU.add))
                blk.vector(_)
                pj_ev[tt] = sV.n
                emitA2(tt)
                emitB2(tt)
            proj_done_v = sV.n
        vsc.close()

        ht_done = tuple(ht_cnt[i] for i in range(4))
        with ExitStack() as big3:
            su = [big3.enter_context(nc.sbuf_tensor(f"su{i}", [128, 8 * 1024], BF16))
                  for i in range(2)]
            guw = [big3.enter_context(nc.sbuf_tensor(f"guw{i}", [128, 4096], BF16))
                   for i in range(2)]
            dnw = [big3.enter_context(nc.sbuf_tensor(f"dnw{i}", [128, 4096], BF16))
                   for i in range(4)]
            silu_sb = big3.enter_context(nc.sbuf_tensor("silu_sb", [128, 1024], BF16))
            lg_sb = big3.enter_context(nc.sbuf_tensor("lg_sb", [8, 1024], F32))
            wdB = big3.enter_context(nc.sbuf_tensor("wdB", [128, 1024], BF16))

            def _(t):
                for i in range(4):
                    t.wait_ge(dHT[i].h, ht_done[i])
                t.wait_ge(dMISCG.h, misc_g)
                for cc in range(2):
                    for kt in range(16):
                        mm = t.matmul(PA[0:8, cc * 512:(cc + 1) * 512],
                                      rts[:, kt * 8:(kt + 1) * 8],
                                      hT[:, kt * 1024 + cc * 512: kt * 1024 + (cc + 1) * 512],
                                      start=(kt == 0), stop=(kt == 15))
                tick(sT, mm)
            blk.tensor(_)
            lg_t = sT.n

            def _(v):
                v.wait_ge(sT.h, lg_t)
                tick(sV, v.tensor_copy(lg_sb[:, :], PA[0:8, 0:1024]))
            blk.vector(_)
            lg_v = sV.n

            def _(t):
                t.wait_ge(sV.h, lg_v)
                t.wait_ge(dMISC.h, misc_done)
                for i in range(7):
                    t.transpose(PB[:, i * 8:(i + 1) * 8], lg_sb[:, i * 128:(i + 1) * 128],
                                id32[0:8, 0:8])
                tick(sT, t.transpose(PB[:, 56:64], lg_sb[:, 896:1024], id32[0:8, 0:8]))
            blk.tensor(_)
            lgT_t = sT.n

            def _(v):
                v.wait_ge(sT.h, lgT_t)
                n1 = tick(sV, v.tensor_copy(lgT[:, :], PB[:, 0:64]))
                v.wait_ge(sV.h, n1)
                for i in range(7):
                    v.reduce_max(mcol[:, i:i + 1], lgT[:, i * 8:(i + 1) * 8], axis=AX)
                n2 = tick(sV, v.reduce_max(mcol[:, 7:8], lgT[:, 56:64], axis=AX))
                v.wait_ge(sV.h, n2)
                tick(sV, v.tensor_scalar_mul(mneg[:, :], mcol[:, :], -1.0))
            blk.vector(_)
            m1_v = sV.n
            lgT_cp_v = m1_v - 2

            def _(s):
                s.wait_ge(sV.h, m1_v)
                for i in range(7):
                    s.activation(e_sb[:, i * 8:(i + 1) * 8], lgT[:, i * 8:(i + 1) * 8],
                                 ACT.Exp, bias=mneg[:, i:i + 1], scale=1.0)
                tick(sS, s.activation(e_sb[:, 56:64], lgT[:, 56:64],
                                      ACT.Exp, bias=mneg[:, 7:8], scale=1.0))
            blk.scalar(_)
            e_s = sS.n

            def _(v):
                v.wait_ge(sS.h, e_s)
                n1 = tick(sV, v.tensor_scalar(msel[:, :], e_sb[:, :], 1.0, -1.0,
                                              op0=ALU.is_ge, op1=ALU.mult))
                v.wait_ge(sV.h, n1)
                n2 = tick(sV, v.tensor_scalar_add(msel[:, :], msel[:, :], 1.0))
                v.wait_ge(sV.h, n2)
                n3 = tick(sV, v.tensor_tensor(msel[:, :], e_sb[:, :], msel[:, :], op=ALU.mult))
                v.wait_ge(sV.h, n3)
                for i in range(7):
                    v.reduce_max(mcol[:, i:i + 1], msel[:, i * 8:(i + 1) * 8], axis=AX)
                n4 = tick(sV, v.reduce_max(mcol[:, 7:8], msel[:, 56:64], axis=AX))
                v.wait_ge(sV.h, n4)
                for i in range(7):
                    v.tensor_scalar(wd_tok[:, i * 8:(i + 1) * 8], e_sb[:, i * 8:(i + 1) * 8],
                                    mcol[:, i:i + 1], None, op0=ALU.is_ge)
                n5 = tick(sV, v.tensor_scalar(wd_tok[:, 56:64], e_sb[:, 56:64],
                                              mcol[:, 7:8], None, op0=ALU.is_ge))
                v.wait_ge(sV.h, n5)
                n6 = tick(sV, v.tensor_tensor(wd_tok[:, :], e_sb[:, :], wd_tok[:, :], op=ALU.mult))
                v.wait_ge(sV.h, n6)
                n7 = tick(sV, v.tensor_scalar_add(mcol[:, :], mcol[:, :], 1.0))
                v.wait_ge(sV.h, n7)
                tick(sV, v.reciprocal(mcol[:, :], mcol[:, :]))
            blk.vector(_)
            w2_v = sV.n

            def _(s):
                s.wait_ge(sV.h, w2_v)
                for i in range(7):
                    s.activation(wd_tok[:, i * 8:(i + 1) * 8], wd_tok[:, i * 8:(i + 1) * 8],
                                 ACT.Identity, bias=0.0, scale=mcol[:, i:i + 1])
                tick(sS, s.activation(wd_tok[:, 56:64], wd_tok[:, 56:64],
                                      ACT.Identity, bias=0.0, scale=mcol[:, 7:8]))
            blk.scalar(_)
            wd_s = sS.n

            gu_mm = {}
            dn_mm = {}
            gu_tgt = {}
            dn_tgt = {}
            su_v = {}
            silu_s = {}
            ev_v = {}
            wdT_done = 0
            wdb_v = 0
            wdb_cp_v = 0

            def load_gu(idx, ft):
                e = EORD[idx]
                sl = ft % 2

                def _(g, idx=idx, e=e, ft=ft, sl=sl):
                    if (idx, ft) == (0, 0):
                        g.wait_ge(sT.h, pj_mm[7])
                    prev = (EORD[idx], ft - 2) if ft >= 2 else \
                        ((EORD[idx - 1], ft + 6) if idx >= 1 else None)
                    if prev in gu_mm:
                        g.wait_ge(sT.h, gu_mm[prev])
                    tick(dGU[sl], g.dma_start(out=guw[sl][:, 0:2048], in_=gu_h[e, ft, 0]), 16)
                    tick(dGU[sl], g.dma_start(out=guw[sl][:, 2048:4096], in_=gu_h[e, ft, 1]), 16)
                blk.gpsimd(_)
                gu_tgt[(e, ft)] = dGU[sl].n

            def load_dn(idx, dq):
                e = EORD[idx]

                def _(g, idx=idx, e=e, dq=dq):
                    if (idx, dq) == (0, 0):
                        g.wait_ge(sT.h, pj_mm[7])
                    if idx >= 1:
                        prev = (EORD[idx - 1], dq // 2)
                        if prev in dn_mm:
                            g.wait_ge(sT.h, dn_mm[prev])
                    tick(dDN[dq], g.dma_start(out=dnw[dq][:, :], in_=dn_h[e, dq]), 16)
                blk.gpsimd(_)
                dn_tgt[(e, dq)] = dDN[dq].n

            load_gu(0, 0)
            load_gu(0, 1)
            for dq in range(4):
                load_dn(0, dq)

            for idx in range(NE):
                e = EORD[idx]
                esl = idx % 2

                for ft in range(FT):
                    sl = ft % 2
                    PX2 = PA if ft % 2 == 0 else PB

                    def _(t, idx=idx, e=e, ft=ft, sl=sl, PX2=PX2,
                          wdb_cp_v=wdb_cp_v):
                        t.wait_ge(dGU[sl].h, gu_tgt[(e, ft)])
                        if (idx, ft) == (0, 0):
                            t.wait_ge(sV.h, lg_v)
                        if (idx, ft) == (0, 1):
                            t.wait_ge(sV.h, lgT_cp_v)
                        prevk = (e, ft - 2) if ft >= 2 else \
                            ((EORD[idx - 1], ft + 6) if idx >= 1 else None)
                        if prevk in su_v:
                            t.wait_ge(sV.h, su_v[prevk])
                        if idx >= 1 and ft < 2:
                            t.wait_ge(sV.h, ev_v[(EORD[idx - 1], 1, 2 + ft)])
                            if ft == 1 and e != 8:
                                t.wait_ge(sV.h, wdb_cp_v)
                        for gui in range(2):
                            for kt in range(16):
                                for cc in range(2):
                                    bank = 2 * gui + cc
                                    mm = t.matmul(
                                        PX2[:, bank * 512:(bank + 1) * 512],
                                        guw[sl][:, gui * 2048 + kt * 128: gui * 2048 + (kt + 1) * 128],
                                        hT[:, kt * 1024 + cc * 512: kt * 1024 + (cc + 1) * 512],
                                        start=(kt == 0), stop=(kt == 15))
                        tick(sT, mm)
                    blk.tensor(_)
                    gu_mm[(e, ft)] = sT.n

                    if ft == 0 and e != 8:
                        def _(t, e=e, idx=idx, wdb_v=wdb_v):
                            t.wait_ge(sV.h, wdb_v)
                            t.wait_ge(sV.h, ev_v[(EORD[idx - 1], 1, 3)])
                            t.matmul(PB[:, 0:512], selb[:, e * 128:(e + 1) * 128],
                                     wd8[:, 0:512], start=True, stop=True)
                            tick(sT, t.matmul(PB[:, 512:1024],
                                              selb[:, e * 128:(e + 1) * 128],
                                              wd8[:, 512:1024], start=True, stop=True))
                        blk.tensor(_)
                        wdb_t = sT.n

                        def _(v, wdb_t=wdb_t):
                            v.wait_ge(sT.h, wdb_t)
                            tick(sV, v.tensor_copy(wdB[:, :], PB[:, 0:1024]))
                        blk.vector(_)
                        wdb_cp_v = sV.n

                    def _(s, e=e, ft=ft, idx=idx, PX2=PX2):
                        s.wait_ge(sT.h, gu_mm[(e, ft)])
                        prevk = (e, ft - 1) if ft >= 1 else \
                            ((EORD[idx - 1], 7) if idx >= 1 else None)
                        if prevk in su_v:
                            s.wait_ge(sV.h, su_v[prevk])
                        tick(sS, s.activation(silu_sb[:, :], PX2[:, 0:1024], ACT.Silu,
                                              bias=0.0, scale=1.0))
                    blk.scalar(_)
                    silu_s[(e, ft)] = sS.n

                    def _(v, e=e, ft=ft, idx=idx, esl=esl, wdb_cp_v=wdb_cp_v, PX2=PX2):
                        v.wait_ge(sS.h, silu_s[(e, ft)])
                        if e != 8 and ft == 0:
                            v.wait_ge(sV.h, wdb_cp_v)
                        if ft == 0 and idx >= 2:
                            v.wait_ge(sT.h, dn_mm[(EORD[idx - 2], 1)])
                        for cc in range(2):
                            ap = su[esl][:, ft * 1024 + cc * 512: ft * 1024 + (cc + 1) * 512]
                            ii = v.tensor_tensor(ap, silu_sb[:, cc * 512:(cc + 1) * 512],
                                                 PX2[:, (2 + cc) * 512:(3 + cc) * 512],
                                                 op=ALU.mult)
                        n1 = tick(sV, ii)
                        if e != 8:
                            v.wait_ge(sV.h, n1)
                            for cc in range(2):
                                ap = su[esl][:, ft * 1024 + cc * 512: ft * 1024 + (cc + 1) * 512]
                                ii = v.tensor_tensor(ap, ap, wdB[:, cc * 512:(cc + 1) * 512],
                                                     op=ALU.mult)
                            tick(sV, ii)
                    blk.vector(_)
                    su_v[(e, ft)] = sV.n

                    nxt = (idx, ft + 2) if ft + 2 < FT else (idx + 1, ft + 2 - FT)
                    if nxt[0] < NE:
                        load_gu(*nxt)

                if idx == 0:
                    def _(t):
                        t.wait_ge(sS.h, wd_s)
                        t.wait_ge(sV.h, su_v[(8, 6)])
                        for i in range(7):
                            t.transpose(PA[0:8, 512 + i * 128: 512 + (i + 1) * 128],
                                        wd_tok[:, i * 8:(i + 1) * 8], id32[:, :])
                        tick(sT, t.transpose(PA[0:8, 1408:1536], wd_tok[:, 56:64], id32[:, :]))
                    blk.tensor(_)
                    wdT_done = sT.n

                    def _(v):
                        v.wait_ge(sT.h, wdT_done)
                        tick(sV, v.tensor_copy(wd8[:, :], PA[0:8, 512:1536]))
                    blk.vector(_)
                    wdb_v = sV.n

                for dqp in range(2):
                    for tq in range(4):
                        PX = PA if tq % 2 == 0 else PB

                        def _(t, idx=idx, e=e, dqp=dqp, tq=tq, PX=PX, wdb_v=wdb_v):
                            if tq == 0:
                                t.wait_ge(dDN[2 * dqp].h, dn_tgt[(e, 2 * dqp)])
                                t.wait_ge(dDN[2 * dqp + 1].h, dn_tgt[(e, 2 * dqp + 1)])
                            if dqp == 0 and tq < 2:
                                t.wait_ge(sV.h, su_v[(e, 6 + tq)])
                            if idx == 0 and dqp == 0 and tq < 2:
                                t.wait_ge(sV.h, wdb_v)
                            if tq >= 2:
                                prev = (e, dqp, tq - 2)
                            elif dqp == 1:
                                prev = (e, 0, tq + 2)
                            elif idx >= 1:
                                prev = (EORD[idx - 1], 1, tq + 2)
                            else:
                                prev = None
                            if prev in ev_v:
                                t.wait_ge(sV.h, ev_v[prev])
                            for ft in range(FT):
                                if tq == 0 and dqp == 0:
                                    t.wait_ge(sV.h, su_v[(e, ft)])
                                for q in range(2):
                                    tt = tq * 2 + q
                                    for dqi in range(2):
                                        bank = q * 2 + dqi
                                        mm = t.matmul(
                                            PX[:, bank * 512:(bank + 1) * 512],
                                            su[idx % 2][:, ft * 1024 + tt * 128: ft * 1024 + (tt + 1) * 128],
                                            dnw[2 * dqp + dqi][:, ft * 512:(ft + 1) * 512],
                                            start=(ft == 0), stop=(ft == FT - 1))
                            tick(sT, mm)
                        blk.tensor(_)
                        dn_mm[(e, dqp, tq)] = sT.n
                        if tq == 3:
                            dn_mm[(e, dqp)] = sT.n

                        def _(v, idx=idx, e=e, dqp=dqp, tq=tq, PX=PX):
                            v.wait_ge(sT.h, dn_mm[(e, dqp, tq)])
                            if idx == 0 and dqp == 0 and tq == 0:
                                v.wait_ge(sV.h, proj_done_v)
                            for q in range(2):
                                tt = tq * 2 + q
                                for dqi in range(2):
                                    dq = 2 * dqp + dqi
                                    bank = q * 2 + dqi
                                    ap = x1[:, tt * 2048 + dq * 512: tt * 2048 + (dq + 1) * 512]
                                    ii = v.tensor_tensor(ap, PX[:, bank * 512:(bank + 1) * 512],
                                                         ap, op=ALU.add)
                            tick(sV, ii)
                        blk.vector(_)
                        ev_v[(e, dqp, tq)] = sV.n
                if idx + 1 < NE:
                    for dq in range(4):
                        load_dn(idx + 1, dq)
            ffn_done_v = sV.n

            e_last = EORD[NE - 1]
            for tq in range(4):
                def _(sy, tq=tq):
                    sy.wait_ge(sV.h, ev_v[(e_last, 1, tq)])
                    tick(dOUT, sy.dma_start(
                        out=out_h.rearrange("(tt p) c -> p tt c", p=128)[:, 2 * tq:2 * tq + 2, :],
                        in_=x1[:, tq * 4096:(tq + 1) * 4096]
                        .rearrange("p (tt c) -> p tt c", tt=2)), 16)
                blk.sync(_)

            def _(sy):
                sy.wait_ge(dOUT.h, dOUT.n)
            blk.sync(_)

            def _(g):
                g.wait_ge(dOUT.h, dOUT.n)
            blk.gpsimd(_)

    return nc


def _pack_core(inp, c):
    b, j = c // 2, c % 2
    wa = np.asarray(inp["L0_attn_w_attn"])
    qb, kb, vb = 1024 * j, 2048 + 1024 * j, 4096 + 1024 * j
    d = {}
    x = np.asarray(inp["L0_attn_x"])
    d["x"] = np.ascontiguousarray(x[b])
    d["xo"] = np.ascontiguousarray(x[b][j * 1024:(j + 1) * 1024])
    qk = np.concatenate([wa[:, qb:qb + 1024], wa[:, kb:kb + 1024]], axis=1)
    d["wqk"] = np.ascontiguousarray(
        qk.reshape(16, 128, 16, 128).transpose(2, 1, 0, 3).reshape(16, 128, 2048))
    wv = wa[:, vb:vb + 1024]
    d["wv"] = np.ascontiguousarray(
        wv.reshape(16, 128, 2, 512).transpose(2, 1, 0, 3).reshape(2, 128, 8192))
    wp = np.asarray(inp["L0_attn_w_proj"])
    perm = list(range(8 * j, 8 * j + 8)) + list(range(8 * (1 - j), 8 * (1 - j) + 8))
    d["wpj"] = np.ascontiguousarray(wp.reshape(16, 128, 2048)[perm].transpose(1, 0, 2))
    ba = np.asarray(inp["L0_attn_b_attn"])
    d["bqk"] = np.ascontiguousarray(
        np.concatenate([ba[qb:qb + 1024], ba[kb:kb + 1024]]).reshape(16, 128).T)
    d["bv"] = np.ascontiguousarray(ba[vb:vb + 1024].reshape(1, 1024))
    d["bpj"] = np.ascontiguousarray(np.asarray(inp["L0_attn_b_proj"]).reshape(1, 2048))
    rw = np.asarray(inp["L0_ffn_router_w"])
    d["rt"] = np.ascontiguousarray(rw.reshape(16, 128, 8).transpose(1, 0, 2).reshape(128, 128))
    d["rmsw"] = np.ascontiguousarray(np.asarray(inp["L0_ffn_rms_weight"]).reshape(16, 128).T)
    d["rmswr"] = np.ascontiguousarray(np.asarray(inp["L0_ffn_rms_weight"]).reshape(1, 2048))
    wg = np.asarray(inp["L0_ffn_w_gate"])
    wu = np.asarray(inp["L0_ffn_w_up"])
    wd = np.asarray(inp["L0_ffn_w_down"])
    swg = np.asarray(inp["L0_ffn_sw_gate"])
    swu = np.asarray(inp["L0_ffn_sw_up"])
    swd = np.asarray(inp["L0_ffn_sw_down"])
    gu = np.empty((NE, FT, 2, 128, 2048), np.float32)
    for e in range(NE):
        g = wg[e] if e < 8 else swg
        u = wu[e] if e < 8 else swu
        gu[e, :, 0] = g.reshape(16, 128, 8, 128).transpose(2, 1, 0, 3).reshape(8, 128, 2048)
        gu[e, :, 1] = u.reshape(16, 128, 8, 128).transpose(2, 1, 0, 3).reshape(8, 128, 2048)
    d["gu"] = np.ascontiguousarray(gu)
    dn = np.empty((NE, 4, 128, 4096), np.float32)
    for e in range(NE):
        D = wd[e] if e < 8 else swd
        dn[e] = D.reshape(8, 128, 4, 512).transpose(2, 1, 0, 3).reshape(4, 128, 4096)
    d["dn"] = np.ascontiguousarray(dn)
    i_idx = np.arange(128)[:, None]
    u_idx = np.arange(512)[None, :]
    mskb = np.empty((128, 2048), BF)
    for s in range(4):
        mskb[:, s * 512:(s + 1) * 512] = (u_idx >= i_idx + 128 * s).astype(BF)
    d["msk"] = mskb
    d["id32"] = np.eye(128, dtype=np.float32)
    selb = np.zeros((8, 1024), BF)
    for e in range(8):
        selb[e, e * 128:(e + 1) * 128] = 1
    d["selb"] = selb
    return d


def kernel(**inputs):
    in_maps = [_pack_core(inputs, c) for c in range(8)]
    nc = _build()
    res = run_bass_kernel_spmd(nc, in_maps, core_ids=list(range(8)))
    out = np.empty((4, 2048, 2048), np.float32)
    for c in range(8):
        b, j = c // 2, c % 2
        out[b, j * 1024:(j + 1) * 1024, :] = res.results[c]["out"]
    return out
